# version 18
# baseline (speedup 1.0000x reference)
import contextlib
import numpy as np
import ml_dtypes
import concourse.bass as bass
import concourse.mybir as mybir

F32 = mybir.dt.float32
BF16 = mybir.dt.bfloat16
U8 = mybir.dt.uint8
AF = mybir.ActivationFunctionType
ALU = mybir.AluOpType
AX = mybir.AxisListType

D = 1024
NCH = 8
FF = 1536
NFC = 12
ENGINES = ("sync", "scalar", "vector", "tensor", "gpsimd")
SLOTS_QK = [0, 4, 1, 5, 2, 6, 3, 7, 8, 12, 9, 13, 10, 14, 11, 15]


class Ctr:
    def __init__(self, sem, name):
        self.sem = sem
        self.name = name
        self.n = 0


class Prog:
    def __init__(self):
        self.ops = {e: [] for e in ENGINES}
        self.hw = {}

    def op(self, eng, ctr, build, k=1):
        ctr.n += k
        self.ops[eng].append(lambda q, b=build, s=ctr.sem, kk=k: b(q).then_inc(s, kk))
        return ctr.n

    def opn(self, eng, build):
        self.ops[eng].append(build)

    def group(self, eng, ctr, builders):
        for f in builders[:-1]:
            self.opn(eng, f)
        return self.op(eng, ctr, builders[-1])

    def wait(self, eng, ctr, val):
        key = (eng, ctr.name)
        if val <= 0 or val <= self.hw.get(key, 0):
            return
        self.hw[key] = val
        self.ops[eng].append(lambda q, s=ctr.sem, v=val: q.wait_ge(s, v))

    def into_block(self, nc):
        with nc.Block() as blk:
            for eng in ENGINES:
                ops = self.ops[eng]
                if ops:
                    getattr(blk, eng)(lambda q, o=ops: [f(q) for f in o])


class Arena:
    def __init__(self, nc):
        self.nc = nc
        self.reqs = []

    def at(self, name, shape, dtype, p0, p1=None):
        if p1 is None:
            p1 = p0
        h = self.nc.virtual_sbuf_tensor(name, list(shape), dtype)
        width = int(np.prod(shape[1:])) * mybir.dt.size(dtype)
        width = (width + 511) // 512 * 512
        self.reqs.append([h, width, p0, p1])
        return h

    def _place(self, order):
        placed = []
        offs = {}
        for i in order:
            h, w, p0, p1 = self.reqs[i]
            if p0 == 9:
                p0, p1 = 0, 9
            cands = sorted({0} | {o + ww for (o, ww, q0, q1) in placed
                                  if not (q1 < p0 or q0 > p1)})
            for off in cands:
                if all(q1 < p0 or q0 > p1 or off + w <= o or off >= o + ww
                       for (o, ww, q0, q1) in placed):
                    placed.append((off, w, p0, p1))
                    offs[i] = off
                    break
        return max(o + w for (o, w, _, _) in placed), offs, placed

    def materialize(self, ctx):
        n = len(self.reqs)
        orders = [
            sorted(range(n), key=lambda i: -self.reqs[i][1]),
            sorted(range(n), key=lambda i: (-(9 if self.reqs[i][2] == 9 else
                                             self.reqs[i][3] - self.reqs[i][2]),
                                            -self.reqs[i][1])),
            sorted(range(n), key=lambda i: (self.reqs[i][2] if self.reqs[i][2] != 9 else -1,
                                            -self.reqs[i][1])),
        ]
        best = None
        for order in orders:
            total, offs, placed = self._place(order)
            if best is None or total < best[0]:
                best = (total, offs, placed)
        total, offs, placed = best
        assert total <= 206 * 1024, f"arena overflow: {total} B/partition"
        back = ctx.enter_context(self.nc.sbuf_tensor("arena", [128, total], U8))
        base = self.nc.lookup_mloc(back).addr
        for i, (h, w, p0, p1) in enumerate(self.reqs):
            real = self.nc.alloc_sbuf_tensor_at(h.name + "_bk", list(h.shape), h.dtype,
                                                offset=base + offs[i])
            self.nc.back_virtual_tensor_with(h, real)
        return total


def build_program(T=2048, n_b=4, debug=False):
    TT = T // 512
    NTB = T // 128
    W_VSB = NTB * 1152
    nc = bass.Bass("TRN2", target_bir_lowering=False)

    dr = {}

    def din(name, shape, dtype=F32):
        dr[name] = nc.dram_tensor(name, list(shape), dtype, kind="ExternalInput").ap()
        return dr[name]

    xT = din("xT", [D, n_b * T])
    din("wq_t", [24, 128, 1024])
    din("wp_t", [8, 128, 1024])
    din("wgu_t", [9, NFC, 128, 2048])
    din("wd_t", [9, 8, 128, FF])
    din("gw_t", [128, 64])
    din("bqk_c", [128, 16])
    din("bv_row", [1, 1024])
    din("bp_c", [128, 8])
    din("cosT", [128, T])
    din("sinT", [128, T])
    din("maskT", [4, 128, 1024], BF16)
    din("selE2", [128, 128], BF16)
    din("selO2", [128, 128], BF16)
    din("hselc", [128, 2], BF16)
    din("negsel", [2, 384])
    din("ident", [128, 128])
    din("identb", [128, 128], BF16)
    din("onesf", [1, 128])
    din("onesb", [1, 128], BF16)
    din("onescol", [128, 1])
    outT = nc.dram_tensor("outT", [D, n_b * T], F32, kind="ExternalOutput").ap()
    dbg = {}
    if debug:
        for nm, shape, dt in [("d_qs", [128, NCH * T], BF16), ("d_ks", [128, NCH * T], BF16),
                              ("d_vsb", [128, W_VSB], BF16), ("d_attn", [128, NCH * T], BF16),
                              ("d_r1", [128, NCH * T], F32), ("d_h", [128, NCH * T], BF16),
                              ("d_comb", [128, NTB * 8], F32), ("d_mb", [128, 16], F32)]:
            dbg[nm] = nc.dram_tensor(nm, shape, dt, kind="ExternalOutput").ap()

    ar = Arena(nc)
    t_selE = ar.at("selEt", [128, 128], BF16, 9)
    t_selO = ar.at("selOt", [128, 128], BF16, 9)
    t_hsel = ar.at("hselt", [128, 2], BF16, 9)
    t_negsel = ar.at("negselt", [2, 384], F32, 9)
    t_ident = ar.at("identt", [128, 128], F32, 9)
    t_identb = ar.at("identbt", [128, 128], BF16, 9)
    t_onesf = ar.at("onesft", [1, 128], F32, 9)
    t_onesb = ar.at("onesbt", [1, 128], BF16, 9)
    t_onescol = ar.at("onescolt", [128, 1], F32, 9)
    t_gw = ar.at("gw", [128, 64], F32, 9)
    t_gwb = ar.at("gwb", [128, 64], BF16, 9)
    t_bqk = ar.at("bqk", [128, 16], F32, 0, 1)
    t_bv = ar.at("bvt", [1, 1024], F32, 0, 1)
    t_bvb = ar.at("bvbt", [1, 1024], BF16, 0, 1)
    t_bp = ar.at("bpt", [128, 8], F32, 9)
    t_mb = ar.at("mbt", [128, 16], F32, 9)
    t_mask = ar.at("maskt", [128, 4096], BF16, 2)
    t_eps = ar.at("epst", [1, 1], F32, 9)
    t_cos = ar.at("cost", [128, T], F32, 0)
    t_sin = ar.at("sint", [128, T], F32, 0)
    t_xbf = ar.at("xbf", [128, NCH * T], BF16, 0, 1)
    t_xst = ar.at("xst", [128, 1024], F32, 0)
    t_wqst = ar.at("wqst", [128, 2048], F32, 0, 1)
    t_wqbf = ar.at("wqbf", [128, 4096], BF16, 0, 1)
    t_wvbf = ar.at("wvbf", [128, 8192], BF16, 1)
    t_ropeEh = ar.at("ropeEh", [128, 2 * T], F32, 0)
    t_ropeO = ar.at("ropeO", [128, 2048], F32, 0)
    t_m1 = ar.at("m1s", [128, 1024], F32, 0)
    t_m2 = ar.at("m2s", [128, 1024], F32, 0)
    t_m3 = ar.at("m3s", [128, 1024], F32, 0)
    t_m4 = ar.at("m4s", [128, 1024], F32, 0)
    t_rebf = ar.at("rebf", [128, 4096], BF16, 0, 1)
    t_robf = ar.at("robf", [128, 4096], BF16, 0, 1)
    t_qs = ar.at("qst", [128, NCH * T], BF16, 0, 2)
    t_ks = ar.at("kst", [128, NCH * T], BF16, 0, 2)
    t_vsb = ar.at("vsbt", [128, W_VSB], BF16, 1, 2)
    t_sqbf = ar.at("sqbf", [128, 1024], BF16, 1)
    t_mq2 = ar.at("mq2", [2, 8], F32, 1)
    t_mk2 = ar.at("mk2", [2, 8], F32, 1)
    t_msr = ar.at("msr", [2, 8], F32, 1)
    t_pab = ar.at("pab", [128, 3072], BF16, 2)
    t_sumr = ar.at("sumr", [1, 2048], F32, 2)
    t_sumrb = ar.at("sumrb", [1, 1024], BF16, 2)
    t_bcs = ar.at("bcs", [128, 1024], F32, 2)
    t_attnB = ar.at("attnB", [64, 1024], BF16, 2)
    t_attn = ar.at("attnt", [128, NCH * T], BF16, 2, 3)
    t_wpst = ar.at("wpst", [128, 2048], F32, 3)
    t_wpbf = ar.at("wpbf", [128, 2048], BF16, 3)
    t_xre = ar.at("xre", [128, 1024], F32, 3)
    t_r1 = ar.at("r1t", [128, NCH * T], F32, 3, 6)
    t_hbf = ar.at("hbft", [128, NCH * T], BF16, 3, 5)
    ln_sets = {}
    for ph, tag in ((3, "c"), (6, "f")):
        ln_sets[tag] = dict(
            sqsc=ar.at(f"sqsc{tag}", [128, 1024], F32, ph),
            sums=ar.at(f"sums{tag}", [1, T], F32, ph),
            avb=ar.at(f"avb{tag}", [1, T], BF16, ph),
            bvb=ar.at(f"bvb{tag}", [1, T], BF16, ph),
            sqs=ar.at(f"sqst{tag}", [1, T], F32, ph),
            av=ar.at(f"avt{tag}", [1, T], F32, ph),
            bv2=ar.at(f"bv2t{tag}", [1, T], F32, ph),
            lnt=ar.at(f"lnt{tag}", [128, 1024], F32, ph),
            absb=ar.at(f"absb{tag}", [128, 2048], BF16, ph - 1 if tag == "f" else ph, ph))
    t_comb = ar.at("combt", [128, NTB * 8], F32, 4, 5)
    t_lgb = ar.at("lgbt", [8, 1024], F32, 4)
    t_cd = ar.at("cdt", [128, 8], F32, 4)
    t_ceq = ar.at("ceqt", [128, 8], F32, 4)
    t_cm1 = ar.at("cm1t", [128, 1], F32, 4)
    t_cm2 = ar.at("cm2t", [128, 1], F32, 4)
    t_cnum = ar.at("cnumt", [128, 8], F32, 4)
    t_cden = ar.at("cdent", [128, 1], F32, 4)
    t_wgust = ar.at("wgust", [128, 2048], F32, 5)
    t_wgubf = ar.at("wgubf", [128, 2 * 2048], BF16, 5)
    t_wdst = ar.at("wdst", [128, 2 * FF], F32, 5)
    t_wdbf = ar.at("wdbf", [128, 2 * FF], BF16, 5)
    t_su = ar.at("sut", [128, NFC * T], BF16, 5)
    t_gbuf = ar.at("gbuft", [128, 2 * T], BF16, 5)
    t_combsb = ar.at("combsbt", [128, T], BF16, 5)
    t_rowt = ar.at("rowtt", [1, 512], F32, 5)
    t_outst = ar.at("outst", [128, 1024], F32, 6)

    ctx = contextlib.ExitStack()
    with ctx:
        total_sbuf = ar.materialize(ctx)

        class BankView:
            def __init__(self, t, c0):
                self.t = t
                self.c0 = c0

            def __getitem__(self, idx):
                rows, cols = idx
                c0 = (cols.start or 0) + self.c0
                c1 = (512 if cols.stop is None else cols.stop) + self.c0
                return self.t[rows, c0:c1]

        psw = [ctx.enter_context(nc.psum_tensor(f"pw{i}", [128, 1024], F32))
               for i in range(4)]
        psb = [BankView(psw[i // 2], (i % 2) * 512) for i in range(8)]
        S = {nm: Ctr(ctx.enter_context(nc.semaphore(nm)), nm)
             for nm in ("dq_x0", "dq_x1", "dq_w0", "dq_w1", "dq_wd0", "dq_wd1", "dq_wgu", "dq_cs",
                        "dq_sh0", "dq_sh1", "dq_o0", "dq_o1", "dq_pm0", "dq_pm1", "dq_pm2", "dq_pm3", "dq_dbg", "dq_init",
                        "c_t", "c_s", "c_v", "c_p")}
        dq_x = [S["dq_x0"], S["dq_x1"]]
        dq_w = [S["dq_w0"], S["dq_w1"]]
        dq_wd = [S["dq_wd0"], S["dq_wd1"]]
        dq_wgu, dq_cs, dq_dbg, dq_init = (S["dq_wgu"], S["dq_cs"], S["dq_dbg"], S["dq_init"])
        dq_sh = [S["dq_sh0"], S["dq_sh1"]]
        dq_o = [S["dq_o0"], S["dq_o1"]]
        dq_pm = [S["dq_pm0"], S["dq_pm1"]]
        dq_pmh = [S["dq_pm2"], S["dq_pm3"]]
        c_t, c_s, c_v, c_p = (S[k] for k in ("c_t", "c_s", "c_v", "c_p"))
        bank_free = [(None, 0)] * 8

        def bwait(P, i, eng="tensor"):
            ctr, v = bank_free[i]
            if ctr is not None:
                P.wait(eng, ctr, v)

        out_cnt = {0: 0, 1: 0}

        def emit_ln(P, src_is_r1=True, sink="hbf"):
            sq_hist = []
            lnt_ctr = {}
            lnt_slot_last = {}
            ab_used = {}
            t_sqsc = ln_sets["c" if sink == "hbf" else "f"]["sqsc"]
            t_sums = ln_sets["c" if sink == "hbf" else "f"]["sums"]
            t_sqs = ln_sets["c" if sink == "hbf" else "f"]["sqs"]
            t_av = ln_sets["c" if sink == "hbf" else "f"]["av"]
            t_bv2 = ln_sets["c" if sink == "hbf" else "f"]["bv2"]
            t_lnt = ln_sets["c" if sink == "hbf" else "f"]["lnt"]
            t_absb = ln_sets["c" if sink == "hbf" else "f"]["absb"]
            t_avb = ln_sets["c" if sink == "hbf" else "f"]["avb"]
            t_bvb = ln_sets["c" if sink == "hbf" else "f"]["bvb"]
            """LayerNorm over d of t_r1 (in place), stats via fp32 matmuls.
            sink: 'hbf' (phase C: write normalized into r1 + cast to hbf)
                  'out' (phase F: write normalized to outstage + DMA out)."""
            stat_bank_v = []
            for tt in range(TT):
                bk = tt % 2
                bk2 = 2 + tt % 2
                bwait(P, bk)
                bwait(P, bk2)
                for oc in range(NCH):
                    sl = oc % 2
                    sq_eng = "scalar" if oc % 2 == 0 else "gpsimd"
                    sq_ctr = c_s if oc % 2 == 0 else c_p
                    rw = r1_written[(oc, tt)]
                    rw_ctr, rw_v = rw if isinstance(rw, tuple) else (c_v, rw)
                    P.wait(sq_eng, rw_ctr, rw_v)
                    if len(sq_hist) >= 2:
                        P.wait(sq_eng, c_t, sq_hist[-2])
                    rsl_ = lambda q, oc=oc, tt=tt: t_r1[:, oc * T + tt * 512: oc * T + (tt + 1) * 512]
                    if oc % 2 == 0:
                        vs = P.op("scalar", c_s, lambda q, oc=oc, tt=tt, sl=sl:
                                  q.activation(out=t_sqsc[:, sl * 512:(sl + 1) * 512],
                                               in_=t_r1[:, oc * T + tt * 512: oc * T + (tt + 1) * 512],
                                               func=AF.Square))
                    else:
                        vs = P.op("gpsimd", c_p, lambda q, oc=oc, tt=tt, sl=sl:
                                  q.tensor_mul(out=t_sqsc[:, sl * 512:(sl + 1) * 512],
                                               in0=t_r1[:, oc * T + tt * 512: oc * T + (tt + 1) * 512],
                                               in1=t_r1[:, oc * T + tt * 512: oc * T + (tt + 1) * 512]))
                    P.wait("tensor", rw_ctr, rw_v)
                    P.wait("tensor", sq_ctr, vs)
                    P.opn("tensor", lambda q, bk=bk, oc=oc, tt=tt:
                          q.matmul(psb[bk][0:1, :], t_onescol[:, :],
                                   t_r1[:, oc * T + tt * 512: oc * T + (tt + 1) * 512],
                                   start=(oc == 0), stop=(oc == NCH - 1)))
                    vts = P.op("tensor", c_t, lambda q, bk2=bk2, sl=sl, oc=oc:
                               q.matmul(psb[bk2][0:1, :], t_onescol[:, :],
                                        t_sqsc[:, sl * 512:(sl + 1) * 512],
                                        start=(oc == 0), stop=(oc == NCH - 1)))
                    sq_hist.append(vts)
                P.wait("scalar", c_t, c_t.n)
                P.op("scalar", c_s, lambda q, bk=bk, tt=tt:
                     q.copy(out=t_sums[0:1, tt * 512:(tt + 1) * 512], in_=psb[bk][0:1, :]))
                v = P.op("scalar", c_s, lambda q, bk2=bk2, tt=tt:
                         q.copy(out=t_sqs[0:1, tt * 512:(tt + 1) * 512], in_=psb[bk2][0:1, :]))
                bank_free[bk] = (c_s, v)
                bank_free[bk2] = (c_s, v)
                stat_bank_v.append(v)
            P.wait("vector", c_s, max(stat_bank_v))
            P.op("vector", c_v, lambda q: q.tensor_scalar_mul(out=t_av[0:1, :], in0=t_sums[0:1, :],
                                                              scalar1=1.0 / 1024.0))
            P.wait("vector", c_v, c_v.n)
            P.op("vector", c_v, lambda q: q.tensor_mul(out=t_bv2[0:1, :], in0=t_av[0:1, :],
                                                       in1=t_av[0:1, :]))
            P.wait("vector", c_v, c_v.n)
            P.op("vector", c_v, lambda q: q.scalar_tensor_tensor(
                out=t_bv2[0:1, :], in0=t_sqs[0:1, :], scalar=1.0 / 1024.0, in1=t_bv2[0:1, :],
                op0=ALU.mult, op1=ALU.subtract))
            P.wait("scalar", c_v, c_v.n)
            P.op("scalar", c_s, lambda q: q.activation(out=t_sqs[0:1, :], in_=t_bv2[0:1, :],
                                                       func=AF.Sqrt, bias=t_eps[0:1, 0:1]))
            P.wait("scalar", c_s, c_s.n)
            P.op("scalar", c_s, lambda q: q.activation(out=t_bv2[0:1, :], in_=t_sqs[0:1, :],
                                                       func=AF.Ln))
            P.wait("scalar", c_s, c_s.n)
            P.op("scalar", c_s, lambda q: q.activation(out=t_sums[0:1, :], in_=t_bv2[0:1, :],
                                                       func=AF.Exp, scale=-1.0))
            rstd_v = c_s.n
            P.wait("vector", c_s, rstd_v)
            P.op("vector", c_v, lambda q: q.scalar_tensor_tensor(
                out=t_bv2[0:1, :], in0=t_av[0:1, :], scalar=-1.0, in1=t_sums[0:1, :],
                op0=ALU.mult, op1=ALU.mult))
            P.wait("scalar", c_v, c_v.n)
            P.op("scalar", c_s, lambda q: q.copy(out=t_avb[0:1, :], in_=t_sums[0:1, :]))
            v_abb = P.op("scalar", c_s, lambda q: q.copy(out=t_bvb[0:1, :], in_=t_bv2[0:1, :]))
            for tt in range(TT):
                for h_i in range(2):
                    bk = 4 + 2 * (tt % 2) + h_i
                    bwait(P, bk)
                    P.wait("tensor", c_s, v_abb)
                    P.op("tensor", c_t, lambda q, bk=bk, src=(t_avb if h_i == 0 else t_bvb), tt=tt:
                         q.matmul(psb[bk][:, :], t_onesb[0:1, :],
                                  src[0:1, tt * 512:(tt + 1) * 512],
                                  start=True, stop=True))
                bc_v = c_t.n
                P.wait("vector", c_t, bc_v)
                if ab_used.get(tt % 2):
                    P.wait("vector", c_v, ab_used[tt % 2][0])
                    P.wait("vector", c_p, ab_used[tt % 2][1])
                P.op("vector", c_v, lambda q, pa=4 + 2 * (tt % 2), ao=(tt % 2) * 1024:
                     q.tensor_copy(out=t_absb[:, ao:ao + 512], in_=psb[pa][:, :]))
                vab_s = P.op("vector", c_v, lambda q, pb=5 + 2 * (tt % 2), ao=(tt % 2) * 1024:
                             q.tensor_copy(out=t_absb[:, ao + 512:ao + 1024], in_=psb[pb][:, :]))
                last_v = 0
                for oc in range(NCH):
                    sl = oc % 2
                    a_eng = "vector" if oc % 2 == 0 else "gpsimd"
                    a_ctr = c_v if oc % 2 == 0 else c_p
                    P.wait(a_eng, c_v, vab_s)
                    prev_sl = lnt_slot_last.get(oc % 2)
                    if prev_sl is not None:
                        P.wait(a_eng, prev_sl[0], prev_sl[1])
                    P.op(a_eng, a_ctr, lambda q, oc=oc, tt=tt, sl=oc % 2, ao=(tt % 2) * 1024:
                         q.tensor_mul(out=t_lnt[:, sl * 512:(sl + 1) * 512],
                                      in0=t_r1[:, oc * T + tt * 512: oc * T + (tt + 1) * 512],
                                      in1=t_absb[:, ao:ao + 512]))
                    P.wait(a_eng, a_ctr, a_ctr.n)
                    if sink == "hbf":
                        v2 = P.op(a_eng, a_ctr, lambda q, oc=oc, tt=tt, sl=oc % 2:
                                  q.tensor_add(out=t_r1[:, oc * T + tt * 512: oc * T + (tt + 1) * 512],
                                               in0=t_lnt[:, sl * 512:(sl + 1) * 512],
                                               in1=t_absb[:, (tt % 2) * 1024 + 512:(tt % 2) * 1024 + 1024]))
                        lnt_used[(tt, oc)] = v2
                        lnt_ctr[(tt, oc)] = a_ctr
                        lnt_slot_last[oc % 2] = (a_ctr, v2)
                        P.wait("scalar", a_ctr, v2)
                        vh = P.op("scalar", c_s, lambda q, oc=oc, tt=tt:
                                  q.copy(out=t_hbf[:, oc * T + tt * 512: oc * T + (tt + 1) * 512],
                                         in_=t_r1[:, oc * T + tt * 512: oc * T + (tt + 1) * 512]))
                        hbf_cast[(oc, tt)] = vh
                        r1_final[(oc, tt)] = (a_ctr, v2)
                        last_v = v2
                    else:
                        osl = (oc % 2)
                        if out_cnt[osl] > 0:
                            P.wait(a_eng, dq_o[osl], 16 * out_cnt[osl])
                        v2 = P.op(a_eng, a_ctr, lambda q, oc=oc, osl=osl, sl=oc % 2, tt=tt:
                                  q.tensor_add(out=t_outst[:, osl * 512:(osl + 1) * 512],
                                               in0=t_lnt[:, sl * 512:(sl + 1) * 512],
                                               in1=t_absb[:, (tt % 2) * 1024 + 512:(tt % 2) * 1024 + 1024]))
                        lnt_used[(tt, oc)] = v2
                        lnt_ctr[(tt, oc)] = a_ctr
                        lnt_slot_last[oc % 2] = (a_ctr, v2)
                        P.wait("scalar", a_ctr, v2)
                        P.op("scalar", dq_o[osl], lambda q, oc=oc, tt=tt, osl=osl:
                             q.dma_start(out=outT[oc * 128:(oc + 1) * 128,
                                                  b * T + tt * 512: b * T + (tt + 1) * 512],
                                         in_=t_outst[:, osl * 512:(osl + 1) * 512]), 16)
                        out_cnt[osl] += 1
                        last_v = v2
                ab_used[tt % 2] = (max(lnt_used[(tt, oc)] for oc in range(NCH)
                                       if lnt_ctr[(tt, oc)] is c_v),
                                   max(lnt_used[(tt, oc)] for oc in range(NCH)
                                       if lnt_ctr[(tt, oc)] is c_p))
                bank_free[4 + 2 * (tt % 2)] = (c_v, vab_s)
                bank_free[5 + 2 * (tt % 2)] = (c_v, vab_s)

        for b in range(n_b):
            if b == 0:
                P = Prog()
                for src, th in [(dr["cosT"], t_cos), (dr["sinT"], t_sin),
                                (dr["selE2"], t_selE), (dr["selO2"], t_selO),
                                (dr["hselc"], t_hsel), (dr["negsel"], t_negsel),
                                (dr["ident"], t_ident), (dr["onesf"], t_onesf),
                                (dr["onesb"], t_onesb), (dr["onescol"], t_onescol),
                                (dr["gw_t"], t_gw), (dr["bp_c"], t_bp),
                                (dr["identb"], t_identb)]:
                    P.op("sync", dq_init, lambda q, s=src, d=th:
                         q.dma_start(out=d[:, :], in_=s[:, :]), 16)
                P.wait("scalar", dq_init, dq_init.n)
                P.op("vector", c_v, lambda q: q.memset(t_eps[0:1, 0:1], 1e-5))
                P.op("scalar", c_s, lambda q: q.copy(out=t_gwb[:, :], in_=t_gw[:, :]))
                P.into_block(nc)

            P = Prog()
            if b > 0:
                for eng in ("sync", "gpsimd", "scalar", "vector"):
                    P.wait(eng, dq_o[0], dq_o[0].n)
                    P.wait(eng, dq_o[1], dq_o[1].n)
            P.op("sync", dq_cs, lambda q: q.dma_start(out=t_cos[:, :], in_=dr["cosT"][:, :]), 16)
            P.op("sync", dq_cs, lambda q: q.dma_start(out=t_sin[:, :], in_=dr["sinT"][:, :]), 16)
            P.op("sync", dq_cs, lambda q: q.dma_start(out=t_bqk[:, :], in_=dr["bqk_c"][:, :]), 16)
            P.op("sync", dq_cs, lambda q: q.dma_start(out=t_bv[:, :], in_=dr["bv_row"][:, :]), 16)
            cs_v = dq_cs.n
            P.wait("scalar", dq_cs, cs_v)
            bvb_v = P.op("scalar", c_s, lambda q: q.copy(out=t_bvb[:, :], in_=t_bv[:, :]))

            xcast_v = {}
            xhist = []
            for i, (tt, dc) in enumerate([(tt, dc) for tt in range(TT) for dc in range(NCH)]):
                if i >= 2:
                    P.wait("sync", c_p, xhist[i - 2])
                P.op("sync", dq_x[i % 2], lambda q, dc=dc, tt=tt, sl=i % 2:
                     q.dma_start(out=t_xst[:, sl * 512:(sl + 1) * 512],
                                 in_=xT[dc * 128:(dc + 1) * 128,
                                        b * T + tt * 512: b * T + (tt + 1) * 512]), 16)
                P.wait("gpsimd", dq_x[i % 2], dq_x[i % 2].n)
                v = P.op("gpsimd", c_p, lambda q, dc=dc, tt=tt, sl=i % 2:
                         q.tensor_copy(out=t_xbf[:, dc * T + tt * 512: dc * T + (tt + 1) * 512],
                                       in_=t_xst[:, sl * 512:(sl + 1) * 512]))
                xcast_v[(tt, dc)] = v
                xhist.append(v)
            xcast_tt = {tt: max(xcast_v[(tt, dc)] for dc in range(NCH)) for tt in range(TT)}

            wcast_v = {}
            qkv_mm_last = {}

            def emit_load_cast(s, preloaded=False):
                if not preloaded:
                    if s >= 2:
                        P.wait("sync", c_s, wcast_v[s - 2])
                    P.op("sync", dq_w[s % 2], lambda q, s=s, sl=(s % 2):
                         q.dma_start(out=t_wqst[:, sl * 1024:(sl + 1) * 1024],
                                     in_=dr["wq_t"][s, :, :]), 16)
                P.wait("scalar", dq_w[s % 2], dq_w[s % 2].n)
                if s < 16:
                    if s - 4 in qkv_mm_last:
                        P.wait("scalar", c_t, qkv_mm_last[s - 4])
                    v = P.op("scalar", c_s, lambda q, sl=(s % 2), bl=(s % 4):
                             q.copy(out=t_wqbf[:, bl * 1024:(bl + 1) * 1024],
                                    in_=t_wqst[:, sl * 1024:(sl + 1) * 1024]))
                else:
                    for dc in range(NCH):
                        v = P.op("scalar", c_s, lambda q, sl=(s % 2), dc=dc, j=s - 16:
                                 q.copy(out=t_wvbf[:, dc * 1024 + j * 128: dc * 1024 + (j + 1) * 128],
                                        in_=t_wqst[:, sl * 1024 + dc * 128: sl * 1024 + (dc + 1) * 128]))
                wcast_v[s] = v

            re_v, ro_v, evacE_v, evacO_v = {}, {}, {}, {}
            perm_dma_v = {}
            rope_prev = [0, 0]
            o_slot_last = {}

            def emit_rope(pair, c2):
                par = pair % 2
                csz = min(2, TT)
                tt0 = c2 * csz
                w = csz * 512
                rsl = par * 2048 + tt0 * 512
                eh = t_ropeEh[:, par * T + tt0 * 512: par * T + tt0 * 512 + w]
                osl = t_ropeO[:, tt0 * 512: tt0 * 512 + w]
                cosx = t_cos[:, tt0 * 512: tt0 * 512 + w]
                sinx = t_sin[:, tt0 * 512: tt0 * 512 + w]
                P.wait("vector", c_v, rope_prev[0])
                P.wait("vector", c_p, rope_prev[1])
                P.wait("gpsimd", c_v, rope_prev[0])
                P.wait("gpsimd", c_p, rope_prev[1])
                ev_e = max(evacE_v[(pair, tt0 + j)] for j in range(csz))
                ev_o = max(evacO_v[(pair, tt0 + j)] for j in range(csz))
                P.wait("vector", c_s, ev_e)
                P.wait("vector", dq_cs, cs_v)
                v_m1 = P.op("vector", c_v, lambda q, eh=eh, cosx=cosx:
                            q.tensor_mul(out=t_m1[:, 0:w], in0=eh, in1=cosx))
                P.wait("gpsimd", c_s, ev_o)
                P.wait("gpsimd", dq_cs, cs_v)
                v_m2 = P.op("gpsimd", c_p, lambda q, osl=osl, sinx=sinx:
                            q.tensor_mul(out=t_m2[:, 0:w], in0=osl, in1=sinx))
                v_m4 = P.op("gpsimd", c_p, lambda q, osl=osl, cosx=cosx:
                            q.tensor_mul(out=t_m4[:, 0:w], in0=osl, in1=cosx))
                pv = perm_dma_v.get(pair - 2)
                if pv is not None:
                    P.wait("vector", dq_pm[par], pv[0])
                    P.wait("vector", dq_pmh[par], pv[1])
                P.wait("vector", c_p, v_m2)
                P.wait("vector", c_v, v_m1)
                v_re = P.op("vector", c_v, lambda q, rsl=rsl: q.tensor_sub(
                    out=t_rebf[:, rsl:rsl + w], in0=t_m1[:, 0:w], in1=t_m2[:, 0:w]))
                v_m3 = P.op("vector", c_v, lambda q, eh=eh, sinx=sinx:
                            q.tensor_mul(out=t_m3[:, 0:w], in0=eh, in1=sinx))
                P.wait("gpsimd", c_v, v_m3)
                if pv is not None:
                    P.wait("gpsimd", dq_pm[par], pv[0])
                    P.wait("gpsimd", dq_pmh[par], pv[1])
                P.wait("gpsimd", c_p, v_m4)
                v_ro = P.op("gpsimd", c_p, lambda q, rsl=rsl: q.tensor_add(
                    out=t_robf[:, rsl:rsl + w], in0=t_m3[:, 0:w], in1=t_m4[:, 0:w]))
                for j in range(csz):
                    re_v[(pair, tt0 + j)] = (v_re, v_m3)
                    ro_v[(pair, tt0 + j)] = v_ro
                rope_prev[0] = v_re
                rope_prev[1] = v_ro

            def emit_qkv_slot(si):
                s = SLOTS_QK[si]
                pair = si // 2
                is_O = (si % 2 == 1)
                par = pair % 2
                for tt in range(TT):
                    bk = tt
                    bwait(P, bk)
                    P.wait("tensor", c_p, xcast_tt[tt])
                    P.wait("tensor", c_s, wcast_v[si])
                    builders = [
                        (lambda q, bk=bk, sl=si % 4, dc=dc, tt=tt:
                         q.matmul(psb[bk][:, :],
                                  t_wqbf[:, sl * 1024 + dc * 128: sl * 1024 + (dc + 1) * 128],
                                  t_xbf[:, dc * T + tt * 512: dc * T + (tt + 1) * 512],
                                  start=(dc == 0), stop=(dc == NCH - 1)))
                        for dc in range(NCH)]
                    mm_v = P.group("tensor", c_t, builders)
                    qkv_mm_last[si] = mm_v
                    P.wait("scalar", c_t, mm_v)
                    P.wait("scalar", dq_cs, cs_v)
                    if not is_O:
                        prevE = (pair - 2, tt)
                        if prevE in re_v:
                            P.wait("scalar", c_v, re_v[prevE][1])
                        v = P.op("scalar", c_s, lambda q, par=par, tt=tt, bk=bk, si=si:
                                 q.activation(out=t_ropeEh[:, par * T + tt * 512: par * T + (tt + 1) * 512],
                                              in_=psb[bk][:, :], func=AF.Identity,
                                              bias=t_bqk[:, si:si + 1], scale=1.0))
                        evacE_v[(pair, tt)] = v
                    else:
                        prevO = o_slot_last.get(tt % 4)
                        if prevO is not None:
                            P.wait("scalar", c_p, ro_v[prevO])
                        o_slot_last[tt % 4] = (pair, tt)
                        v = P.op("scalar", c_s, lambda q, tt=tt, bk=bk, si=si:
                                 q.activation(out=t_ropeO[:, (tt % 4) * 512: (tt % 4 + 1) * 512],
                                              in_=psb[bk][:, :], func=AF.Identity,
                                              bias=t_bqk[:, si:si + 1], scale=1.0))
                        evacO_v[(pair, tt)] = v
                    bank_free[bk] = (c_s, c_s.n)
                    if is_O and (TT == 1 or tt % 2 == 1):
                        emit_rope(pair, tt // min(2, TT))

            def emit_perm(pair):
                qk = 0 if pair < 4 else 1
                ccl = pair % 4
                dstt = t_qs if qk == 0 else t_ks
                par = pair % 2
                wall = TT * 512
                P.wait("scalar", c_v, max(re_v[(pair, tt)][0] for tt in range(TT)))
                P.wait("gpsimd", c_p, max(ro_v[(pair, tt)] for tt in range(TT)))
                for j in range(4):
                    cc = 2 * ccl + j // 2
                    rbase = 64 * (j % 2)
                    P.op("scalar", dq_pmh[par], lambda q, j=j, cc=cc, rbase=rbase, par=par, dstt=dstt, wall=wall:
                         q.dma_start(out=dstt[rbase:rbase + 32, cc * T: cc * T + wall],
                                     in_=t_rebf[32 * j:32 * j + 32, par * 2048: par * 2048 + wall]), 16)
                    P.op("gpsimd", dq_pm[par], lambda q, j=j, cc=cc, rbase=rbase, par=par, dstt=dstt, wall=wall:
                         q.dma_start(out=dstt[rbase + 32:rbase + 64, cc * T: cc * T + wall],
                                     in_=t_robf[32 * j:32 * j + 32, par * 2048: par * 2048 + wall]), 16)
                perm_dma_v[pair] = (dq_pm[par].n, dq_pmh[par].n)

            emit_load_cast(0)
            emit_load_cast(1)
            for si in range(16):
                if si + 2 < 16:
                    emit_load_cast(si + 2)
                emit_qkv_slot(si)
                if si >= 3 and si % 2 == 1:
                    emit_perm((si - 3) // 2)
            emit_perm(7)
            P.wait("sync", c_s, wcast_v[14])
            P.op("sync", dq_w[0], lambda q: q.dma_start(out=t_wqst[:, 0:1024],
                                                        in_=dr["wq_t"][16, :, :]), 16)
            P.wait("sync", c_s, wcast_v[15])
            P.op("sync", dq_w[1], lambda q: q.dma_start(out=t_wqst[:, 1024:2048],
                                                        in_=dr["wq_t"][17, :, :]), 16)
            P.into_block(nc)

            P = Prog()
            for s2 in range(16, 24):
                emit_load_cast(s2, preloaded=(s2 < 18))

            sq_mm_last = {}
            b0_prev = [0]

            def b0_unit(qk, cc):
                srct = t_qs if qk == 0 else t_ks
                m2t = t_mq2 if qk == 0 else t_mk2
                pair = 4 * qk + cc // 2
                pv = perm_dma_v[pair]
                for tt in range(TT):
                    sl = tt % 2
                    P.wait("scalar", dq_pm[pair % 2], pv[0])
                    P.wait("scalar", dq_pmh[pair % 2], pv[1])
                    if sq_mm_last.get(sl):
                        P.wait("scalar", c_t, sq_mm_last[sl])
                    vs = P.op("scalar", c_s, lambda q, srct=srct, cc=cc, tt=tt, sl=sl:
                              q.activation(out=t_sqbf[:, sl * 512:(sl + 1) * 512],
                                           in_=srct[:, cc * T + tt * 512: cc * T + (tt + 1) * 512],
                                           func=AF.Square))
                    bk = tt % 2
                    bwait(P, bk)
                    P.wait("tensor", c_s, vs)
                    vm = P.op("tensor", c_t, lambda q, bk=bk, sl=sl:
                              q.matmul(psb[bk][0:2, :], t_hsel[:, :],
                                       t_sqbf[:, sl * 512:(sl + 1) * 512],
                                       start=True, stop=True))
                    sq_mm_last[sl] = vm
                    P.wait("vector", c_t, vm)
                    P.wait("vector", c_v, b0_prev[0])
                    vr = P.op("vector", c_v, lambda q, bk=bk:
                              q.reduce_max(out=t_msr[0:2, 0:1], in_=psb[bk][0:2, :], axis=AX.X))
                    bank_free[bk] = (c_v, vr)
                    P.wait("vector", c_v, vr)
                    if tt == 0:
                        vmax = P.op("vector", c_v, lambda q, m2t=m2t, cc=cc:
                                    q.tensor_copy(out=m2t[0:2, cc:cc + 1], in_=t_msr[0:2, 0:1]))
                    else:
                        vmax = P.op("vector", c_v, lambda q, m2t=m2t, cc=cc:
                                    q.tensor_max(out=m2t[0:2, cc:cc + 1],
                                                 in0=m2t[0:2, cc:cc + 1],
                                                 in1=t_msr[0:2, 0:1]))
                    b0_prev[0] = vmax

            b0_units = [(qk, cc) for qk in range(2) for cc in range(NCH)]

            vones_v = P.op("gpsimd", c_p, lambda q: q.memset(
                bass.AP(t_vsb, 64, [[W_VSB, 128], [1152, NTB], [72, 16], [1, 1]]), 1.0))
            wv_all = wcast_v[23]
            for i, (tb, dvt) in enumerate([(tb, dvt) for tb in range(NTB) for dvt in range(2)]):
                bk = 4 + (i % 4)
                bwait(P, bk)
                P.wait("tensor", c_p, max(xcast_tt[tb // 4], vones_v))
                P.wait("tensor", c_s, max(wv_all, bvb_v))
                builders = [
                    (lambda q, bk=bk, dc=dc, tb=tb, dvt=dvt:
                     q.matmul(psb[bk][:, :],
                              t_xbf[:, dc * T + tb * 128: dc * T + (tb + 1) * 128],
                              t_wvbf[:, dc * 1024 + dvt * 512: dc * 1024 + (dvt + 1) * 512],
                              start=(dc == 0), stop=False))
                    for dc in range(NCH)]
                builders.append(lambda q, bk=bk, dvt=dvt:
                                q.matmul(psb[bk][:, :], t_onesb[:, :],
                                         t_bvb[0:1, dvt * 512:(dvt + 1) * 512],
                                         start=False, stop=True))
                vt = P.group("tensor", c_t, builders)
                P.wait("vector", c_t, vt)
                for j in range(8):
                    v = P.op("vector", c_v, lambda q, bk=bk, tb=tb, dvt=dvt, j=j:
                             q.tensor_copy(out=bass.AP(t_vsb,
                                                       1152 * tb + 72 * (8 * dvt + j),
                                                       [[W_VSB, 128], [1, 64]]),
                                           in_=psb[bk][:, j * 64:(j + 1) * 64]))
                bank_free[bk] = (c_v, v)
                if i % 2 == 1 and b0_units:
                    b0_unit(*b0_units.pop(0))
            while b0_units:
                b0_unit(*b0_units.pop(0))
            P.wait("vector", c_v, c_v.n)
            P.op("vector", c_v, lambda q: q.tensor_mul(out=t_msr[0:2, :], in0=t_mq2[0:2, :],
                                                       in1=t_mk2[0:2, :]))
            P.wait("scalar", c_v, c_v.n)
            P.op("scalar", c_s, lambda q: q.activation(out=t_mq2[0:2, :], in_=t_msr[0:2, :],
                                                       func=AF.Sqrt, scale=1.0 / 64.0))
            sqrt_v = c_s.n
            for cc in range(NCH):
                bk = cc % 2
                bwait(P, bk)
                P.wait("tensor", c_s, sqrt_v)
                P.op("tensor", c_t, lambda q, bk=bk, cc=cc:
                     q.matmul(psb[bk][0:128, 0:1],
                              t_negsel[0:2, 256:384],
                              t_mq2[0:2, cc:cc + 1], start=True, stop=True))
                P.wait("scalar", c_t, c_t.n)
                v = P.op("scalar", c_s, lambda q, bk=bk, cc=cc:
                         q.copy(out=t_mb[:, cc:cc + 1], in_=psb[bk][0:128, 0:1]))
                bank_free[bk] = (c_s, v)
            mb_v = c_s.n
            if debug and b == 0:
                for ctr in (c_t, c_s, c_v, c_p):
                    P.wait("gpsimd", ctr, ctr.n)
                for nm, th in [("d_qs", t_qs), ("d_ks", t_ks), ("d_vsb", t_vsb)]:
                    P.op("gpsimd", dq_dbg, lambda q, nm=nm, th=th:
                         q.dma_start(out=dbg[nm][:, :], in_=th[:, :]), 16)
                P.wait("gpsimd", dq_dbg, dq_dbg.n)
            P.into_block(nc)

            P = Prog()
            for eng in ("tensor", "scalar", "gpsimd", "vector", "sync"):
                P.wait(eng, dq_pm[0], dq_pm[0].n)
                P.wait(eng, dq_pm[1], dq_pm[1].n)
                P.wait(eng, dq_pmh[0], dq_pmh[0].n)
                P.wait(eng, dq_pmh[1], dq_pmh[1].n)
            for i in range(4):
                P.op("sync", dq_cs, lambda q, i=i:
                     q.dma_start(out=t_mask[:, i * 1024:(i + 1) * 1024],
                                 in_=dr["maskT"][i, :, :]), 16)
            mask_v = dq_cs.n
            pass

            prev_bc = [0]
            prev_vb = [0]
            pending_fin = [None]
            flash_g = [0]
            pv_gv = {}
            for cc in range(NCH):
                hA, hB = 2 * cc, 2 * cc + 1
                for qt in range(TT):
                    nkb = min(4 * qt + 4, NTB)
                    pv_mm_v = {}
                    exp_done = {}

                    def emit_scores(kb):
                        for h_i in range(2):
                            sb = (kb % 2) * 2 + h_i
                            base = 64 * h_i
                            bwait(P, sb)
                            mm_v = P.op("tensor", c_t, lambda q, sb=sb, base=base, cc=cc, kb=kb, qt=qt:
                                        q.matmul(psb[sb][:, :],
                                                 t_ks[base:base + 64,
                                                      cc * T + kb * 128: cc * T + (kb + 1) * 128],
                                                 t_qs[base:base + 64,
                                                      cc * T + qt * 512: cc * T + (qt + 1) * 512],
                                                 start=True, stop=True))
                        g = flash_g[0]
                        flash_g[0] += 1
                        poff = (g % 3) * 1024
                        P.wait("scalar", c_t, mm_v)
                        P.wait("scalar", c_s, mb_v)
                        if g - 3 in pv_gv:
                            P.wait("scalar", c_t, pv_gv[g - 3])
                        v_exp = P.op("scalar", c_s, lambda q, kb=kb, poff=poff, cc=cc:
                                     q.activation(out=t_pab[:, poff:poff + 1024],
                                                  in_=psw[kb % 2][:, :], func=AF.Exp,
                                                  bias=t_mb[:, cc:cc + 1], scale=0.125))
                        bank_free[(kb % 2) * 2] = (c_s, v_exp)
                        bank_free[(kb % 2) * 2 + 1] = (c_s, v_exp)
                        straddle = kb >= 4 * qt
                        if straddle:
                            di = kb - 4 * qt
                            P.wait("gpsimd", dq_cs, mask_v)
                            P.wait("gpsimd", c_s, v_exp)
                            v_exp = P.op("gpsimd", c_p, lambda q, poff=poff, di=di:
                                         q.tensor_mul(out=t_pab[:, poff:poff + 1024],
                                                      in0=t_pab[:, poff:poff + 1024],
                                                      in1=t_mask[:, di * 1024:(di + 1) * 1024]))
                        exp_done[(kb, 0)] = (g, v_exp, straddle)
                        exp_done[(kb, 1)] = (g, v_exp, straddle)

                    def emit_pv(kb, exp_done=exp_done, pv_mm_v=pv_mm_v,
                                nkb=nkb, hA=hA, hB=hB, cc=cc, qt=qt):
                        g, vv, straddle = exp_done[(kb, 0)]
                        for h_i in range(2):
                            hh = hA if h_i == 0 else hB
                            poff = (g % 3) * 1024 + h_i * 512
                            pvb = 4 + h_i
                            if kb == 0:
                                bwait(P, pvb)
                            P.wait("tensor", c_p if straddle else c_s, vv)
                            vpv = P.op("tensor", c_t, lambda q, pvb=pvb, kb=kb, hh=hh, poff=poff,
                                       first=(kb == 0), last=(kb == nkb - 1):
                                       q.matmul(psb[pvb][0:65, :],
                                                bass.AP(t_vsb, 1152 * kb + 72 * hh,
                                                        [[W_VSB, 128], [1, 65]]),
                                                t_pab[:, poff:poff + 512],
                                                start=first, stop=last))
                            pv_mm_v[(kb, h_i)] = vpv

                    emit_scores(0)
                    emit_scores(1)
                    if pending_fin[0] is not None:
                        pending_fin[0]()
                        pending_fin[0] = None
                    for kb in range(2, nkb):
                        emit_scores(kb)
                        emit_pv(kb - 2)
                    emit_pv(nkb - 2)

                    def tile_tail(cc=cc, qt=qt, nkb=nkb, emit_pv=emit_pv,
                                  exp_done=exp_done, pv_mm_v=pv_mm_v):
                        emit_pv(nkb - 1)
                        return c_t.n

                    pv_done = None

                    def finalize(cc=cc, qt=qt, tile_tail=tile_tail):
                        pv_done = tile_tail()
                        P.wait("scalar", c_t, pv_done)
                        P.wait("scalar", c_t, prev_bc[0])
                        P.op("scalar", c_s, lambda q: q.copy(out=t_sumr[0:1, 0:512],
                                                             in_=psb[4][64:65, :]))
                        P.op("scalar", c_s, lambda q: q.copy(out=t_sumr[0:1, 512:1024],
                                                             in_=psb[5][64:65, :]))
                        P.wait("scalar", c_s, c_s.n)
                        P.op("scalar", c_s, lambda q: q.activation(out=t_sumr[0:1, 1024:2048],
                                                                   in_=t_sumr[0:1, 0:1024],
                                                                   func=AF.Ln))
                        P.wait("scalar", c_s, c_s.n)
                        v_rec = P.op("scalar", c_s, lambda q: q.activation(
                            out=t_sumrb[0:1, 0:1024], in_=t_sumr[0:1, 1024:2048],
                            func=AF.Exp, scale=-1.0))
                        for h_i in range(2):
                            bk = 6 + h_i
                            bwait(P, bk)
                            P.wait("tensor", c_s, v_rec)
                            P.op("tensor", c_t, lambda q, bk=bk, h_i=h_i:
                                 q.matmul(psb[bk][0:64, :], t_onesb[0:1, 0:64],
                                          t_sumrb[0:1, h_i * 512:(h_i + 1) * 512],
                                          start=True, stop=True))
                        bc_v = c_t.n
                        prev_bc[0] = bc_v
                        P.wait("vector", c_t, bc_v)
                        P.wait("vector", c_v, prev_vb[0])
                        for h_i in range(2):
                            P.op("vector", c_v, lambda q, h_i=h_i:
                                 q.tensor_copy(out=t_bcs[0:64, h_i * 512:(h_i + 1) * 512],
                                               in_=psb[6 + h_i][0:64, :]))
                        vbc = c_v.n
                        bank_free[6] = (c_v, vbc)
                        bank_free[7] = (c_v, vbc)
                        P.wait("vector", c_v, vbc)
                        vA = P.op("vector", c_v, lambda q, cc=cc, qt=qt:
                                  q.tensor_mul(out=t_attn[0:64, cc * T + qt * 512:
                                                          cc * T + (qt + 1) * 512],
                                               in0=psb[4][0:64, :], in1=t_bcs[0:64, 0:512]))
                        if dq_sh[qt % 2].n > 0:
                            P.wait("vector", dq_sh[qt % 2], dq_sh[qt % 2].n)
                        vB = P.op("vector", c_v, lambda q, sl=qt % 2:
                                  q.tensor_mul(out=t_attnB[0:64, sl * 512:(sl + 1) * 512],
                                               in0=psb[5][0:64, :], in1=t_bcs[0:64, 512:1024]))
                        bank_free[4] = (c_v, vA)
                        bank_free[5] = (c_v, vB)
                        prev_vb[0] = vB
                        P.wait("scalar", c_v, vB)
                        P.op("scalar", dq_sh[qt % 2], lambda q, cc=cc, qt=qt, sl=qt % 2:
                             q.dma_start(out=t_attn[64:128, cc * T + qt * 512:
                                                    cc * T + (qt + 1) * 512],
                                         in_=t_attnB[0:64, sl * 512:(sl + 1) * 512]), 16)
                    pending_fin[0] = finalize
            pending_fin[0]()
            pending_fin[0] = None
            if debug and b == 0:
                for ctr in (c_t, c_s, c_v, c_p):
                    P.wait("gpsimd", ctr, ctr.n)
                P.wait("gpsimd", dq_sh[0], dq_sh[0].n)
                P.wait("gpsimd", dq_sh[1], dq_sh[1].n)
                P.op("gpsimd", dq_dbg, lambda q: q.dma_start(out=dbg["d_attn"][:, :], in_=t_attn[:, :]), 16)
                P.op("gpsimd", dq_dbg, lambda q: q.dma_start(out=dbg["d_mb"][:, :], in_=t_mb[:, :]), 16)
                P.wait("gpsimd", dq_dbg, dq_dbg.n)
            P.into_block(nc)

            P = Prog()
            r1_written = {}
            hbf_cast = {}
            r1_final = {}
            sq_used = {}
            lnt_used = {}
            for eng in ("tensor", "sync", "scalar", "vector", "gpsimd"):
                P.wait(eng, dq_sh[0], dq_sh[0].n)
                P.wait(eng, dq_sh[1], dq_sh[1].n)
            wpcast_v = []
            xre_hist = []
            proj_mm_last = {}
            for oc in range(NCH):
                if oc >= 2:
                    P.wait("sync", c_s, wpcast_v[oc - 2])
                P.op("sync", dq_w[oc % 2], lambda q, oc=oc, sl=oc % 2:
                     q.dma_start(out=t_wpst[:, sl * 1024:(sl + 1) * 1024],
                                 in_=dr["wp_t"][oc, :, :]), 16)
                P.wait("scalar", dq_w[oc % 2], dq_w[oc % 2].n)
                if oc - 2 in proj_mm_last:
                    P.wait("scalar", c_t, proj_mm_last[oc - 2])
                vw = P.op("scalar", c_s, lambda q, sl=oc % 2:
                          q.copy(out=t_wpbf[:, sl * 1024:(sl + 1) * 1024],
                                 in_=t_wpst[:, sl * 1024:(sl + 1) * 1024]))
                wpcast_v.append(vw)
                for tt in range(TT):
                    i = oc * TT + tt
                    if i >= 2:
                        P.wait("sync", c_v, xre_hist[i - 2])
                    P.op("sync", dq_x[i % 2], lambda q, oc=oc, tt=tt, sl=i % 2:
                         q.dma_start(out=t_xre[:, sl * 512:(sl + 1) * 512],
                                     in_=xT[oc * 128:(oc + 1) * 128,
                                            b * T + tt * 512: b * T + (tt + 1) * 512]), 16)
                    xre_dma = (i % 2, dq_x[i % 2].n)
                    bk = (oc % 2) * 4 + tt
                    bwait(P, bk)
                    P.wait("tensor", c_s, vw)
                    builders = [
                        (lambda q, bk=bk, sl=oc % 2, dc=dc, tt=tt:
                         q.matmul(psb[bk][:, :],
                                  t_wpbf[:, sl * 1024 + dc * 128: sl * 1024 + (dc + 1) * 128],
                                  t_attn[:, dc * T + tt * 512: dc * T + (tt + 1) * 512],
                                  start=(dc == 0), stop=(dc == NCH - 1)))
                        for dc in range(NCH)]
                    vt = P.group("tensor", c_t, builders)
                    proj_mm_last[oc] = vt
                    P.wait("vector", c_t, vt)
                    P.wait("vector", dq_x[xre_dma[0]], xre_dma[1])
                    vv = P.op("vector", c_v, lambda q, bk=bk, oc=oc, tt=tt, sl=i % 2:
                              q.scalar_tensor_tensor(
                                  out=t_r1[:, oc * T + tt * 512: oc * T + (tt + 1) * 512],
                                  in0=psb[bk][:, :], scalar=t_bp[:, oc:oc + 1],
                                  in1=t_xre[:, sl * 512:(sl + 1) * 512],
                                  op0=ALU.add, op1=ALU.add))
                    r1_written[(oc, tt)] = vv
                    xre_hist.append(vv)
                    bank_free[bk] = (c_v, vv)
            emit_ln(P, sink="hbf")
            if debug and b == 0:
                for ctr in (c_t, c_s, c_v, c_p):
                    P.wait("gpsimd", ctr, ctr.n)
                P.op("gpsimd", dq_dbg, lambda q: q.dma_start(out=dbg["d_r1"][:, :], in_=t_r1[:, :]), 16)
                P.op("gpsimd", dq_dbg, lambda q: q.dma_start(out=dbg["d_h"][:, :], in_=t_hbf[:, :]), 16)
                P.wait("gpsimd", dq_dbg, dq_dbg.n)
            P.into_block(nc)

            P = Prog()
            prev_comb = [0]
            lg_mm = {}
            for tt in range(TT):
                bk = 4 + tt % 2
                bwait(P, bk)
                builders = [
                    (lambda q, bk=bk, dc=dc, tt=tt:
                     q.matmul(psb[bk][0:8, :], t_gwb[:, dc * 8:(dc + 1) * 8],
                              t_hbf[:, dc * T + tt * 512: dc * T + (tt + 1) * 512],
                              start=(dc == 0), stop=(dc == NCH - 1)))
                    for dc in range(NCH)]
                vt = P.group("tensor", c_t, builders)
                P.wait("scalar", c_t, vt)
                if tt >= 2:
                    P.wait("scalar", c_t, lg_mm.get(tt - 2, 0))
                vlg = P.op("scalar", c_s, lambda q, bk=bk, sl=tt % 2:
                           q.copy(out=t_lgb[0:8, sl * 512:(sl + 1) * 512], in_=psb[bk][0:8, :]))
                bank_free[bk] = (c_s, vlg)
                for i in range(4):
                    tb = tt * 4 + i
                    bk2 = tb % 2
                    bwait(P, bk2)
                    P.wait("tensor", c_s, vlg)
                    vtr = P.op("tensor", c_t, lambda q, bk2=bk2, sl=tt % 2, i=i:
                               q.transpose(psb[bk2][0:128, 0:8],
                                           t_lgb[0:8, sl * 512 + i * 128: sl * 512 + (i + 1) * 128],
                                           t_ident[0:8, 0:8]))
                    lg_mm[tt] = vtr
                    P.wait("vector", c_t, vtr)
                    P.wait("vector", c_v, prev_comb[0])
                    P.op("vector", c_v, lambda q, bk2=bk2: q.reduce_max(
                        out=t_cm1[:, :], in_=psb[bk2][0:128, 0:8], axis=AX.X))
                    P.wait("vector", c_v, c_v.n)
                    vd = P.op("vector", c_v, lambda q, bk2=bk2: q.tensor_scalar_sub(
                        out=t_cd[:, :], in0=psb[bk2][0:128, 0:8], scalar1=t_cm1[:, 0:1]))
                    bank_free[bk2] = (c_v, vd)
                    P.wait("vector", c_v, vd)
                    P.op("vector", c_v, lambda q: q.tensor_single_scalar(
                        out=t_ceq[:, :], in_=t_cd[:, :], scalar=0.0, op=ALU.is_equal))
                    P.wait("vector", c_v, c_v.n)
                    P.op("vector", c_v, lambda q: q.scalar_tensor_tensor(
                        out=t_ceq[:, :], in0=t_ceq[:, :], scalar=-1e30, in1=t_cd[:, :],
                        op0=ALU.mult, op1=ALU.add))
                    P.wait("vector", c_v, c_v.n)
                    P.op("vector", c_v, lambda q: q.reduce_max(out=t_cm2[:, :], in_=t_ceq[:, :],
                                                               axis=AX.X))
                    P.wait("vector", c_v, c_v.n)
                    vk = P.op("vector", c_v, lambda q: q.tensor_single_scalar(
                        out=t_ceq[:, :], in_=t_cd[:, :], scalar=t_cm2[:, 0:1], op=ALU.is_ge))
                    P.wait("scalar", c_v, vd)
                    ve = P.op("scalar", c_s, lambda q: q.activation(out=t_cnum[:, :],
                                                                   in_=t_cd[:, :], func=AF.Exp))
                    P.wait("vector", c_s, ve)
                    P.wait("vector", c_v, vk)
                    P.op("vector", c_v, lambda q: q.tensor_mul(out=t_cnum[:, :], in0=t_cnum[:, :],
                                                               in1=t_ceq[:, :]))
                    P.wait("vector", c_v, c_v.n)
                    P.op("vector", c_v, lambda q: q.reduce_sum(out=t_cden[:, :], in_=t_cnum[:, :],
                                                               axis=AX.X))
                    P.wait("vector", c_v, c_v.n)
                    P.op("vector", c_v, lambda q: q.reciprocal(out=t_cm1[:, :], in_=t_cden[:, :]))
                    P.wait("vector", c_v, c_v.n)
                    prev_comb[0] = P.op("vector", c_v, lambda q, tb=tb: q.tensor_scalar_mul(
                        out=t_comb[:, tb * 8:(tb + 1) * 8], in0=t_cnum[:, :],
                        scalar1=t_cm1[:, 0:1]))
            comb_done = c_v.n
            if debug and b == 0:
                for ctr in (c_t, c_s, c_v, c_p):
                    P.wait("gpsimd", ctr, ctr.n)
                P.op("gpsimd", dq_dbg, lambda q: q.dma_start(out=dbg["d_comb"][:, :], in_=t_comb[:, :]), 16)
                P.wait("gpsimd", dq_dbg, dq_dbg.n)
            P.into_block(nc)

            P = Prog()
            combsb_v = {}
            gcm_last = {}
            rowt_used = {}

            def emit_extract(e):
                prev_gcm = gcm_last.get(e - 1)
                for tt in range(TT):
                    evs = []
                    for i in range(4):
                        tb = tt * 4 + i
                        bwait(P, 0)
                        P.op("tensor", c_t, lambda q, tb=tb, e=e:
                             q.transpose(psb[0][0:1, 0:128],
                                         t_comb[:, tb * 8 + e: tb * 8 + e + 1],
                                         t_ident[:, :]))
                        P.wait("scalar", c_t, c_t.n)
                        if rowt_used.get(0):
                            P.wait("scalar", c_t, rowt_used[0])
                        v = P.op("scalar", c_s, lambda q, i=i:
                                 q.copy(out=t_rowt[0:1, i * 128:(i + 1) * 128],
                                        in_=psb[0][0:1, 0:128]))
                        evs.append(v)
                        bank_free[0] = (c_s, v)
                    bwait(P, 1)
                    P.wait("tensor", c_s, max(evs))
                    rowt_used[0] = P.op("tensor", c_t, lambda q:
                                        q.matmul(psb[1][:, :], t_onesf[0:1, :],
                                                 t_rowt[0:1, 0:512],
                                                 start=True, stop=True))
                    P.wait("vector", c_t, c_t.n)
                    if prev_gcm is not None:
                        P.wait("vector", c_p, prev_gcm)
                    v = P.op("vector", c_v, lambda q, e=e, tt=tt:
                             q.tensor_copy(out=t_combsb[:, tt * 512:(tt + 1) * 512],
                                           in_=psb[1][:, :]))
                    bank_free[1] = (c_v, v)
                combsb_v[e] = c_v.n

            emit_extract(0)
            su_done_v = 0
            gbuf_used = {}
            wgucast = []
            wdcast = []
            gu_mm_last = []
            dn_mm_last = []
            for ei in range(9):
                is_sh = (ei == 8)
                e_w = ei
                for fp in range(NFC):
                    if len(wgucast) >= 1:
                        P.wait("sync", c_s, wgucast[-1])
                    P.op("sync", dq_wgu, lambda q, e_w=e_w, fp=fp:
                         q.dma_start(out=t_wgust[:, :],
                                     in_=dr["wgu_t"][e_w, fp, :, :]), 16)
                    P.wait("scalar", dq_wgu, dq_wgu.n)
                    if len(gu_mm_last) >= 2:
                        P.wait("scalar", c_t, gu_mm_last[-2])
                    v = P.op("scalar", c_s, lambda q, sl=fp % 2:
                             q.copy(out=t_wgubf[:, sl * 2048:(sl + 1) * 2048],
                                    in_=t_wgust[:, :]))
                    wgucast.append(v)
                    wguv = v
                    for tt in range(TT):
                        bk = tt
                        bwait(P, bk)
                        P.wait("tensor", c_s, wguv)
                        if b == 0 and ei == 0:
                            pass
                        builders = [
                            (lambda q, bk=bk, sl=fp % 2, dc=dc, tt=tt:
                             q.matmul(psb[bk][:, :],
                                      t_wgubf[:, sl * 2048 + dc * 256: sl * 2048 + dc * 256 + 128],
                                      t_hbf[:, dc * T + tt * 512: dc * T + (tt + 1) * 512],
                                      start=(dc == 0), stop=(dc == NCH - 1)))
                            for dc in range(NCH)]
                        vg = P.group("tensor", c_t, builders)
                        P.wait("scalar", c_t, vg)
                        gu = gbuf_used.get((fp % 2, tt))
                        if gu is not None:
                            P.wait("scalar", c_v, gu)
                        vsil = P.op("scalar", c_s, lambda q, bk=bk, tt=tt, gsl=fp % 2:
                                    q.activation(out=t_gbuf[:, gsl * T + tt * 512: gsl * T + (tt + 1) * 512],
                                                 in_=psb[bk][:, :], func=AF.Silu))
                        bank_free[bk] = (c_s, vsil)
                        if not is_sh:
                            P.wait("gpsimd", c_s, vsil)
                            P.wait("gpsimd", c_v, combsb_v[ei])
                            vsil2 = P.op("gpsimd", c_p, lambda q, e=ei, tt=tt, gsl=fp % 2:
                                         q.tensor_mul(out=t_gbuf[:, gsl * T + tt * 512: gsl * T + (tt + 1) * 512],
                                                      in0=t_gbuf[:, gsl * T + tt * 512: gsl * T + (tt + 1) * 512],
                                                      in1=t_combsb[:, tt * 512:(tt + 1) * 512]))
                            gcm_last[ei] = vsil2
                        bku = 4 + tt
                        bwait(P, bku)
                        builders = [
                            (lambda q, bk=bku, sl=fp % 2, dc=dc, tt=tt:
                             q.matmul(psb[bk][:, :],
                                      t_wgubf[:, sl * 2048 + dc * 256 + 128: sl * 2048 + (dc + 1) * 256],
                                      t_hbf[:, dc * T + tt * 512: dc * T + (tt + 1) * 512],
                                      start=(dc == 0), stop=(dc == NCH - 1)))
                            for dc in range(NCH)]
                        vu = P.group("tensor", c_t, builders)
                        if tt == TT - 1:
                            gu_mm_last.append(vu)
                        P.wait("vector", c_t, vu)
                        if is_sh:
                            P.wait("vector", c_s, vsil)
                        else:
                            P.wait("vector", c_p, vsil2)
                        vsu = P.op("vector", c_v, lambda q, bk=bku, fp=fp, tt=tt, gsl=fp % 2:
                                   q.tensor_mul(out=t_su[:, fp * T + tt * 512: fp * T + (tt + 1) * 512],
                                                in0=psb[bk][:, :],
                                                in1=t_gbuf[:, gsl * T + tt * 512: gsl * T + (tt + 1) * 512]))
                        gbuf_used[(fp % 2, tt)] = vsu
                        bank_free[bku] = (c_v, vsu)
                su_done_v = c_v.n
                if ei < 7:
                    emit_extract(ei + 1)
                for dd in range(NCH):
                    if len(wdcast) >= 2:
                        P.wait("sync", c_v, wdcast[-2])
                    P.op("sync", dq_wd[dd % 2], lambda q, e_w=e_w, dd=dd, sl=dd % 2:
                         q.dma_start(out=t_wdst[:, sl * FF:(sl + 1) * FF],
                                     in_=dr["wd_t"][e_w, dd, :, :]), 16)
                    P.wait("vector", dq_wd[dd % 2], dq_wd[dd % 2].n)
                    if len(dn_mm_last) >= 2:
                        P.wait("vector", c_t, dn_mm_last[-2])
                    vwd = P.op("vector", c_v, lambda q, sl=dd % 2:
                               q.tensor_copy(out=t_wdbf[:, sl * FF:(sl + 1) * FF],
                                             in_=t_wdst[:, sl * FF:(sl + 1) * FF]))
                    wdcast.append(vwd)
                    for tt in range(TT):
                        bk = (dd % 2) * 4 + tt
                        bwait(P, bk)
                        P.wait("tensor", c_v, max(vwd, su_done_v))
                        builders = [
                            (lambda q, bk=bk, sl=dd % 2, fc=fc, tt=tt:
                             q.matmul(psb[bk][:, :],
                                      t_wdbf[:, sl * FF + fc * 128: sl * FF + (fc + 1) * 128],
                                      t_su[:, fc * T + tt * 512: fc * T + (tt + 1) * 512],
                                      start=(fc == 0), stop=(fc == NFC - 1)))
                            for fc in range(NFC)]
                        vt = P.group("tensor", c_t, builders)
                        if tt == TT - 1:
                            dn_mm_last.append(vt)
                        P.wait("vector", c_t, vt)
                        prev = r1_final.get((dd, tt))
                        if prev is not None:
                            if isinstance(prev, tuple):
                                P.wait("vector", prev[0], prev[1])
                            else:
                                P.wait("vector", c_v, prev)
                        va = P.op("vector", c_v, lambda q, bk=bk, dd=dd, tt=tt:
                                  q.tensor_add(out=t_r1[:, dd * T + tt * 512: dd * T + (tt + 1) * 512],
                                               in0=psb[bk][:, :],
                                               in1=t_r1[:, dd * T + tt * 512: dd * T + (tt + 1) * 512]))
                        r1_final[(dd, tt)] = va
                        bank_free[bk] = (c_v, va)
            P.into_block(nc)

            P = Prog()
            r1_written = {k: v for k, v in r1_final.items()}
            sq_used = {}
            lnt_used = {}
            emit_ln(P, sink="out")
            P.into_block(nc)

        P = Prog()
        P.wait("scalar", dq_o[0], dq_o[0].n)
        P.wait("scalar", dq_o[1], dq_o[1].n)
        P.into_block(nc)

        for ctr in S.values():
            assert ctr.n < 60000, f"semaphore overflow risk: {ctr.name}={ctr.n}"
        meta = dict(total_sbuf=total_sbuf, n_inst=nc.n_instructions(),
                    sems={k: v.n for k, v in S.items()})
    return nc, meta


def qk_perm():
    perm = np.zeros(1024, dtype=np.int64)
    for h in range(16):
        for i in range(32):
            perm[32 * h + i] = 64 * h + 2 * i
            perm[512 + 32 * h + i] = 64 * h + 2 * i + 1
    return perm


def host_constants(T):
    bf = ml_dtypes.bfloat16
    c = {}
    freqs = 10000.0 ** (-np.arange(32) / 32.0)
    ang = np.arange(T)[:, None] * freqs[None, :]
    rowf = np.tile(freqs, 4)
    angs = np.arange(T)[None, :] * rowf[:, None]
    c["cosT"] = np.cos(angs).astype(np.float32)
    c["sinT"] = np.sin(angs).astype(np.float32)
    m = np.zeros((4, 128, 512), np.float32)
    for d in range(4):
        p = np.arange(128)[:, None]
        cc = np.arange(512)[None, :]
        m[d] = (cc >= d * 128 + p).astype(np.float32)
    c["maskT"] = np.concatenate([m, m], axis=2).astype(bf)
    selE = np.zeros((128, 128), np.float32)
    selO = np.zeros((128, 128), np.float32)
    for p in range(64):
        mE = p if p < 32 else p + 32
        mO = p + 32 if p < 32 else p + 64
        selE[p, mE] = 1.0
        selE[64 + p, mE] = 1.0
        selO[p, mO] = 1.0
        selO[64 + p, mO] = 1.0
    c["selE2"] = selE.astype(bf)
    c["selO2"] = selO.astype(bf)
    hs = np.zeros((128, 2), np.float32)
    hs[0:64, 0] = 1.0
    hs[64:128, 1] = 1.0
    c["hselc"] = hs.astype(bf)
    ns = np.zeros((2, 384), np.float32)
    ns[0, 0:128] = -1.0
    ns[1, 128:256] = -1.0
    ns[:, 256:384] = -1.0
    c["negsel"] = ns
    c["ident"] = np.eye(128, dtype=np.float32)
    c["identb"] = np.eye(128, dtype=np.float32).astype(bf)
    c["onesf"] = np.ones((1, 128), np.float32)
    c["onesb"] = np.ones((1, 128), np.float32).astype(bf)
    c["onescol"] = np.ones((128, 1), np.float32)
    return c


def host_weights(W_qkv, b_qkv, W_proj, b_proj, gate_w, w_up, w_down,
                 sg_gate, sg_up, sg_down):
    out = {}
    perm = qk_perm()
    full_perm = np.concatenate([perm, 1024 + perm, 2048 + np.arange(1024)])
    Wq = np.asarray(W_qkv)[:, full_perm]
    bq = np.asarray(b_qkv)[full_perm]
    slots = SLOTS_QK + list(range(16, 24))
    wq_t = np.zeros((24, 128, 1024), np.float32)
    for s_i, jc in enumerate(slots):
        blk = Wq[:, jc * 128:(jc + 1) * 128]
        wq_t[s_i] = blk.reshape(8, 128, 128).transpose(1, 0, 2).reshape(128, 1024)
    out["wq_t"] = np.ascontiguousarray(wq_t)
    bqk = np.zeros((128, 16), np.float32)
    for s_i in range(16):
        jc = SLOTS_QK[s_i]
        bqk[:, s_i] = bq[jc * 128:(jc + 1) * 128]
    out["bqk_c"] = bqk
    out["bv_row"] = np.ascontiguousarray(bq[2048:].reshape(1, 1024).astype(np.float32))
    Wp = np.asarray(W_proj)
    wp_t = np.zeros((8, 128, 1024), np.float32)
    for oc in range(8):
        blk = Wp[:, oc * 128:(oc + 1) * 128]
        wp_t[oc] = blk.reshape(8, 128, 128).transpose(1, 0, 2).reshape(128, 1024)
    out["wp_t"] = wp_t
    out["bp_c"] = np.ascontiguousarray(np.asarray(b_proj).reshape(8, 128).T.astype(np.float32))
    out["gw_t"] = np.ascontiguousarray(np.asarray(gate_w).T.reshape(8, 128, 8)
                                       .transpose(1, 0, 2).reshape(128, 64).astype(np.float32))
    wgu_t = np.zeros((9, NFC, 128, 2048), np.float32)
    wd_t = np.zeros((9, 8, 128, FF), np.float32)
    wu = np.asarray(w_up)
    wd = np.asarray(w_down)
    for e in range(8):
        wgT = wu[e, :FF, :].T
        wuT = wu[e, FF:, :].T
        wdT = wd[e].T
        for fp in range(NFC):
            g = wgT[:, fp * 128:(fp + 1) * 128].reshape(8, 128, 128)
            u = wuT[:, fp * 128:(fp + 1) * 128].reshape(8, 128, 128)
            gu = np.concatenate([g, u], axis=2)
            wgu_t[e, fp] = gu.transpose(1, 0, 2).reshape(128, 2048)
        for dd in range(8):
            blk = wdT[:, dd * 128:(dd + 1) * 128]
            wd_t[e, dd] = blk.reshape(NFC, 128, 128).transpose(1, 0, 2).reshape(128, FF)
    sgT = np.concatenate([np.asarray(sg_gate).T, np.asarray(sg_up).T], axis=1)
    sdT = np.asarray(sg_down).T
    for fp in range(NFC):
        g = sgT[:, fp * 128:(fp + 1) * 128].reshape(8, 128, 128)
        u = sgT[:, FF + fp * 128: FF + (fp + 1) * 128].reshape(8, 128, 128)
        gu = np.concatenate([g, u], axis=2)
        wgu_t[8, fp] = gu.transpose(1, 0, 2).reshape(128, 2048)
    for dd in range(8):
        blk = sdT[:, dd * 128:(dd + 1) * 128]
        wd_t[8, dd] = blk.reshape(NFC, 128, 128).transpose(1, 0, 2).reshape(128, FF)
    out["wgu_t"] = wgu_t
    out["wd_t"] = wd_t
    return out


_T = 2048
_NB = 4
_NCORES = 8


def kernel(L0_attn_x, L0_attn_W_qkv, L0_attn_b_qkv, L0_attn_W_proj, L0_attn_b_proj,
           L0_ffn_gate_w, L0_ffn_w_up, L0_ffn_w_down, L0_ffn_sg_gate, L0_ffn_sg_up,
           L0_ffn_sg_down):
    from concourse.bass_utils import run_bass_kernel_spmd
    x = np.ascontiguousarray(np.asarray(L0_attn_x, dtype=np.float32))
    m = host_weights(L0_attn_W_qkv, L0_attn_b_qkv, L0_attn_W_proj, L0_attn_b_proj,
                     L0_ffn_gate_w, L0_ffn_w_up, L0_ffn_w_down,
                     L0_ffn_sg_gate, L0_ffn_sg_up, L0_ffn_sg_down)
    m.update(host_constants(_T))
    for k in list(m):
        if m[k].dtype == np.float32:
            m[k] = np.ascontiguousarray(m[k])
    in_maps = []
    for c in range(_NCORES):
        mc = dict(m)
        xc = x[c * _NB:(c + 1) * _NB]
        mc["xT"] = np.ascontiguousarray(xc.transpose(2, 0, 1).reshape(1024, _NB * _T))
        in_maps.append(mc)
    nc, _meta = build_program(T=_T, n_b=_NB, debug=False)
    res = run_bass_kernel_spmd(nc, in_maps, core_ids=list(range(_NCORES)))
    outs = []
    for c in range(_NCORES):
        oT = np.asarray(res.results[c]["outT"])
        outs.append(oT.reshape(1024, _NB, _T).transpose(1, 2, 0))
    return np.ascontiguousarray(np.concatenate(outs, axis=0), dtype=np.float32)


# version 19
# speedup vs baseline: 1.0281x; 1.0281x over previous
import contextlib
import numpy as np
import ml_dtypes
import concourse.bass as bass
import concourse.mybir as mybir

F32 = mybir.dt.float32
BF16 = mybir.dt.bfloat16
U8 = mybir.dt.uint8
AF = mybir.ActivationFunctionType
ALU = mybir.AluOpType
AX = mybir.AxisListType

D = 1024
NCH = 8
FF = 1536
NFC = 12
ENGINES = ("sync", "scalar", "vector", "tensor", "gpsimd")
SLOTS_QK = [0, 4, 1, 5, 2, 6, 3, 7, 8, 12, 9, 13, 10, 14, 11, 15]


class Ctr:
    def __init__(self, sem, name):
        self.sem = sem
        self.name = name
        self.n = 0


class Prog:
    def __init__(self):
        self.ops = {e: [] for e in ENGINES}
        self.hw = {}

    def op(self, eng, ctr, build, k=1):
        ctr.n += k
        self.ops[eng].append(lambda q, b=build, s=ctr.sem, kk=k: b(q).then_inc(s, kk))
        return ctr.n

    def opn(self, eng, build):
        self.ops[eng].append(build)

    def group(self, eng, ctr, builders):
        for f in builders[:-1]:
            self.opn(eng, f)
        return self.op(eng, ctr, builders[-1])

    def wait(self, eng, ctr, val):
        key = (eng, ctr.name)
        if val <= 0 or val <= self.hw.get(key, 0):
            return
        self.hw[key] = val
        self.ops[eng].append(lambda q, s=ctr.sem, v=val: q.wait_ge(s, v))

    def into_block(self, nc):
        with nc.Block() as blk:
            for eng in ENGINES:
                ops = self.ops[eng]
                if ops:
                    getattr(blk, eng)(lambda q, o=ops: [f(q) for f in o])


class Arena:
    def __init__(self, nc):
        self.nc = nc
        self.reqs = []

    def at(self, name, shape, dtype, p0, p1=None):
        if p1 is None:
            p1 = p0
        h = self.nc.virtual_sbuf_tensor(name, list(shape), dtype)
        width = int(np.prod(shape[1:])) * mybir.dt.size(dtype)
        width = (width + 511) // 512 * 512
        self.reqs.append([h, width, p0, p1])
        return h

    def _place(self, order):
        placed = []
        offs = {}
        for i in order:
            h, w, p0, p1 = self.reqs[i]
            if p0 == 9:
                p0, p1 = 0, 9
            cands = sorted({0} | {o + ww for (o, ww, q0, q1) in placed
                                  if not (q1 < p0 or q0 > p1)})
            for off in cands:
                if all(q1 < p0 or q0 > p1 or off + w <= o or off >= o + ww
                       for (o, ww, q0, q1) in placed):
                    placed.append((off, w, p0, p1))
                    offs[i] = off
                    break
        return max(o + w for (o, w, _, _) in placed), offs, placed

    def materialize(self, ctx):
        n = len(self.reqs)
        orders = [
            sorted(range(n), key=lambda i: -self.reqs[i][1]),
            sorted(range(n), key=lambda i: (-(9 if self.reqs[i][2] == 9 else
                                             self.reqs[i][3] - self.reqs[i][2]),
                                            -self.reqs[i][1])),
            sorted(range(n), key=lambda i: (self.reqs[i][2] if self.reqs[i][2] != 9 else -1,
                                            -self.reqs[i][1])),
        ]
        best = None
        for order in orders:
            total, offs, placed = self._place(order)
            if best is None or total < best[0]:
                best = (total, offs, placed)
        total, offs, placed = best
        assert total <= 206 * 1024, f"arena overflow: {total} B/partition"
        back = ctx.enter_context(self.nc.sbuf_tensor("arena", [128, total], U8))
        base = self.nc.lookup_mloc(back).addr
        for i, (h, w, p0, p1) in enumerate(self.reqs):
            real = self.nc.alloc_sbuf_tensor_at(h.name + "_bk", list(h.shape), h.dtype,
                                                offset=base + offs[i])
            self.nc.back_virtual_tensor_with(h, real)
        return total


def build_program(T=2048, n_b=4, debug=False):
    TT = T // 512
    NTB = T // 128
    W_VSB = NTB * 1152
    nc = bass.Bass("TRN2", target_bir_lowering=False)

    dr = {}

    def din(name, shape, dtype=F32):
        dr[name] = nc.dram_tensor(name, list(shape), dtype, kind="ExternalInput").ap()
        return dr[name]

    xT = din("xT", [D, n_b * T])
    din("wq_t", [24, 128, 1024])
    din("wp_t", [8, 128, 1024])
    din("wgu_t", [9, NFC, 128, 2048])
    din("wd_t", [9, 8, 128, FF])
    din("gw_t", [128, 64])
    din("bqk_c", [128, 16])
    din("bv_row", [1, 1024])
    din("bp_c", [128, 8])
    din("cosT", [128, T])
    din("sinT", [128, T])
    din("maskT", [4, 128, 1024], BF16)
    din("selE2", [128, 128], BF16)
    din("selO2", [128, 128], BF16)
    din("hselc", [128, 2], BF16)
    din("negsel", [2, 384])
    din("ident", [128, 128])
    din("identb", [128, 128], BF16)
    din("onesf", [1, 128])
    din("onesb", [1, 128], BF16)
    din("onescol", [128, 1])
    outT = nc.dram_tensor("outT", [D, n_b * T], F32, kind="ExternalOutput").ap()
    dbg = {}
    if debug:
        for nm, shape, dt in [("d_qs", [128, NCH * T], BF16), ("d_ks", [128, NCH * T], BF16),
                              ("d_vsb", [128, W_VSB], BF16), ("d_attn", [128, NCH * T], BF16),
                              ("d_r1", [128, NCH * T], F32), ("d_h", [128, NCH * T], BF16),
                              ("d_comb", [128, NTB * 8], F32), ("d_mb", [128, 16], F32)]:
            dbg[nm] = nc.dram_tensor(nm, shape, dt, kind="ExternalOutput").ap()

    ar = Arena(nc)
    t_selE = ar.at("selEt", [128, 128], BF16, 9)
    t_selO = ar.at("selOt", [128, 128], BF16, 9)
    t_hsel = ar.at("hselt", [128, 2], BF16, 9)
    t_negsel = ar.at("negselt", [2, 384], F32, 9)
    t_ident = ar.at("identt", [128, 128], F32, 9)
    t_identb = ar.at("identbt", [128, 128], BF16, 9)
    t_onesf = ar.at("onesft", [1, 128], F32, 9)
    t_onesb = ar.at("onesbt", [1, 128], BF16, 9)
    t_onescol = ar.at("onescolt", [128, 1], F32, 9)
    t_gw = ar.at("gw", [128, 64], F32, 9)
    t_gwb = ar.at("gwb", [128, 64], BF16, 9)
    t_bqk = ar.at("bqk", [128, 16], F32, 0, 1)
    t_bv = ar.at("bvt", [1, 1024], F32, 0, 1)
    t_bvb = ar.at("bvbt", [1, 1024], BF16, 0, 1)
    t_bp = ar.at("bpt", [128, 8], F32, 9)
    t_mb = ar.at("mbt", [128, 16], F32, 9)
    t_mask = ar.at("maskt", [128, 4096], BF16, 2)
    t_eps = ar.at("epst", [1, 1], F32, 9)
    t_cos = ar.at("cost", [128, T], F32, 0)
    t_sin = ar.at("sint", [128, T], F32, 0)
    t_xbf = ar.at("xbf", [128, NCH * T], BF16, 0, 1)
    t_xst = ar.at("xst", [128, 1024], F32, 0)
    t_wqst = ar.at("wqst", [128, 2048], F32, 0, 1)
    t_wqbf = ar.at("wqbf", [128, 4096], BF16, 0, 1)
    t_wvbf = ar.at("wvbf", [128, 8192], BF16, 1)
    t_ropeEh = ar.at("ropeEh", [128, 2 * T], F32, 0)
    t_ropeO = ar.at("ropeO", [128, 2048], F32, 0)
    t_m1 = ar.at("m1s", [128, 1024], F32, 0)
    t_m2 = ar.at("m2s", [128, 1024], F32, 0)
    t_m3 = ar.at("m3s", [128, 1024], F32, 0)
    t_m4 = ar.at("m4s", [128, 1024], F32, 0)
    t_rebf = ar.at("rebf", [128, 4096], BF16, 0, 1)
    t_robf = ar.at("robf", [128, 4096], BF16, 0, 1)
    t_qs = ar.at("qst", [128, NCH * T], BF16, 0, 2)
    t_ks = ar.at("kst", [128, NCH * T], BF16, 0, 2)
    t_vsb = ar.at("vsbt", [128, W_VSB], BF16, 1, 2)
    t_sqbf = ar.at("sqbf", [128, 1024], BF16, 1)
    t_mq2 = ar.at("mq2", [2, 8], F32, 1)
    t_mk2 = ar.at("mk2", [2, 8], F32, 1)
    t_msr = ar.at("msr", [2, 8], F32, 1)
    t_pab = ar.at("pab", [128, 3072], BF16, 2)
    t_sumr = ar.at("sumr", [1, 2048], F32, 2)
    t_sumrb = ar.at("sumrb", [1, 1024], BF16, 2)
    t_bcs = ar.at("bcs", [128, 1024], F32, 2)
    t_attnB = ar.at("attnB", [64, 1024], BF16, 2)
    t_attn = ar.at("attnt", [128, NCH * T], BF16, 2, 3)
    t_wpst = ar.at("wpst", [128, 2048], F32, 3)
    t_wpbf = ar.at("wpbf", [128, 2048], BF16, 3)
    t_xre = ar.at("xre", [128, 1024], F32, 3)
    t_r1 = ar.at("r1t", [128, NCH * T], F32, 3, 6)
    t_hbf = ar.at("hbft", [128, NCH * T], BF16, 3, 5)
    ln_sets = {}
    for ph, tag in ((3, "c"), (6, "f")):
        ln_sets[tag] = dict(
            sqsc=ar.at(f"sqsc{tag}", [128, 1024], F32, ph),
            sums=ar.at(f"sums{tag}", [1, T], F32, ph),
            abb=ar.at(f"abb{tag}", [1, 2 * T], BF16, ph),
            sqs=ar.at(f"sqst{tag}", [1, T], F32, ph),
            av=ar.at(f"avt{tag}", [1, T], F32, ph),
            bv2=ar.at(f"bv2t{tag}", [1, T], F32, ph),
            lnt=ar.at(f"lnt{tag}", [128, 1024], F32, ph),
            absb=ar.at(f"absb{tag}", [128, 1024], F32, ph))
    t_comb = ar.at("combt", [128, NTB * 8], F32, 4, 5)
    t_lgb = ar.at("lgbt", [8, 1024], F32, 4)
    t_cd = ar.at("cdt", [128, 8], F32, 4)
    t_ceq = ar.at("ceqt", [128, 8], F32, 4)
    t_cm1 = ar.at("cm1t", [128, 1], F32, 4)
    t_cm2 = ar.at("cm2t", [128, 1], F32, 4)
    t_cnum = ar.at("cnumt", [128, 8], F32, 4)
    t_cden = ar.at("cdent", [128, 1], F32, 4)
    t_wgust = ar.at("wgust", [128, 2048], F32, 5)
    t_wgubf = ar.at("wgubf", [128, 2 * 2048], BF16, 5)
    t_wdst = ar.at("wdst", [128, 2 * FF], F32, 5)
    t_wdbf = ar.at("wdbf", [128, 2 * FF], BF16, 5)
    t_su = ar.at("sut", [128, NFC * T], BF16, 5)
    t_gbuf = ar.at("gbuft", [128, 2 * T], BF16, 5)
    t_combsb = ar.at("combsbt", [128, T], BF16, 5)
    t_rowt = ar.at("rowtt", [1, 512], F32, 5)
    t_outst = ar.at("outst", [128, 1024], F32, 6)

    ctx = contextlib.ExitStack()
    with ctx:
        total_sbuf = ar.materialize(ctx)

        class BankView:
            def __init__(self, t, c0):
                self.t = t
                self.c0 = c0

            def __getitem__(self, idx):
                rows, cols = idx
                c0 = (cols.start or 0) + self.c0
                c1 = (512 if cols.stop is None else cols.stop) + self.c0
                return self.t[rows, c0:c1]

        psw = [ctx.enter_context(nc.psum_tensor(f"pw{i}", [128, 1024], F32))
               for i in range(4)]
        psb = [BankView(psw[i // 2], (i % 2) * 512) for i in range(8)]
        S = {nm: Ctr(ctx.enter_context(nc.semaphore(nm)), nm)
             for nm in ("dq_x0", "dq_x1", "dq_w0", "dq_w1", "dq_wd0", "dq_wd1", "dq_wgu", "dq_cs",
                        "dq_sh0", "dq_sh1", "dq_o0", "dq_o1", "dq_pm0", "dq_pm1", "dq_pm2", "dq_pm3", "dq_dbg", "dq_init",
                        "c_t", "c_s", "c_v", "c_p")}
        dq_x = [S["dq_x0"], S["dq_x1"]]
        dq_w = [S["dq_w0"], S["dq_w1"]]
        dq_wd = [S["dq_wd0"], S["dq_wd1"]]
        dq_wgu, dq_cs, dq_dbg, dq_init = (S["dq_wgu"], S["dq_cs"], S["dq_dbg"], S["dq_init"])
        dq_sh = [S["dq_sh0"], S["dq_sh1"]]
        dq_o = [S["dq_o0"], S["dq_o1"]]
        dq_pm = [S["dq_pm0"], S["dq_pm1"]]
        dq_pmh = [S["dq_pm2"], S["dq_pm3"]]
        c_t, c_s, c_v, c_p = (S[k] for k in ("c_t", "c_s", "c_v", "c_p"))
        bank_free = [(None, 0)] * 8

        def bwait(P, i, eng="tensor"):
            ctr, v = bank_free[i]
            if ctr is not None:
                P.wait(eng, ctr, v)

        out_cnt = {0: 0, 1: 0}

        def emit_ln(P, src_is_r1=True, sink="hbf"):
            sq_hist = []
            lnt_ctr = {}
            ab_used = [None]
            t_sqsc = ln_sets["c" if sink == "hbf" else "f"]["sqsc"]
            t_sums = ln_sets["c" if sink == "hbf" else "f"]["sums"]
            t_sqs = ln_sets["c" if sink == "hbf" else "f"]["sqs"]
            t_av = ln_sets["c" if sink == "hbf" else "f"]["av"]
            t_bv2 = ln_sets["c" if sink == "hbf" else "f"]["bv2"]
            t_lnt = ln_sets["c" if sink == "hbf" else "f"]["lnt"]
            t_absb = ln_sets["c" if sink == "hbf" else "f"]["absb"]
            t_abb = ln_sets["c" if sink == "hbf" else "f"]["abb"]
            """LayerNorm over d of t_r1 (in place), stats via fp32 matmuls.
            sink: 'hbf' (phase C: write normalized into r1 + cast to hbf)
                  'out' (phase F: write normalized to outstage + DMA out)."""
            stat_bank_v = []
            for tt in range(TT):
                bk = tt % 2
                bk2 = 2 + tt % 2
                bwait(P, bk)
                bwait(P, bk2)
                for oc in range(NCH):
                    sl = oc % 2
                    sq_eng = "scalar" if oc % 2 == 0 else "gpsimd"
                    sq_ctr = c_s if oc % 2 == 0 else c_p
                    rw = r1_written[(oc, tt)]
                    rw_ctr, rw_v = rw if isinstance(rw, tuple) else (c_v, rw)
                    P.wait(sq_eng, rw_ctr, rw_v)
                    if len(sq_hist) >= 2:
                        P.wait(sq_eng, c_t, sq_hist[-2])
                    rsl_ = lambda q, oc=oc, tt=tt: t_r1[:, oc * T + tt * 512: oc * T + (tt + 1) * 512]
                    if oc % 2 == 0:
                        vs = P.op("scalar", c_s, lambda q, oc=oc, tt=tt, sl=sl:
                                  q.activation(out=t_sqsc[:, sl * 512:(sl + 1) * 512],
                                               in_=t_r1[:, oc * T + tt * 512: oc * T + (tt + 1) * 512],
                                               func=AF.Square))
                    else:
                        vs = P.op("gpsimd", c_p, lambda q, oc=oc, tt=tt, sl=sl:
                                  q.tensor_mul(out=t_sqsc[:, sl * 512:(sl + 1) * 512],
                                               in0=t_r1[:, oc * T + tt * 512: oc * T + (tt + 1) * 512],
                                               in1=t_r1[:, oc * T + tt * 512: oc * T + (tt + 1) * 512]))
                    P.wait("tensor", rw_ctr, rw_v)
                    P.wait("tensor", sq_ctr, vs)
                    P.opn("tensor", lambda q, bk=bk, oc=oc, tt=tt:
                          q.matmul(psb[bk][0:1, :], t_onescol[:, :],
                                   t_r1[:, oc * T + tt * 512: oc * T + (tt + 1) * 512],
                                   start=(oc == 0), stop=(oc == NCH - 1)))
                    vts = P.op("tensor", c_t, lambda q, bk2=bk2, sl=sl, oc=oc:
                               q.matmul(psb[bk2][0:1, :], t_onescol[:, :],
                                        t_sqsc[:, sl * 512:(sl + 1) * 512],
                                        start=(oc == 0), stop=(oc == NCH - 1)))
                    sq_hist.append(vts)
                P.wait("scalar", c_t, c_t.n)
                P.op("scalar", c_s, lambda q, bk=bk, tt=tt:
                     q.copy(out=t_sums[0:1, tt * 512:(tt + 1) * 512], in_=psb[bk][0:1, :]))
                v = P.op("scalar", c_s, lambda q, bk2=bk2, tt=tt:
                         q.copy(out=t_sqs[0:1, tt * 512:(tt + 1) * 512], in_=psb[bk2][0:1, :]))
                bank_free[bk] = (c_s, v)
                bank_free[bk2] = (c_s, v)
                stat_bank_v.append(v)
            P.wait("vector", c_s, max(stat_bank_v))
            P.op("vector", c_v, lambda q: q.tensor_scalar_mul(out=t_av[0:1, :], in0=t_sums[0:1, :],
                                                              scalar1=1.0 / 1024.0))
            P.wait("vector", c_v, c_v.n)
            P.op("vector", c_v, lambda q: q.tensor_mul(out=t_bv2[0:1, :], in0=t_av[0:1, :],
                                                       in1=t_av[0:1, :]))
            P.wait("vector", c_v, c_v.n)
            P.op("vector", c_v, lambda q: q.scalar_tensor_tensor(
                out=t_bv2[0:1, :], in0=t_sqs[0:1, :], scalar=1.0 / 1024.0, in1=t_bv2[0:1, :],
                op0=ALU.mult, op1=ALU.subtract))
            P.wait("scalar", c_v, c_v.n)
            P.op("scalar", c_s, lambda q: q.activation(out=t_sqs[0:1, :], in_=t_bv2[0:1, :],
                                                       func=AF.Sqrt, bias=t_eps[0:1, 0:1]))
            P.wait("scalar", c_s, c_s.n)
            P.op("scalar", c_s, lambda q: q.activation(out=t_bv2[0:1, :], in_=t_sqs[0:1, :],
                                                       func=AF.Ln))
            P.wait("scalar", c_s, c_s.n)
            P.op("scalar", c_s, lambda q: q.activation(out=t_sums[0:1, :], in_=t_bv2[0:1, :],
                                                       func=AF.Exp, scale=-1.0))
            rstd_v = c_s.n
            P.wait("vector", c_s, rstd_v)
            P.op("vector", c_v, lambda q: q.scalar_tensor_tensor(
                out=t_bv2[0:1, :], in0=t_av[0:1, :], scalar=-1.0, in1=t_sums[0:1, :],
                op0=ALU.mult, op1=ALU.mult))
            P.wait("vector", c_v, c_v.n)
            v_ab = P.op("vector", c_v, lambda q: q.tensor_copy(out=t_av[0:1, :], in_=t_sums[0:1, :]))
            P.wait("scalar", c_v, v_ab)
            P.op("scalar", c_s, lambda q: q.copy(out=t_abb[0:1, 0:T], in_=t_av[0:1, :]))
            v_abb = P.op("scalar", c_s, lambda q: q.copy(out=t_abb[0:1, T:2 * T], in_=t_bv2[0:1, :]))
            for tt in range(TT):
                for h_i in range(2):
                    bk = 4 + h_i
                    bwait(P, bk)
                    P.wait("tensor", c_s, v_abb)
                    P.op("tensor", c_t, lambda q, bk=bk, h_i=h_i, tt=tt:
                         q.matmul(psb[bk][:, :], t_onesb[0:1, :],
                                  t_abb[0:1, h_i * T + tt * 512: h_i * T + (tt + 1) * 512],
                                  start=True, stop=True))
                bc_v = c_t.n
                P.wait("vector", c_t, bc_v)
                if ab_used[0]:
                    P.wait("vector", c_v, ab_used[0][0])
                    P.wait("vector", c_p, ab_used[0][1])
                P.op("vector", c_v, lambda q: q.tensor_copy(out=t_absb[:, 0:512], in_=psb[4][:, :]))
                vab_s = P.op("vector", c_v, lambda q: q.tensor_copy(out=t_absb[:, 512:1024],
                                                                    in_=psb[5][:, :]))
                last_v = 0
                for oc in range(NCH):
                    sl = oc % 2
                    a_eng = "vector" if oc % 2 == 0 else "gpsimd"
                    a_ctr = c_v if oc % 2 == 0 else c_p
                    P.wait(a_eng, c_v, vab_s)
                    if oc >= 2:
                        P.wait(a_eng, lnt_ctr.get((tt, oc - 2), c_v), lnt_used.get((tt, oc - 2), 0))
                    P.op(a_eng, a_ctr, lambda q, oc=oc, tt=tt, sl=oc % 2:
                         q.tensor_mul(out=t_lnt[:, sl * 512:(sl + 1) * 512],
                                      in0=t_r1[:, oc * T + tt * 512: oc * T + (tt + 1) * 512],
                                      in1=t_absb[:, 0:512]))
                    P.wait(a_eng, a_ctr, a_ctr.n)
                    if sink == "hbf":
                        v2 = P.op(a_eng, a_ctr, lambda q, oc=oc, tt=tt, sl=oc % 2:
                                  q.tensor_add(out=t_r1[:, oc * T + tt * 512: oc * T + (tt + 1) * 512],
                                               in0=t_lnt[:, sl * 512:(sl + 1) * 512],
                                               in1=t_absb[:, 512:1024]))
                        lnt_used[(tt, oc)] = v2
                        lnt_ctr[(tt, oc)] = a_ctr
                        P.wait("scalar", a_ctr, v2)
                        vh = P.op("scalar", c_s, lambda q, oc=oc, tt=tt:
                                  q.copy(out=t_hbf[:, oc * T + tt * 512: oc * T + (tt + 1) * 512],
                                         in_=t_r1[:, oc * T + tt * 512: oc * T + (tt + 1) * 512]))
                        hbf_cast[(oc, tt)] = vh
                        r1_final[(oc, tt)] = (a_ctr, v2)
                        last_v = v2
                    else:
                        osl = (oc % 2)
                        if out_cnt[osl] > 0:
                            P.wait(a_eng, dq_o[osl], 16 * out_cnt[osl])
                        v2 = P.op(a_eng, a_ctr, lambda q, oc=oc, osl=osl, sl=oc % 2:
                                  q.tensor_add(out=t_outst[:, osl * 512:(osl + 1) * 512],
                                               in0=t_lnt[:, sl * 512:(sl + 1) * 512],
                                               in1=t_absb[:, 512:1024]))
                        lnt_used[(tt, oc)] = v2
                        lnt_ctr[(tt, oc)] = a_ctr
                        P.wait("scalar", a_ctr, v2)
                        P.op("scalar", dq_o[osl], lambda q, oc=oc, tt=tt, osl=osl:
                             q.dma_start(out=outT[oc * 128:(oc + 1) * 128,
                                                  b * T + tt * 512: b * T + (tt + 1) * 512],
                                         in_=t_outst[:, osl * 512:(osl + 1) * 512]), 16)
                        out_cnt[osl] += 1
                        last_v = v2
                ab_used[0] = (max(lnt_used[(tt, oc)] for oc in range(NCH)
                                  if lnt_ctr[(tt, oc)] is c_v),
                              max(lnt_used[(tt, oc)] for oc in range(NCH)
                                  if lnt_ctr[(tt, oc)] is c_p))
                bank_free[4] = (c_v, vab_s)
                bank_free[5] = (c_v, vab_s)

        for b in range(n_b):
            if b == 0:
                P = Prog()
                for src, th in [(dr["cosT"], t_cos), (dr["sinT"], t_sin),
                                (dr["selE2"], t_selE), (dr["selO2"], t_selO),
                                (dr["hselc"], t_hsel), (dr["negsel"], t_negsel),
                                (dr["ident"], t_ident), (dr["onesf"], t_onesf),
                                (dr["onesb"], t_onesb), (dr["onescol"], t_onescol),
                                (dr["gw_t"], t_gw), (dr["bp_c"], t_bp),
                                (dr["identb"], t_identb)]:
                    P.op("sync", dq_init, lambda q, s=src, d=th:
                         q.dma_start(out=d[:, :], in_=s[:, :]), 16)
                P.wait("scalar", dq_init, dq_init.n)
                P.op("vector", c_v, lambda q: q.memset(t_eps[0:1, 0:1], 1e-5))
                P.op("scalar", c_s, lambda q: q.copy(out=t_gwb[:, :], in_=t_gw[:, :]))
                P.into_block(nc)

            P = Prog()
            if b > 0:
                for eng in ("sync", "gpsimd", "scalar", "vector"):
                    P.wait(eng, dq_o[0], dq_o[0].n)
                    P.wait(eng, dq_o[1], dq_o[1].n)
            P.op("sync", dq_cs, lambda q: q.dma_start(out=t_cos[:, :], in_=dr["cosT"][:, :]), 16)
            P.op("sync", dq_cs, lambda q: q.dma_start(out=t_sin[:, :], in_=dr["sinT"][:, :]), 16)
            P.op("sync", dq_cs, lambda q: q.dma_start(out=t_bqk[:, :], in_=dr["bqk_c"][:, :]), 16)
            P.op("sync", dq_cs, lambda q: q.dma_start(out=t_bv[:, :], in_=dr["bv_row"][:, :]), 16)
            cs_v = dq_cs.n
            P.wait("scalar", dq_cs, cs_v)
            bvb_v = P.op("scalar", c_s, lambda q: q.copy(out=t_bvb[:, :], in_=t_bv[:, :]))

            xcast_v = {}
            xhist = []
            for i, (tt, dc) in enumerate([(tt, dc) for tt in range(TT) for dc in range(NCH)]):
                if i >= 2:
                    P.wait("sync", c_p, xhist[i - 2])
                P.op("sync", dq_x[i % 2], lambda q, dc=dc, tt=tt, sl=i % 2:
                     q.dma_start(out=t_xst[:, sl * 512:(sl + 1) * 512],
                                 in_=xT[dc * 128:(dc + 1) * 128,
                                        b * T + tt * 512: b * T + (tt + 1) * 512]), 16)
                P.wait("gpsimd", dq_x[i % 2], dq_x[i % 2].n)
                v = P.op("gpsimd", c_p, lambda q, dc=dc, tt=tt, sl=i % 2:
                         q.tensor_copy(out=t_xbf[:, dc * T + tt * 512: dc * T + (tt + 1) * 512],
                                       in_=t_xst[:, sl * 512:(sl + 1) * 512]))
                xcast_v[(tt, dc)] = v
                xhist.append(v)
            xcast_tt = {tt: max(xcast_v[(tt, dc)] for dc in range(NCH)) for tt in range(TT)}

            wcast_v = {}
            qkv_mm_last = {}

            def emit_load_cast(s, preloaded=False):
                if not preloaded:
                    if s >= 2:
                        P.wait("sync", c_s, wcast_v[s - 2])
                    P.op("sync", dq_w[s % 2], lambda q, s=s, sl=(s % 2):
                         q.dma_start(out=t_wqst[:, sl * 1024:(sl + 1) * 1024],
                                     in_=dr["wq_t"][s, :, :]), 16)
                P.wait("scalar", dq_w[s % 2], dq_w[s % 2].n)
                if s < 16:
                    if s - 4 in qkv_mm_last:
                        P.wait("scalar", c_t, qkv_mm_last[s - 4])
                    v = P.op("scalar", c_s, lambda q, sl=(s % 2), bl=(s % 4):
                             q.copy(out=t_wqbf[:, bl * 1024:(bl + 1) * 1024],
                                    in_=t_wqst[:, sl * 1024:(sl + 1) * 1024]))
                else:
                    for dc in range(NCH):
                        v = P.op("scalar", c_s, lambda q, sl=(s % 2), dc=dc, j=s - 16:
                                 q.copy(out=t_wvbf[:, dc * 1024 + j * 128: dc * 1024 + (j + 1) * 128],
                                        in_=t_wqst[:, sl * 1024 + dc * 128: sl * 1024 + (dc + 1) * 128]))
                wcast_v[s] = v

            re_v, ro_v, evacE_v, evacO_v = {}, {}, {}, {}
            perm_dma_v = {}
            rope_prev = [0, 0]
            o_slot_last = {}

            def emit_rope(pair, c2):
                par = pair % 2
                csz = min(2, TT)
                tt0 = c2 * csz
                w = csz * 512
                rsl = par * 2048 + tt0 * 512
                eh = t_ropeEh[:, par * T + tt0 * 512: par * T + tt0 * 512 + w]
                osl = t_ropeO[:, tt0 * 512: tt0 * 512 + w]
                cosx = t_cos[:, tt0 * 512: tt0 * 512 + w]
                sinx = t_sin[:, tt0 * 512: tt0 * 512 + w]
                P.wait("vector", c_v, rope_prev[0])
                P.wait("vector", c_p, rope_prev[1])
                P.wait("gpsimd", c_v, rope_prev[0])
                P.wait("gpsimd", c_p, rope_prev[1])
                ev_e = max(evacE_v[(pair, tt0 + j)] for j in range(csz))
                ev_o = max(evacO_v[(pair, tt0 + j)] for j in range(csz))
                P.wait("vector", c_s, ev_e)
                P.wait("vector", dq_cs, cs_v)
                v_m1 = P.op("vector", c_v, lambda q, eh=eh, cosx=cosx:
                            q.tensor_mul(out=t_m1[:, 0:w], in0=eh, in1=cosx))
                P.wait("gpsimd", c_s, ev_o)
                P.wait("gpsimd", dq_cs, cs_v)
                v_m2 = P.op("gpsimd", c_p, lambda q, osl=osl, sinx=sinx:
                            q.tensor_mul(out=t_m2[:, 0:w], in0=osl, in1=sinx))
                v_m4 = P.op("gpsimd", c_p, lambda q, osl=osl, cosx=cosx:
                            q.tensor_mul(out=t_m4[:, 0:w], in0=osl, in1=cosx))
                pv = perm_dma_v.get(pair - 2)
                if pv is not None:
                    P.wait("vector", dq_pm[par], pv[0])
                    P.wait("vector", dq_pmh[par], pv[1])
                P.wait("vector", c_p, v_m2)
                P.wait("vector", c_v, v_m1)
                v_re = P.op("vector", c_v, lambda q, rsl=rsl: q.tensor_sub(
                    out=t_rebf[:, rsl:rsl + w], in0=t_m1[:, 0:w], in1=t_m2[:, 0:w]))
                v_m3 = P.op("vector", c_v, lambda q, eh=eh, sinx=sinx:
                            q.tensor_mul(out=t_m3[:, 0:w], in0=eh, in1=sinx))
                P.wait("gpsimd", c_v, v_m3)
                if pv is not None:
                    P.wait("gpsimd", dq_pm[par], pv[0])
                    P.wait("gpsimd", dq_pmh[par], pv[1])
                P.wait("gpsimd", c_p, v_m4)
                v_ro = P.op("gpsimd", c_p, lambda q, rsl=rsl: q.tensor_add(
                    out=t_robf[:, rsl:rsl + w], in0=t_m3[:, 0:w], in1=t_m4[:, 0:w]))
                for j in range(csz):
                    re_v[(pair, tt0 + j)] = (v_re, v_m3)
                    ro_v[(pair, tt0 + j)] = v_ro
                rope_prev[0] = v_re
                rope_prev[1] = v_ro

            def emit_qkv_slot(si):
                s = SLOTS_QK[si]
                pair = si // 2
                is_O = (si % 2 == 1)
                par = pair % 2
                for tt in range(TT):
                    bk = tt
                    bwait(P, bk)
                    P.wait("tensor", c_p, xcast_tt[tt])
                    P.wait("tensor", c_s, wcast_v[si])
                    builders = [
                        (lambda q, bk=bk, sl=si % 4, dc=dc, tt=tt:
                         q.matmul(psb[bk][:, :],
                                  t_wqbf[:, sl * 1024 + dc * 128: sl * 1024 + (dc + 1) * 128],
                                  t_xbf[:, dc * T + tt * 512: dc * T + (tt + 1) * 512],
                                  start=(dc == 0), stop=(dc == NCH - 1)))
                        for dc in range(NCH)]
                    mm_v = P.group("tensor", c_t, builders)
                    qkv_mm_last[si] = mm_v
                    P.wait("scalar", c_t, mm_v)
                    P.wait("scalar", dq_cs, cs_v)
                    if not is_O:
                        prevE = (pair - 2, tt)
                        if prevE in re_v:
                            P.wait("scalar", c_v, re_v[prevE][1])
                        v = P.op("scalar", c_s, lambda q, par=par, tt=tt, bk=bk, si=si:
                                 q.activation(out=t_ropeEh[:, par * T + tt * 512: par * T + (tt + 1) * 512],
                                              in_=psb[bk][:, :], func=AF.Identity,
                                              bias=t_bqk[:, si:si + 1], scale=1.0))
                        evacE_v[(pair, tt)] = v
                    else:
                        prevO = o_slot_last.get(tt % 4)
                        if prevO is not None:
                            P.wait("scalar", c_p, ro_v[prevO])
                        o_slot_last[tt % 4] = (pair, tt)
                        v = P.op("scalar", c_s, lambda q, tt=tt, bk=bk, si=si:
                                 q.activation(out=t_ropeO[:, (tt % 4) * 512: (tt % 4 + 1) * 512],
                                              in_=psb[bk][:, :], func=AF.Identity,
                                              bias=t_bqk[:, si:si + 1], scale=1.0))
                        evacO_v[(pair, tt)] = v
                    bank_free[bk] = (c_s, c_s.n)
                    if is_O and (TT == 1 or tt % 2 == 1):
                        emit_rope(pair, tt // min(2, TT))

            def emit_perm(pair):
                qk = 0 if pair < 4 else 1
                ccl = pair % 4
                dstt = t_qs if qk == 0 else t_ks
                par = pair % 2
                wall = TT * 512
                P.wait("scalar", c_v, max(re_v[(pair, tt)][0] for tt in range(TT)))
                P.wait("gpsimd", c_p, max(ro_v[(pair, tt)] for tt in range(TT)))
                for j in range(4):
                    cc = 2 * ccl + j // 2
                    rbase = 64 * (j % 2)
                    P.op("scalar", dq_pmh[par], lambda q, j=j, cc=cc, rbase=rbase, par=par, dstt=dstt, wall=wall:
                         q.dma_start(out=dstt[rbase:rbase + 32, cc * T: cc * T + wall],
                                     in_=t_rebf[32 * j:32 * j + 32, par * 2048: par * 2048 + wall]), 16)
                    P.op("gpsimd", dq_pm[par], lambda q, j=j, cc=cc, rbase=rbase, par=par, dstt=dstt, wall=wall:
                         q.dma_start(out=dstt[rbase + 32:rbase + 64, cc * T: cc * T + wall],
                                     in_=t_robf[32 * j:32 * j + 32, par * 2048: par * 2048 + wall]), 16)
                perm_dma_v[pair] = (dq_pm[par].n, dq_pmh[par].n)

            emit_load_cast(0)
            emit_load_cast(1)
            for si in range(16):
                if si + 2 < 16:
                    emit_load_cast(si + 2)
                emit_qkv_slot(si)
                if si >= 3 and si % 2 == 1:
                    emit_perm((si - 3) // 2)
            emit_perm(7)
            P.wait("sync", c_s, wcast_v[14])
            P.op("sync", dq_w[0], lambda q: q.dma_start(out=t_wqst[:, 0:1024],
                                                        in_=dr["wq_t"][16, :, :]), 16)
            P.wait("sync", c_s, wcast_v[15])
            P.op("sync", dq_w[1], lambda q: q.dma_start(out=t_wqst[:, 1024:2048],
                                                        in_=dr["wq_t"][17, :, :]), 16)
            P.into_block(nc)

            P = Prog()
            for s2 in range(16, 24):
                emit_load_cast(s2, preloaded=(s2 < 18))

            sq_mm_last = {}
            b0_prev = [0]

            def b0_unit(qk, cc):
                srct = t_qs if qk == 0 else t_ks
                m2t = t_mq2 if qk == 0 else t_mk2
                pair = 4 * qk + cc // 2
                pv = perm_dma_v[pair]
                for tt in range(TT):
                    sl = tt % 2
                    P.wait("scalar", dq_pm[pair % 2], pv[0])
                    P.wait("scalar", dq_pmh[pair % 2], pv[1])
                    if sq_mm_last.get(sl):
                        P.wait("scalar", c_t, sq_mm_last[sl])
                    vs = P.op("scalar", c_s, lambda q, srct=srct, cc=cc, tt=tt, sl=sl:
                              q.activation(out=t_sqbf[:, sl * 512:(sl + 1) * 512],
                                           in_=srct[:, cc * T + tt * 512: cc * T + (tt + 1) * 512],
                                           func=AF.Square))
                    bk = tt % 2
                    bwait(P, bk)
                    P.wait("tensor", c_s, vs)
                    vm = P.op("tensor", c_t, lambda q, bk=bk, sl=sl:
                              q.matmul(psb[bk][0:2, :], t_hsel[:, :],
                                       t_sqbf[:, sl * 512:(sl + 1) * 512],
                                       start=True, stop=True))
                    sq_mm_last[sl] = vm
                    P.wait("vector", c_t, vm)
                    P.wait("vector", c_v, b0_prev[0])
                    vr = P.op("vector", c_v, lambda q, bk=bk:
                              q.reduce_max(out=t_msr[0:2, 0:1], in_=psb[bk][0:2, :], axis=AX.X))
                    bank_free[bk] = (c_v, vr)
                    P.wait("vector", c_v, vr)
                    if tt == 0:
                        vmax = P.op("vector", c_v, lambda q, m2t=m2t, cc=cc:
                                    q.tensor_copy(out=m2t[0:2, cc:cc + 1], in_=t_msr[0:2, 0:1]))
                    else:
                        vmax = P.op("vector", c_v, lambda q, m2t=m2t, cc=cc:
                                    q.tensor_max(out=m2t[0:2, cc:cc + 1],
                                                 in0=m2t[0:2, cc:cc + 1],
                                                 in1=t_msr[0:2, 0:1]))
                    b0_prev[0] = vmax

            b0_units = [(qk, cc) for qk in range(2) for cc in range(NCH)]

            vones_v = P.op("gpsimd", c_p, lambda q: q.memset(
                bass.AP(t_vsb, 64, [[W_VSB, 128], [1152, NTB], [72, 16], [1, 1]]), 1.0))
            wv_all = wcast_v[23]
            for i, (tb, dvt) in enumerate([(tb, dvt) for tb in range(NTB) for dvt in range(2)]):
                bk = 4 + (i % 4)
                bwait(P, bk)
                P.wait("tensor", c_p, max(xcast_tt[tb // 4], vones_v))
                P.wait("tensor", c_s, max(wv_all, bvb_v))
                builders = [
                    (lambda q, bk=bk, dc=dc, tb=tb, dvt=dvt:
                     q.matmul(psb[bk][:, :],
                              t_xbf[:, dc * T + tb * 128: dc * T + (tb + 1) * 128],
                              t_wvbf[:, dc * 1024 + dvt * 512: dc * 1024 + (dvt + 1) * 512],
                              start=(dc == 0), stop=False))
                    for dc in range(NCH)]
                builders.append(lambda q, bk=bk, dvt=dvt:
                                q.matmul(psb[bk][:, :], t_onesb[:, :],
                                         t_bvb[0:1, dvt * 512:(dvt + 1) * 512],
                                         start=False, stop=True))
                vt = P.group("tensor", c_t, builders)
                P.wait("vector", c_t, vt)
                for j in range(8):
                    v = P.op("vector", c_v, lambda q, bk=bk, tb=tb, dvt=dvt, j=j:
                             q.tensor_copy(out=bass.AP(t_vsb,
                                                       1152 * tb + 72 * (8 * dvt + j),
                                                       [[W_VSB, 128], [1, 64]]),
                                           in_=psb[bk][:, j * 64:(j + 1) * 64]))
                bank_free[bk] = (c_v, v)
                if i % 2 == 1 and b0_units:
                    b0_unit(*b0_units.pop(0))
            while b0_units:
                b0_unit(*b0_units.pop(0))
            P.wait("vector", c_v, c_v.n)
            P.op("vector", c_v, lambda q: q.tensor_mul(out=t_msr[0:2, :], in0=t_mq2[0:2, :],
                                                       in1=t_mk2[0:2, :]))
            P.wait("scalar", c_v, c_v.n)
            P.op("scalar", c_s, lambda q: q.activation(out=t_mq2[0:2, :], in_=t_msr[0:2, :],
                                                       func=AF.Sqrt, scale=1.0 / 64.0))
            sqrt_v = c_s.n
            for cc in range(NCH):
                bk = cc % 2
                bwait(P, bk)
                P.wait("tensor", c_s, sqrt_v)
                P.op("tensor", c_t, lambda q, bk=bk, cc=cc:
                     q.matmul(psb[bk][0:128, 0:1],
                              t_negsel[0:2, 256:384],
                              t_mq2[0:2, cc:cc + 1], start=True, stop=True))
                P.wait("scalar", c_t, c_t.n)
                v = P.op("scalar", c_s, lambda q, bk=bk, cc=cc:
                         q.copy(out=t_mb[:, cc:cc + 1], in_=psb[bk][0:128, 0:1]))
                bank_free[bk] = (c_s, v)
            mb_v = c_s.n
            if debug and b == 0:
                for ctr in (c_t, c_s, c_v, c_p):
                    P.wait("gpsimd", ctr, ctr.n)
                for nm, th in [("d_qs", t_qs), ("d_ks", t_ks), ("d_vsb", t_vsb)]:
                    P.op("gpsimd", dq_dbg, lambda q, nm=nm, th=th:
                         q.dma_start(out=dbg[nm][:, :], in_=th[:, :]), 16)
                P.wait("gpsimd", dq_dbg, dq_dbg.n)
            P.into_block(nc)

            P = Prog()
            for eng in ("tensor", "scalar", "gpsimd", "vector", "sync"):
                P.wait(eng, dq_pm[0], dq_pm[0].n)
                P.wait(eng, dq_pm[1], dq_pm[1].n)
                P.wait(eng, dq_pmh[0], dq_pmh[0].n)
                P.wait(eng, dq_pmh[1], dq_pmh[1].n)
            for i in range(4):
                P.op("sync", dq_cs, lambda q, i=i:
                     q.dma_start(out=t_mask[:, i * 1024:(i + 1) * 1024],
                                 in_=dr["maskT"][i, :, :]), 16)
            mask_v = dq_cs.n
            pass

            prev_bc = [0]
            prev_vb = [0]
            pending_fin = [None]
            flash_g = [0]
            pv_gv = {}
            for cc in range(NCH):
                hA, hB = 2 * cc, 2 * cc + 1
                for qt in range(TT):
                    nkb = min(4 * qt + 4, NTB)
                    pv_mm_v = {}
                    exp_done = {}

                    def emit_scores(kb):
                        for h_i in range(2):
                            sb = (kb % 2) * 2 + h_i
                            base = 64 * h_i
                            bwait(P, sb)
                            mm_v = P.op("tensor", c_t, lambda q, sb=sb, base=base, cc=cc, kb=kb, qt=qt:
                                        q.matmul(psb[sb][:, :],
                                                 t_ks[base:base + 64,
                                                      cc * T + kb * 128: cc * T + (kb + 1) * 128],
                                                 t_qs[base:base + 64,
                                                      cc * T + qt * 512: cc * T + (qt + 1) * 512],
                                                 start=True, stop=True))
                        g = flash_g[0]
                        flash_g[0] += 1
                        poff = (g % 3) * 1024
                        P.wait("scalar", c_t, mm_v)
                        P.wait("scalar", c_s, mb_v)
                        if g - 3 in pv_gv:
                            P.wait("scalar", c_t, pv_gv[g - 3])
                        v_exp = P.op("scalar", c_s, lambda q, kb=kb, poff=poff, cc=cc:
                                     q.activation(out=t_pab[:, poff:poff + 1024],
                                                  in_=psw[kb % 2][:, :], func=AF.Exp,
                                                  bias=t_mb[:, cc:cc + 1], scale=0.125))
                        bank_free[(kb % 2) * 2] = (c_s, v_exp)
                        bank_free[(kb % 2) * 2 + 1] = (c_s, v_exp)
                        straddle = kb >= 4 * qt
                        if straddle:
                            di = kb - 4 * qt
                            P.wait("gpsimd", dq_cs, mask_v)
                            P.wait("gpsimd", c_s, v_exp)
                            v_exp = P.op("gpsimd", c_p, lambda q, poff=poff, di=di:
                                         q.tensor_mul(out=t_pab[:, poff:poff + 1024],
                                                      in0=t_pab[:, poff:poff + 1024],
                                                      in1=t_mask[:, di * 1024:(di + 1) * 1024]))
                        exp_done[(kb, 0)] = (g, v_exp, straddle)
                        exp_done[(kb, 1)] = (g, v_exp, straddle)

                    def emit_pv(kb, exp_done=exp_done, pv_mm_v=pv_mm_v,
                                nkb=nkb, hA=hA, hB=hB, cc=cc, qt=qt):
                        g, vv, straddle = exp_done[(kb, 0)]
                        for h_i in range(2):
                            hh = hA if h_i == 0 else hB
                            poff = (g % 3) * 1024 + h_i * 512
                            pvb = 4 + h_i
                            if kb == 0:
                                bwait(P, pvb)
                            P.wait("tensor", c_p if straddle else c_s, vv)
                            vpv = P.op("tensor", c_t, lambda q, pvb=pvb, kb=kb, hh=hh, poff=poff,
                                       first=(kb == 0), last=(kb == nkb - 1):
                                       q.matmul(psb[pvb][0:65, :],
                                                bass.AP(t_vsb, 1152 * kb + 72 * hh,
                                                        [[W_VSB, 128], [1, 65]]),
                                                t_pab[:, poff:poff + 512],
                                                start=first, stop=last))
                            pv_mm_v[(kb, h_i)] = vpv

                    emit_scores(0)
                    emit_scores(1)
                    if pending_fin[0] is not None:
                        pending_fin[0]()
                        pending_fin[0] = None
                    for kb in range(2, nkb):
                        emit_scores(kb)
                        emit_pv(kb - 2)
                    emit_pv(nkb - 2)

                    def tile_tail(cc=cc, qt=qt, nkb=nkb, emit_pv=emit_pv,
                                  exp_done=exp_done, pv_mm_v=pv_mm_v):
                        emit_pv(nkb - 1)
                        return c_t.n

                    pv_done = None

                    def finalize(cc=cc, qt=qt, tile_tail=tile_tail):
                        pv_done = tile_tail()
                        P.wait("scalar", c_t, pv_done)
                        P.wait("scalar", c_t, prev_bc[0])
                        P.op("scalar", c_s, lambda q: q.copy(out=t_sumr[0:1, 0:512],
                                                             in_=psb[4][64:65, :]))
                        P.op("scalar", c_s, lambda q: q.copy(out=t_sumr[0:1, 512:1024],
                                                             in_=psb[5][64:65, :]))
                        P.wait("scalar", c_s, c_s.n)
                        P.op("scalar", c_s, lambda q: q.activation(out=t_sumr[0:1, 1024:2048],
                                                                   in_=t_sumr[0:1, 0:1024],
                                                                   func=AF.Ln))
                        P.wait("scalar", c_s, c_s.n)
                        v_rec = P.op("scalar", c_s, lambda q: q.activation(
                            out=t_sumrb[0:1, 0:1024], in_=t_sumr[0:1, 1024:2048],
                            func=AF.Exp, scale=-1.0))
                        for h_i in range(2):
                            bk = 6 + h_i
                            bwait(P, bk)
                            P.wait("tensor", c_s, v_rec)
                            P.op("tensor", c_t, lambda q, bk=bk, h_i=h_i:
                                 q.matmul(psb[bk][0:64, :], t_onesb[0:1, 0:64],
                                          t_sumrb[0:1, h_i * 512:(h_i + 1) * 512],
                                          start=True, stop=True))
                        bc_v = c_t.n
                        prev_bc[0] = bc_v
                        P.wait("vector", c_t, bc_v)
                        P.wait("vector", c_v, prev_vb[0])
                        for h_i in range(2):
                            P.op("vector", c_v, lambda q, h_i=h_i:
                                 q.tensor_copy(out=t_bcs[0:64, h_i * 512:(h_i + 1) * 512],
                                               in_=psb[6 + h_i][0:64, :]))
                        vbc = c_v.n
                        bank_free[6] = (c_v, vbc)
                        bank_free[7] = (c_v, vbc)
                        P.wait("vector", c_v, vbc)
                        vA = P.op("vector", c_v, lambda q, cc=cc, qt=qt:
                                  q.tensor_mul(out=t_attn[0:64, cc * T + qt * 512:
                                                          cc * T + (qt + 1) * 512],
                                               in0=psb[4][0:64, :], in1=t_bcs[0:64, 0:512]))
                        if dq_sh[qt % 2].n > 0:
                            P.wait("vector", dq_sh[qt % 2], dq_sh[qt % 2].n)
                        vB = P.op("vector", c_v, lambda q, sl=qt % 2:
                                  q.tensor_mul(out=t_attnB[0:64, sl * 512:(sl + 1) * 512],
                                               in0=psb[5][0:64, :], in1=t_bcs[0:64, 512:1024]))
                        bank_free[4] = (c_v, vA)
                        bank_free[5] = (c_v, vB)
                        prev_vb[0] = vB
                        P.wait("scalar", c_v, vB)
                        P.op("scalar", dq_sh[qt % 2], lambda q, cc=cc, qt=qt, sl=qt % 2:
                             q.dma_start(out=t_attn[64:128, cc * T + qt * 512:
                                                    cc * T + (qt + 1) * 512],
                                         in_=t_attnB[0:64, sl * 512:(sl + 1) * 512]), 16)
                    pending_fin[0] = finalize
            pending_fin[0]()
            pending_fin[0] = None
            if debug and b == 0:
                for ctr in (c_t, c_s, c_v, c_p):
                    P.wait("gpsimd", ctr, ctr.n)
                P.wait("gpsimd", dq_sh[0], dq_sh[0].n)
                P.wait("gpsimd", dq_sh[1], dq_sh[1].n)
                P.op("gpsimd", dq_dbg, lambda q: q.dma_start(out=dbg["d_attn"][:, :], in_=t_attn[:, :]), 16)
                P.op("gpsimd", dq_dbg, lambda q: q.dma_start(out=dbg["d_mb"][:, :], in_=t_mb[:, :]), 16)
                P.wait("gpsimd", dq_dbg, dq_dbg.n)
            P.into_block(nc)

            P = Prog()
            r1_written = {}
            hbf_cast = {}
            r1_final = {}
            sq_used = {}
            lnt_used = {}
            for eng in ("tensor", "sync", "scalar", "vector", "gpsimd"):
                P.wait(eng, dq_sh[0], dq_sh[0].n)
                P.wait(eng, dq_sh[1], dq_sh[1].n)
            wpcast_v = []
            xre_hist = []
            proj_mm_last = {}
            for oc in range(NCH):
                if oc >= 2:
                    P.wait("sync", c_s, wpcast_v[oc - 2])
                P.op("sync", dq_w[oc % 2], lambda q, oc=oc, sl=oc % 2:
                     q.dma_start(out=t_wpst[:, sl * 1024:(sl + 1) * 1024],
                                 in_=dr["wp_t"][oc, :, :]), 16)
                P.wait("scalar", dq_w[oc % 2], dq_w[oc % 2].n)
                if oc - 2 in proj_mm_last:
                    P.wait("scalar", c_t, proj_mm_last[oc - 2])
                vw = P.op("scalar", c_s, lambda q, sl=oc % 2:
                          q.copy(out=t_wpbf[:, sl * 1024:(sl + 1) * 1024],
                                 in_=t_wpst[:, sl * 1024:(sl + 1) * 1024]))
                wpcast_v.append(vw)
                for tt in range(TT):
                    i = oc * TT + tt
                    if i >= 2:
                        P.wait("sync", c_v, xre_hist[i - 2])
                    P.op("sync", dq_x[i % 2], lambda q, oc=oc, tt=tt, sl=i % 2:
                         q.dma_start(out=t_xre[:, sl * 512:(sl + 1) * 512],
                                     in_=xT[oc * 128:(oc + 1) * 128,
                                            b * T + tt * 512: b * T + (tt + 1) * 512]), 16)
                    xre_dma = (i % 2, dq_x[i % 2].n)
                    bk = (oc % 2) * 4 + tt
                    bwait(P, bk)
                    P.wait("tensor", c_s, vw)
                    builders = [
                        (lambda q, bk=bk, sl=oc % 2, dc=dc, tt=tt:
                         q.matmul(psb[bk][:, :],
                                  t_wpbf[:, sl * 1024 + dc * 128: sl * 1024 + (dc + 1) * 128],
                                  t_attn[:, dc * T + tt * 512: dc * T + (tt + 1) * 512],
                                  start=(dc == 0), stop=(dc == NCH - 1)))
                        for dc in range(NCH)]
                    vt = P.group("tensor", c_t, builders)
                    proj_mm_last[oc] = vt
                    P.wait("vector", c_t, vt)
                    P.wait("vector", dq_x[xre_dma[0]], xre_dma[1])
                    vv = P.op("vector", c_v, lambda q, bk=bk, oc=oc, tt=tt, sl=i % 2:
                              q.scalar_tensor_tensor(
                                  out=t_r1[:, oc * T + tt * 512: oc * T + (tt + 1) * 512],
                                  in0=psb[bk][:, :], scalar=t_bp[:, oc:oc + 1],
                                  in1=t_xre[:, sl * 512:(sl + 1) * 512],
                                  op0=ALU.add, op1=ALU.add))
                    r1_written[(oc, tt)] = vv
                    xre_hist.append(vv)
                    bank_free[bk] = (c_v, vv)
            emit_ln(P, sink="hbf")
            if debug and b == 0:
                for ctr in (c_t, c_s, c_v, c_p):
                    P.wait("gpsimd", ctr, ctr.n)
                P.op("gpsimd", dq_dbg, lambda q: q.dma_start(out=dbg["d_r1"][:, :], in_=t_r1[:, :]), 16)
                P.op("gpsimd", dq_dbg, lambda q: q.dma_start(out=dbg["d_h"][:, :], in_=t_hbf[:, :]), 16)
                P.wait("gpsimd", dq_dbg, dq_dbg.n)
            P.into_block(nc)

            P = Prog()
            prev_comb = [0]
            lg_mm = {}
            for tt in range(TT):
                bk = 4 + tt % 2
                bwait(P, bk)
                builders = [
                    (lambda q, bk=bk, dc=dc, tt=tt:
                     q.matmul(psb[bk][0:8, :], t_gwb[:, dc * 8:(dc + 1) * 8],
                              t_hbf[:, dc * T + tt * 512: dc * T + (tt + 1) * 512],
                              start=(dc == 0), stop=(dc == NCH - 1)))
                    for dc in range(NCH)]
                vt = P.group("tensor", c_t, builders)
                P.wait("scalar", c_t, vt)
                if tt >= 2:
                    P.wait("scalar", c_t, lg_mm.get(tt - 2, 0))
                vlg = P.op("scalar", c_s, lambda q, bk=bk, sl=tt % 2:
                           q.copy(out=t_lgb[0:8, sl * 512:(sl + 1) * 512], in_=psb[bk][0:8, :]))
                bank_free[bk] = (c_s, vlg)
                for i in range(4):
                    tb = tt * 4 + i
                    bk2 = tb % 2
                    bwait(P, bk2)
                    P.wait("tensor", c_s, vlg)
                    vtr = P.op("tensor", c_t, lambda q, bk2=bk2, sl=tt % 2, i=i:
                               q.transpose(psb[bk2][0:128, 0:8],
                                           t_lgb[0:8, sl * 512 + i * 128: sl * 512 + (i + 1) * 128],
                                           t_ident[0:8, 0:8]))
                    lg_mm[tt] = vtr
                    P.wait("vector", c_t, vtr)
                    P.wait("vector", c_v, prev_comb[0])
                    P.op("vector", c_v, lambda q, bk2=bk2: q.reduce_max(
                        out=t_cm1[:, :], in_=psb[bk2][0:128, 0:8], axis=AX.X))
                    P.wait("vector", c_v, c_v.n)
                    vd = P.op("vector", c_v, lambda q, bk2=bk2: q.tensor_scalar_sub(
                        out=t_cd[:, :], in0=psb[bk2][0:128, 0:8], scalar1=t_cm1[:, 0:1]))
                    bank_free[bk2] = (c_v, vd)
                    P.wait("vector", c_v, vd)
                    P.op("vector", c_v, lambda q: q.tensor_single_scalar(
                        out=t_ceq[:, :], in_=t_cd[:, :], scalar=0.0, op=ALU.is_equal))
                    P.wait("vector", c_v, c_v.n)
                    P.op("vector", c_v, lambda q: q.scalar_tensor_tensor(
                        out=t_ceq[:, :], in0=t_ceq[:, :], scalar=-1e30, in1=t_cd[:, :],
                        op0=ALU.mult, op1=ALU.add))
                    P.wait("vector", c_v, c_v.n)
                    P.op("vector", c_v, lambda q: q.reduce_max(out=t_cm2[:, :], in_=t_ceq[:, :],
                                                               axis=AX.X))
                    P.wait("vector", c_v, c_v.n)
                    vk = P.op("vector", c_v, lambda q: q.tensor_single_scalar(
                        out=t_ceq[:, :], in_=t_cd[:, :], scalar=t_cm2[:, 0:1], op=ALU.is_ge))
                    P.wait("scalar", c_v, vd)
                    ve = P.op("scalar", c_s, lambda q: q.activation(out=t_cnum[:, :],
                                                                   in_=t_cd[:, :], func=AF.Exp))
                    P.wait("vector", c_s, ve)
                    P.wait("vector", c_v, vk)
                    P.op("vector", c_v, lambda q: q.tensor_mul(out=t_cnum[:, :], in0=t_cnum[:, :],
                                                               in1=t_ceq[:, :]))
                    P.wait("vector", c_v, c_v.n)
                    P.op("vector", c_v, lambda q: q.reduce_sum(out=t_cden[:, :], in_=t_cnum[:, :],
                                                               axis=AX.X))
                    P.wait("vector", c_v, c_v.n)
                    P.op("vector", c_v, lambda q: q.reciprocal(out=t_cm1[:, :], in_=t_cden[:, :]))
                    P.wait("vector", c_v, c_v.n)
                    prev_comb[0] = P.op("vector", c_v, lambda q, tb=tb: q.tensor_scalar_mul(
                        out=t_comb[:, tb * 8:(tb + 1) * 8], in0=t_cnum[:, :],
                        scalar1=t_cm1[:, 0:1]))
            comb_done = c_v.n
            if debug and b == 0:
                for ctr in (c_t, c_s, c_v, c_p):
                    P.wait("gpsimd", ctr, ctr.n)
                P.op("gpsimd", dq_dbg, lambda q: q.dma_start(out=dbg["d_comb"][:, :], in_=t_comb[:, :]), 16)
                P.wait("gpsimd", dq_dbg, dq_dbg.n)
            P.into_block(nc)

            P = Prog()
            combsb_v = {}
            gcm_last = {}
            rowt_used = {}

            def emit_extract(e):
                prev_gcm = gcm_last.get(e - 1)
                for tt in range(TT):
                    evs = []
                    for i in range(4):
                        tb = tt * 4 + i
                        bwait(P, 0)
                        P.op("tensor", c_t, lambda q, tb=tb, e=e:
                             q.transpose(psb[0][0:1, 0:128],
                                         t_comb[:, tb * 8 + e: tb * 8 + e + 1],
                                         t_ident[:, :]))
                        P.wait("scalar", c_t, c_t.n)
                        if rowt_used.get(0):
                            P.wait("scalar", c_t, rowt_used[0])
                        v = P.op("scalar", c_s, lambda q, i=i:
                                 q.copy(out=t_rowt[0:1, i * 128:(i + 1) * 128],
                                        in_=psb[0][0:1, 0:128]))
                        evs.append(v)
                        bank_free[0] = (c_s, v)
                    bwait(P, 1)
                    P.wait("tensor", c_s, max(evs))
                    rowt_used[0] = P.op("tensor", c_t, lambda q:
                                        q.matmul(psb[1][:, :], t_onesf[0:1, :],
                                                 t_rowt[0:1, 0:512],
                                                 start=True, stop=True))
                    P.wait("vector", c_t, c_t.n)
                    if prev_gcm is not None:
                        P.wait("vector", c_p, prev_gcm)
                    v = P.op("vector", c_v, lambda q, e=e, tt=tt:
                             q.tensor_copy(out=t_combsb[:, tt * 512:(tt + 1) * 512],
                                           in_=psb[1][:, :]))
                    bank_free[1] = (c_v, v)
                combsb_v[e] = c_v.n

            emit_extract(0)
            su_done_v = 0
            gbuf_used = {}
            wgucast = []
            wdcast = []
            gu_mm_last = []
            dn_mm_last = []
            for ei in range(9):
                is_sh = (ei == 8)
                e_w = ei
                for fp in range(NFC):
                    if len(wgucast) >= 1:
                        P.wait("sync", c_s, wgucast[-1])
                    P.op("sync", dq_wgu, lambda q, e_w=e_w, fp=fp:
                         q.dma_start(out=t_wgust[:, :],
                                     in_=dr["wgu_t"][e_w, fp, :, :]), 16)
                    P.wait("scalar", dq_wgu, dq_wgu.n)
                    if len(gu_mm_last) >= 2:
                        P.wait("scalar", c_t, gu_mm_last[-2])
                    v = P.op("scalar", c_s, lambda q, sl=fp % 2:
                             q.copy(out=t_wgubf[:, sl * 2048:(sl + 1) * 2048],
                                    in_=t_wgust[:, :]))
                    wgucast.append(v)
                    wguv = v
                    for tt in range(TT):
                        bk = tt
                        bwait(P, bk)
                        P.wait("tensor", c_s, wguv)
                        if b == 0 and ei == 0:
                            pass
                        builders = [
                            (lambda q, bk=bk, sl=fp % 2, dc=dc, tt=tt:
                             q.matmul(psb[bk][:, :],
                                      t_wgubf[:, sl * 2048 + dc * 256: sl * 2048 + dc * 256 + 128],
                                      t_hbf[:, dc * T + tt * 512: dc * T + (tt + 1) * 512],
                                      start=(dc == 0), stop=(dc == NCH - 1)))
                            for dc in range(NCH)]
                        vg = P.group("tensor", c_t, builders)
                        P.wait("scalar", c_t, vg)
                        gu = gbuf_used.get((fp % 2, tt))
                        if gu is not None:
                            P.wait("scalar", c_v, gu)
                        vsil = P.op("scalar", c_s, lambda q, bk=bk, tt=tt, gsl=fp % 2:
                                    q.activation(out=t_gbuf[:, gsl * T + tt * 512: gsl * T + (tt + 1) * 512],
                                                 in_=psb[bk][:, :], func=AF.Silu))
                        bank_free[bk] = (c_s, vsil)
                        if not is_sh:
                            P.wait("gpsimd", c_s, vsil)
                            P.wait("gpsimd", c_v, combsb_v[ei])
                            vsil2 = P.op("gpsimd", c_p, lambda q, e=ei, tt=tt, gsl=fp % 2:
                                         q.tensor_mul(out=t_gbuf[:, gsl * T + tt * 512: gsl * T + (tt + 1) * 512],
                                                      in0=t_gbuf[:, gsl * T + tt * 512: gsl * T + (tt + 1) * 512],
                                                      in1=t_combsb[:, tt * 512:(tt + 1) * 512]))
                            gcm_last[ei] = vsil2
                        bku = 4 + tt
                        bwait(P, bku)
                        builders = [
                            (lambda q, bk=bku, sl=fp % 2, dc=dc, tt=tt:
                             q.matmul(psb[bk][:, :],
                                      t_wgubf[:, sl * 2048 + dc * 256 + 128: sl * 2048 + (dc + 1) * 256],
                                      t_hbf[:, dc * T + tt * 512: dc * T + (tt + 1) * 512],
                                      start=(dc == 0), stop=(dc == NCH - 1)))
                            for dc in range(NCH)]
                        vu = P.group("tensor", c_t, builders)
                        if tt == TT - 1:
                            gu_mm_last.append(vu)
                        P.wait("vector", c_t, vu)
                        if is_sh:
                            P.wait("vector", c_s, vsil)
                        else:
                            P.wait("vector", c_p, vsil2)
                        vsu = P.op("vector", c_v, lambda q, bk=bku, fp=fp, tt=tt, gsl=fp % 2:
                                   q.tensor_mul(out=t_su[:, fp * T + tt * 512: fp * T + (tt + 1) * 512],
                                                in0=psb[bk][:, :],
                                                in1=t_gbuf[:, gsl * T + tt * 512: gsl * T + (tt + 1) * 512]))
                        gbuf_used[(fp % 2, tt)] = vsu
                        bank_free[bku] = (c_v, vsu)
                su_done_v = c_v.n
                if ei < 7:
                    emit_extract(ei + 1)
                for dd in range(NCH):
                    if len(wdcast) >= 2:
                        P.wait("sync", c_p, wdcast[-2])
                    P.op("sync", dq_wd[dd % 2], lambda q, e_w=e_w, dd=dd, sl=dd % 2:
                         q.dma_start(out=t_wdst[:, sl * FF:(sl + 1) * FF],
                                     in_=dr["wd_t"][e_w, dd, :, :]), 16)
                    P.wait("gpsimd", dq_wd[dd % 2], dq_wd[dd % 2].n)
                    if len(dn_mm_last) >= 2:
                        P.wait("gpsimd", c_t, dn_mm_last[-2])
                    vwd = P.op("gpsimd", c_p, lambda q, sl=dd % 2:
                               q.tensor_copy(out=t_wdbf[:, sl * FF:(sl + 1) * FF],
                                             in_=t_wdst[:, sl * FF:(sl + 1) * FF]))
                    wdcast.append(vwd)
                    for tt in range(TT):
                        bk = (dd % 2) * 4 + tt
                        bwait(P, bk)
                        P.wait("tensor", c_p, vwd)
                        P.wait("tensor", c_v, su_done_v)
                        builders = [
                            (lambda q, bk=bk, sl=dd % 2, fc=fc, tt=tt:
                             q.matmul(psb[bk][:, :],
                                      t_wdbf[:, sl * FF + fc * 128: sl * FF + (fc + 1) * 128],
                                      t_su[:, fc * T + tt * 512: fc * T + (tt + 1) * 512],
                                      start=(fc == 0), stop=(fc == NFC - 1)))
                            for fc in range(NFC)]
                        vt = P.group("tensor", c_t, builders)
                        if tt == TT - 1:
                            dn_mm_last.append(vt)
                        P.wait("vector", c_t, vt)
                        prev = r1_final.get((dd, tt))
                        if prev is not None:
                            if isinstance(prev, tuple):
                                P.wait("vector", prev[0], prev[1])
                            else:
                                P.wait("vector", c_v, prev)
                        va = P.op("vector", c_v, lambda q, bk=bk, dd=dd, tt=tt:
                                  q.tensor_add(out=t_r1[:, dd * T + tt * 512: dd * T + (tt + 1) * 512],
                                               in0=psb[bk][:, :],
                                               in1=t_r1[:, dd * T + tt * 512: dd * T + (tt + 1) * 512]))
                        r1_final[(dd, tt)] = va
                        bank_free[bk] = (c_v, va)
            P.into_block(nc)

            P = Prog()
            r1_written = {k: v for k, v in r1_final.items()}
            sq_used = {}
            lnt_used = {}
            emit_ln(P, sink="out")
            P.into_block(nc)

        P = Prog()
        P.wait("scalar", dq_o[0], dq_o[0].n)
        P.wait("scalar", dq_o[1], dq_o[1].n)
        P.into_block(nc)

        for ctr in S.values():
            assert ctr.n < 60000, f"semaphore overflow risk: {ctr.name}={ctr.n}"
        meta = dict(total_sbuf=total_sbuf, n_inst=nc.n_instructions(),
                    sems={k: v.n for k, v in S.items()})
    return nc, meta


def qk_perm():
    perm = np.zeros(1024, dtype=np.int64)
    for h in range(16):
        for i in range(32):
            perm[32 * h + i] = 64 * h + 2 * i
            perm[512 + 32 * h + i] = 64 * h + 2 * i + 1
    return perm


def host_constants(T):
    bf = ml_dtypes.bfloat16
    c = {}
    freqs = 10000.0 ** (-np.arange(32) / 32.0)
    ang = np.arange(T)[:, None] * freqs[None, :]
    rowf = np.tile(freqs, 4)
    angs = np.arange(T)[None, :] * rowf[:, None]
    c["cosT"] = np.cos(angs).astype(np.float32)
    c["sinT"] = np.sin(angs).astype(np.float32)
    m = np.zeros((4, 128, 512), np.float32)
    for d in range(4):
        p = np.arange(128)[:, None]
        cc = np.arange(512)[None, :]
        m[d] = (cc >= d * 128 + p).astype(np.float32)
    c["maskT"] = np.concatenate([m, m], axis=2).astype(bf)
    selE = np.zeros((128, 128), np.float32)
    selO = np.zeros((128, 128), np.float32)
    for p in range(64):
        mE = p if p < 32 else p + 32
        mO = p + 32 if p < 32 else p + 64
        selE[p, mE] = 1.0
        selE[64 + p, mE] = 1.0
        selO[p, mO] = 1.0
        selO[64 + p, mO] = 1.0
    c["selE2"] = selE.astype(bf)
    c["selO2"] = selO.astype(bf)
    hs = np.zeros((128, 2), np.float32)
    hs[0:64, 0] = 1.0
    hs[64:128, 1] = 1.0
    c["hselc"] = hs.astype(bf)
    ns = np.zeros((2, 384), np.float32)
    ns[0, 0:128] = -1.0
    ns[1, 128:256] = -1.0
    ns[:, 256:384] = -1.0
    c["negsel"] = ns
    c["ident"] = np.eye(128, dtype=np.float32)
    c["identb"] = np.eye(128, dtype=np.float32).astype(bf)
    c["onesf"] = np.ones((1, 128), np.float32)
    c["onesb"] = np.ones((1, 128), np.float32).astype(bf)
    c["onescol"] = np.ones((128, 1), np.float32)
    return c


def host_weights(W_qkv, b_qkv, W_proj, b_proj, gate_w, w_up, w_down,
                 sg_gate, sg_up, sg_down):
    out = {}
    perm = qk_perm()
    full_perm = np.concatenate([perm, 1024 + perm, 2048 + np.arange(1024)])
    Wq = np.asarray(W_qkv)[:, full_perm]
    bq = np.asarray(b_qkv)[full_perm]
    slots = SLOTS_QK + list(range(16, 24))
    wq_t = np.zeros((24, 128, 1024), np.float32)
    for s_i, jc in enumerate(slots):
        blk = Wq[:, jc * 128:(jc + 1) * 128]
        wq_t[s_i] = blk.reshape(8, 128, 128).transpose(1, 0, 2).reshape(128, 1024)
    out["wq_t"] = np.ascontiguousarray(wq_t)
    bqk = np.zeros((128, 16), np.float32)
    for s_i in range(16):
        jc = SLOTS_QK[s_i]
        bqk[:, s_i] = bq[jc * 128:(jc + 1) * 128]
    out["bqk_c"] = bqk
    out["bv_row"] = np.ascontiguousarray(bq[2048:].reshape(1, 1024).astype(np.float32))
    Wp = np.asarray(W_proj)
    wp_t = np.zeros((8, 128, 1024), np.float32)
    for oc in range(8):
        blk = Wp[:, oc * 128:(oc + 1) * 128]
        wp_t[oc] = blk.reshape(8, 128, 128).transpose(1, 0, 2).reshape(128, 1024)
    out["wp_t"] = wp_t
    out["bp_c"] = np.ascontiguousarray(np.asarray(b_proj).reshape(8, 128).T.astype(np.float32))
    out["gw_t"] = np.ascontiguousarray(np.asarray(gate_w).T.reshape(8, 128, 8)
                                       .transpose(1, 0, 2).reshape(128, 64).astype(np.float32))
    wgu_t = np.zeros((9, NFC, 128, 2048), np.float32)
    wd_t = np.zeros((9, 8, 128, FF), np.float32)
    wu = np.asarray(w_up)
    wd = np.asarray(w_down)
    for e in range(8):
        wgT = wu[e, :FF, :].T
        wuT = wu[e, FF:, :].T
        wdT = wd[e].T
        for fp in range(NFC):
            g = wgT[:, fp * 128:(fp + 1) * 128].reshape(8, 128, 128)
            u = wuT[:, fp * 128:(fp + 1) * 128].reshape(8, 128, 128)
            gu = np.concatenate([g, u], axis=2)
            wgu_t[e, fp] = gu.transpose(1, 0, 2).reshape(128, 2048)
        for dd in range(8):
            blk = wdT[:, dd * 128:(dd + 1) * 128]
            wd_t[e, dd] = blk.reshape(NFC, 128, 128).transpose(1, 0, 2).reshape(128, FF)
    sgT = np.concatenate([np.asarray(sg_gate).T, np.asarray(sg_up).T], axis=1)
    sdT = np.asarray(sg_down).T
    for fp in range(NFC):
        g = sgT[:, fp * 128:(fp + 1) * 128].reshape(8, 128, 128)
        u = sgT[:, FF + fp * 128: FF + (fp + 1) * 128].reshape(8, 128, 128)
        gu = np.concatenate([g, u], axis=2)
        wgu_t[8, fp] = gu.transpose(1, 0, 2).reshape(128, 2048)
    for dd in range(8):
        blk = sdT[:, dd * 128:(dd + 1) * 128]
        wd_t[8, dd] = blk.reshape(NFC, 128, 128).transpose(1, 0, 2).reshape(128, FF)
    out["wgu_t"] = wgu_t
    out["wd_t"] = wd_t
    return out


_T = 2048
_NB = 4
_NCORES = 8


def kernel(L0_attn_x, L0_attn_W_qkv, L0_attn_b_qkv, L0_attn_W_proj, L0_attn_b_proj,
           L0_ffn_gate_w, L0_ffn_w_up, L0_ffn_w_down, L0_ffn_sg_gate, L0_ffn_sg_up,
           L0_ffn_sg_down):
    from concourse.bass_utils import run_bass_kernel_spmd
    x = np.ascontiguousarray(np.asarray(L0_attn_x, dtype=np.float32))
    m = host_weights(L0_attn_W_qkv, L0_attn_b_qkv, L0_attn_W_proj, L0_attn_b_proj,
                     L0_ffn_gate_w, L0_ffn_w_up, L0_ffn_w_down,
                     L0_ffn_sg_gate, L0_ffn_sg_up, L0_ffn_sg_down)
    m.update(host_constants(_T))
    for k in list(m):
        if m[k].dtype == np.float32:
            m[k] = np.ascontiguousarray(m[k])
    in_maps = []
    for c in range(_NCORES):
        mc = dict(m)
        xc = x[c * _NB:(c + 1) * _NB]
        mc["xT"] = np.ascontiguousarray(xc.transpose(2, 0, 1).reshape(1024, _NB * _T))
        in_maps.append(mc)
    nc, _meta = build_program(T=_T, n_b=_NB, debug=False)
    res = run_bass_kernel_spmd(nc, in_maps, core_ids=list(range(_NCORES)))
    outs = []
    for c in range(_NCORES):
        oT = np.asarray(res.results[c]["outT"])
        outs.append(oT.reshape(1024, _NB, _T).transpose(1, 2, 0))
    return np.ascontiguousarray(np.concatenate(outs, axis=0), dtype=np.float32)


# version 21
# speedup vs baseline: 1.0753x; 1.0459x over previous
import contextlib
import numpy as np
import ml_dtypes
import concourse.bass as bass
import concourse.mybir as mybir

F32 = mybir.dt.float32
BF16 = mybir.dt.bfloat16
U8 = mybir.dt.uint8
AF = mybir.ActivationFunctionType
ALU = mybir.AluOpType
AX = mybir.AxisListType

D = 1024
NCH = 8
FF = 1536
NFC = 12
ENGINES = ("sync", "scalar", "vector", "tensor", "gpsimd")
SLOTS_QK = [0, 4, 1, 5, 2, 6, 3, 7, 8, 12, 9, 13, 10, 14, 11, 15]


class Ctr:
    def __init__(self, sem, name):
        self.sem = sem
        self.name = name
        self.n = 0


class Prog:
    def __init__(self):
        self.ops = {e: [] for e in ENGINES}
        self.hw = {}

    def op(self, eng, ctr, build, k=1):
        ctr.n += k
        self.ops[eng].append(lambda q, b=build, s=ctr.sem, kk=k: b(q).then_inc(s, kk))
        return ctr.n

    def opn(self, eng, build):
        self.ops[eng].append(build)

    def group(self, eng, ctr, builders):
        for f in builders[:-1]:
            self.opn(eng, f)
        return self.op(eng, ctr, builders[-1])

    def wait(self, eng, ctr, val):
        key = (eng, ctr.name)
        if val <= 0 or val <= self.hw.get(key, 0):
            return
        self.hw[key] = val
        self.ops[eng].append(lambda q, s=ctr.sem, v=val: q.wait_ge(s, v))

    def into_block(self, nc):
        with nc.Block() as blk:
            for eng in ENGINES:
                ops = self.ops[eng]
                if ops:
                    getattr(blk, eng)(lambda q, o=ops: [f(q) for f in o])


class Arena:
    def __init__(self, nc):
        self.nc = nc
        self.reqs = []

    def at(self, name, shape, dtype, p0, p1=None):
        if p1 is None:
            p1 = p0
        h = self.nc.virtual_sbuf_tensor(name, list(shape), dtype)
        width = int(np.prod(shape[1:])) * mybir.dt.size(dtype)
        width = (width + 511) // 512 * 512
        self.reqs.append([h, width, p0, p1])
        return h

    def _place(self, order):
        placed = []
        offs = {}
        for i in order:
            h, w, p0, p1 = self.reqs[i]
            if p0 == 9:
                p0, p1 = 0, 9
            cands = sorted({0} | {o + ww for (o, ww, q0, q1) in placed
                                  if not (q1 < p0 or q0 > p1)})
            for off in cands:
                if all(q1 < p0 or q0 > p1 or off + w <= o or off >= o + ww
                       for (o, ww, q0, q1) in placed):
                    placed.append((off, w, p0, p1))
                    offs[i] = off
                    break
        return max(o + w for (o, w, _, _) in placed), offs, placed

    def materialize(self, ctx):
        n = len(self.reqs)
        orders = [
            sorted(range(n), key=lambda i: -self.reqs[i][1]),
            sorted(range(n), key=lambda i: (-(9 if self.reqs[i][2] == 9 else
                                             self.reqs[i][3] - self.reqs[i][2]),
                                            -self.reqs[i][1])),
            sorted(range(n), key=lambda i: (self.reqs[i][2] if self.reqs[i][2] != 9 else -1,
                                            -self.reqs[i][1])),
        ]
        best = None
        for order in orders:
            total, offs, placed = self._place(order)
            if best is None or total < best[0]:
                best = (total, offs, placed)
        total, offs, placed = best
        assert total <= 206 * 1024, f"arena overflow: {total} B/partition"
        back = ctx.enter_context(self.nc.sbuf_tensor("arena", [128, total], U8))
        base = self.nc.lookup_mloc(back).addr
        for i, (h, w, p0, p1) in enumerate(self.reqs):
            real = self.nc.alloc_sbuf_tensor_at(h.name + "_bk", list(h.shape), h.dtype,
                                                offset=base + offs[i])
            self.nc.back_virtual_tensor_with(h, real)
        return total


def build_program(T=2048, n_b=4, debug=False):
    TT = T // 512
    NTB = T // 128
    W_VSB = NTB * 1152
    nc = bass.Bass("TRN2", target_bir_lowering=False)

    dr = {}

    def din(name, shape, dtype=F32):
        dr[name] = nc.dram_tensor(name, list(shape), dtype, kind="ExternalInput").ap()
        return dr[name]

    xT = din("xT", [D, n_b * T])
    din("wq_t", [24, 128, 1024])
    din("wp_t", [8, 128, 1024])
    din("wgu_t", [9, NFC, 128, 2048])
    din("wd_t", [9, 8, 128, FF])
    din("gw_t", [128, 64])
    din("bqk_c", [128, 16])
    din("bv_row", [1, 1024])
    din("bp_c", [128, 8])
    din("cosT", [128, T])
    din("sinT", [128, T])
    din("maskT", [4, 128, 1024], BF16)
    din("selE2", [128, 128], BF16)
    din("selO2", [128, 128], BF16)
    din("hselc", [128, 2], BF16)
    din("negsel", [2, 384])
    din("ident", [128, 128])
    din("identb", [128, 128], BF16)
    din("onesf", [1, 128])
    din("onesb", [1, 128], BF16)
    din("onescol", [128, 1])
    outT = nc.dram_tensor("outT", [D, n_b * T], F32, kind="ExternalOutput").ap()
    dbg = {}
    if debug:
        for nm, shape, dt in [("d_qs", [128, NCH * T], BF16), ("d_ks", [128, NCH * T], BF16),
                              ("d_vsb", [128, W_VSB], BF16), ("d_attn", [128, NCH * T], BF16),
                              ("d_r1", [128, NCH * T], F32), ("d_h", [128, NCH * T], BF16),
                              ("d_comb", [128, NTB * 8], F32), ("d_mb", [128, 16], F32)]:
            dbg[nm] = nc.dram_tensor(nm, shape, dt, kind="ExternalOutput").ap()

    ar = Arena(nc)
    t_selE = ar.at("selEt", [128, 128], BF16, 9)
    t_selO = ar.at("selOt", [128, 128], BF16, 9)
    t_hsel = ar.at("hselt", [128, 2], BF16, 9)
    t_negsel = ar.at("negselt", [2, 384], F32, 9)
    t_ident = ar.at("identt", [128, 128], F32, 9)
    t_identb = ar.at("identbt", [128, 128], BF16, 9)
    t_onesf = ar.at("onesft", [1, 128], F32, 9)
    t_onesb = ar.at("onesbt", [1, 128], BF16, 9)
    t_onescol = ar.at("onescolt", [128, 1], F32, 9)
    t_gw = ar.at("gw", [128, 64], F32, 9)
    t_gwb = ar.at("gwb", [128, 64], BF16, 9)
    t_bqk = ar.at("bqk", [128, 16], F32, 0, 1)
    t_bv = ar.at("bvt", [1, 1024], F32, 0, 1)
    t_bvb = ar.at("bvbt", [1, 1024], BF16, 0, 1)
    t_bp = ar.at("bpt", [128, 8], F32, 9)
    t_mb = ar.at("mbt", [128, 16], F32, 9)
    t_mask = ar.at("maskt", [128, 4096], BF16, 2)
    t_eps = ar.at("epst", [1, 1], F32, 9)
    t_cos = ar.at("cost", [128, T], F32, 0)
    t_sin = ar.at("sint", [128, T], F32, 0)
    t_xbf = ar.at("xbf", [128, NCH * T], BF16, 0, 1)
    t_xst = ar.at("xst", [128, 1024], F32, 0)
    t_wqst = ar.at("wqst", [128, 2048], F32, 0, 1)
    t_wqbf = ar.at("wqbf", [128, 4096], BF16, 0, 1)
    t_wvbf = ar.at("wvbf", [128, 8192], BF16, 1)
    t_ropeEh = ar.at("ropeEh", [128, 2 * T], F32, 0)
    t_ropeO = ar.at("ropeO", [128, 2048], F32, 0)
    t_m1 = ar.at("m1s", [128, 1024], F32, 0)
    t_m2 = ar.at("m2s", [128, 1024], F32, 0)
    t_m3 = ar.at("m3s", [128, 1024], F32, 0)
    t_m4 = ar.at("m4s", [128, 1024], F32, 0)
    t_rebf = ar.at("rebf", [128, 4096], BF16, 0, 1)
    t_robf = ar.at("robf", [128, 4096], BF16, 0, 1)
    t_qs = ar.at("qst", [128, NCH * T], BF16, 0, 2)
    t_ks = ar.at("kst", [128, NCH * T], BF16, 0, 2)
    t_vsb = ar.at("vsbt", [128, W_VSB], BF16, 1, 2)
    t_sqbf = ar.at("sqbf", [128, 1024], BF16, 1)
    t_mq2 = ar.at("mq2", [2, 8], F32, 1)
    t_mk2 = ar.at("mk2", [2, 8], F32, 1)
    t_msr = ar.at("msr", [2, 8], F32, 1)
    t_pab = ar.at("pab", [128, 3072], BF16, 2)
    t_sumr = ar.at("sumr", [1, 2048], F32, 2)
    t_sumrb = ar.at("sumrb", [1, 1024], BF16, 2)
    t_bcs = ar.at("bcs", [128, 1024], F32, 2)
    t_attnB = ar.at("attnB", [64, 1024], BF16, 2)
    t_attn = ar.at("attnt", [128, NCH * T], BF16, 2, 3)
    t_wpst = ar.at("wpst", [128, 2048], F32, 3)
    t_wpbf = ar.at("wpbf", [128, 2048], BF16, 3)
    t_xre = ar.at("xre", [128, 1024], F32, 3)
    t_r1 = ar.at("r1t", [128, NCH * T], F32, 3, 6)
    t_hbf = ar.at("hbft", [128, NCH * T], BF16, 3, 5)
    ln_sets = {}
    for ph, tag in ((3, "c"), (6, "f")):
        ln_sets[tag] = dict(
            sqsc=ar.at(f"sqsc{tag}", [128, 1024], F32, ph),
            sums=ar.at(f"sums{tag}", [1, T], F32, ph),
            abb=ar.at(f"abb{tag}", [1, 2 * T], BF16, ph),
            sqs=ar.at(f"sqst{tag}", [1, T], F32, ph),
            av=ar.at(f"avt{tag}", [1, T], F32, ph),
            bv2=ar.at(f"bv2t{tag}", [1, T], F32, ph),
            lnt=ar.at(f"lnt{tag}", [128, 1024], F32, ph),
            absb=ar.at(f"absb{tag}", [128, 1024], F32, ph))
    t_comb = ar.at("combt", [128, NTB * 8], F32, 4, 5)
    t_lgb = ar.at("lgbt", [8, 1024], F32, 4)
    t_cd = ar.at("cdt", [128, 8], F32, 4)
    t_ceq = ar.at("ceqt", [128, 8], F32, 4)
    t_cm1 = ar.at("cm1t", [128, 1], F32, 4)
    t_cm2 = ar.at("cm2t", [128, 1], F32, 4)
    t_cnum = ar.at("cnumt", [128, 8], F32, 4)
    t_cden = ar.at("cdent", [128, 1], F32, 4)
    t_wgust = ar.at("wgust", [128, 2048], F32, 5)
    t_wgubf = ar.at("wgubf", [128, 2 * 2048], BF16, 5)
    t_wdst = ar.at("wdst", [128, 2 * FF], F32, 5)
    t_wdbf = ar.at("wdbf", [128, 2 * FF], BF16, 5)
    t_su = ar.at("sut", [128, NFC * T], BF16, 5)
    t_gbuf = ar.at("gbuft", [128, 2 * T], BF16, 5)
    t_combsb = ar.at("combsbt", [128, T], BF16, 5)
    t_rowt = ar.at("rowtt", [1, 512], F32, 5)
    t_outst = ar.at("outst", [128, 1024], F32, 6)

    ctx = contextlib.ExitStack()
    with ctx:
        total_sbuf = ar.materialize(ctx)

        class BankView:
            def __init__(self, t, c0):
                self.t = t
                self.c0 = c0

            def __getitem__(self, idx):
                rows, cols = idx
                c0 = (cols.start or 0) + self.c0
                c1 = (512 if cols.stop is None else cols.stop) + self.c0
                return self.t[rows, c0:c1]

        psw = [ctx.enter_context(nc.psum_tensor(f"pw{i}", [128, 1024], F32))
               for i in range(4)]
        psb = [BankView(psw[i // 2], (i % 2) * 512) for i in range(8)]
        S = {nm: Ctr(ctx.enter_context(nc.semaphore(nm)), nm)
             for nm in ("dq_x0", "dq_x1", "dq_w0", "dq_w1", "dq_wd0", "dq_wd1", "dq_wgu", "dq_cs",
                        "dq_sh0", "dq_sh1", "dq_o0", "dq_o1", "dq_pm0", "dq_pm1", "dq_pm2", "dq_pm3", "dq_dbg", "dq_init",
                        "c_t", "c_s", "c_v", "c_p")}
        dq_x = [S["dq_x0"], S["dq_x1"]]
        dq_w = [S["dq_w0"], S["dq_w1"]]
        dq_wd = [S["dq_wd0"], S["dq_wd1"]]
        dq_wgu, dq_cs, dq_dbg, dq_init = (S["dq_wgu"], S["dq_cs"], S["dq_dbg"], S["dq_init"])
        dq_sh = [S["dq_sh0"], S["dq_sh1"]]
        dq_o = [S["dq_o0"], S["dq_o1"]]
        dq_pm = [S["dq_pm0"], S["dq_pm1"]]
        dq_pmh = [S["dq_pm2"], S["dq_pm3"]]
        c_t, c_s, c_v, c_p = (S[k] for k in ("c_t", "c_s", "c_v", "c_p"))
        bank_free = [(None, 0)] * 8

        def bwait(P, i, eng="tensor"):
            ctr, v = bank_free[i]
            if ctr is not None:
                P.wait(eng, ctr, v)

        out_cnt = {0: 0, 1: 0}

        def emit_ln(P, src_is_r1=True, sink="hbf"):
            sq_hist = []
            lnt_ctr = {}
            ab_used = [None]
            t_sqsc = ln_sets["c" if sink == "hbf" else "f"]["sqsc"]
            t_sums = ln_sets["c" if sink == "hbf" else "f"]["sums"]
            t_sqs = ln_sets["c" if sink == "hbf" else "f"]["sqs"]
            t_av = ln_sets["c" if sink == "hbf" else "f"]["av"]
            t_bv2 = ln_sets["c" if sink == "hbf" else "f"]["bv2"]
            t_lnt = ln_sets["c" if sink == "hbf" else "f"]["lnt"]
            t_absb = ln_sets["c" if sink == "hbf" else "f"]["absb"]
            t_abb = ln_sets["c" if sink == "hbf" else "f"]["abb"]
            """LayerNorm over d of t_r1 (in place), stats via fp32 matmuls.
            sink: 'hbf' (phase C: write normalized into r1 + cast to hbf)
                  'out' (phase F: write normalized to outstage + DMA out)."""
            stat_bank_v = []
            for tt in range(TT):
                bk = tt % 2
                bk2 = 2 + tt % 2
                bwait(P, bk)
                bwait(P, bk2)
                for oc in range(NCH):
                    sl = oc % 2
                    sq_eng = "scalar" if oc % 2 == 0 else "gpsimd"
                    sq_ctr = c_s if oc % 2 == 0 else c_p
                    rw = r1_written[(oc, tt)]
                    rw_ctr, rw_v = rw if isinstance(rw, tuple) else (c_v, rw)
                    P.wait(sq_eng, rw_ctr, rw_v)
                    if len(sq_hist) >= 2:
                        P.wait(sq_eng, c_t, sq_hist[-2])
                    rsl_ = lambda q, oc=oc, tt=tt: t_r1[:, oc * T + tt * 512: oc * T + (tt + 1) * 512]
                    if oc % 2 == 0:
                        vs = P.op("scalar", c_s, lambda q, oc=oc, tt=tt, sl=sl:
                                  q.activation(out=t_sqsc[:, sl * 512:(sl + 1) * 512],
                                               in_=t_r1[:, oc * T + tt * 512: oc * T + (tt + 1) * 512],
                                               func=AF.Square))
                    else:
                        vs = P.op("gpsimd", c_p, lambda q, oc=oc, tt=tt, sl=sl:
                                  q.tensor_mul(out=t_sqsc[:, sl * 512:(sl + 1) * 512],
                                               in0=t_r1[:, oc * T + tt * 512: oc * T + (tt + 1) * 512],
                                               in1=t_r1[:, oc * T + tt * 512: oc * T + (tt + 1) * 512]))
                    P.wait("tensor", rw_ctr, rw_v)
                    P.wait("tensor", sq_ctr, vs)
                    P.opn("tensor", lambda q, bk=bk, oc=oc, tt=tt:
                          q.matmul(psb[bk][0:1, :], t_onescol[:, :],
                                   t_r1[:, oc * T + tt * 512: oc * T + (tt + 1) * 512],
                                   start=(oc == 0), stop=(oc == NCH - 1)))
                    vts = P.op("tensor", c_t, lambda q, bk2=bk2, sl=sl, oc=oc:
                               q.matmul(psb[bk2][0:1, :], t_onescol[:, :],
                                        t_sqsc[:, sl * 512:(sl + 1) * 512],
                                        start=(oc == 0), stop=(oc == NCH - 1)))
                    sq_hist.append(vts)
                P.wait("scalar", c_t, c_t.n)
                P.op("scalar", c_s, lambda q, bk=bk, tt=tt:
                     q.copy(out=t_sums[0:1, tt * 512:(tt + 1) * 512], in_=psb[bk][0:1, :]))
                v = P.op("scalar", c_s, lambda q, bk2=bk2, tt=tt:
                         q.copy(out=t_sqs[0:1, tt * 512:(tt + 1) * 512], in_=psb[bk2][0:1, :]))
                bank_free[bk] = (c_s, v)
                bank_free[bk2] = (c_s, v)
                stat_bank_v.append(v)
            P.wait("vector", c_s, max(stat_bank_v))
            P.op("vector", c_v, lambda q: q.tensor_scalar_mul(out=t_av[0:1, :], in0=t_sums[0:1, :],
                                                              scalar1=1.0 / 1024.0))
            P.wait("vector", c_v, c_v.n)
            P.op("vector", c_v, lambda q: q.tensor_mul(out=t_bv2[0:1, :], in0=t_av[0:1, :],
                                                       in1=t_av[0:1, :]))
            P.wait("vector", c_v, c_v.n)
            P.op("vector", c_v, lambda q: q.scalar_tensor_tensor(
                out=t_bv2[0:1, :], in0=t_sqs[0:1, :], scalar=1.0 / 1024.0, in1=t_bv2[0:1, :],
                op0=ALU.mult, op1=ALU.subtract))
            P.wait("scalar", c_v, c_v.n)
            P.op("scalar", c_s, lambda q: q.activation(out=t_sqs[0:1, :], in_=t_bv2[0:1, :],
                                                       func=AF.Sqrt, bias=t_eps[0:1, 0:1]))
            P.wait("scalar", c_s, c_s.n)
            P.op("scalar", c_s, lambda q: q.activation(out=t_bv2[0:1, :], in_=t_sqs[0:1, :],
                                                       func=AF.Ln))
            P.wait("scalar", c_s, c_s.n)
            P.op("scalar", c_s, lambda q: q.activation(out=t_sums[0:1, :], in_=t_bv2[0:1, :],
                                                       func=AF.Exp, scale=-1.0))
            rstd_v = c_s.n
            P.wait("vector", c_s, rstd_v)
            P.op("vector", c_v, lambda q: q.scalar_tensor_tensor(
                out=t_bv2[0:1, :], in0=t_av[0:1, :], scalar=-1.0, in1=t_sums[0:1, :],
                op0=ALU.mult, op1=ALU.mult))
            P.wait("vector", c_v, c_v.n)
            v_ab = P.op("vector", c_v, lambda q: q.tensor_copy(out=t_av[0:1, :], in_=t_sums[0:1, :]))
            P.wait("scalar", c_v, v_ab)
            P.op("scalar", c_s, lambda q: q.copy(out=t_abb[0:1, 0:T], in_=t_av[0:1, :]))
            v_abb = P.op("scalar", c_s, lambda q: q.copy(out=t_abb[0:1, T:2 * T], in_=t_bv2[0:1, :]))
            for tt in range(TT):
                for h_i in range(2):
                    bk = 4 + h_i
                    bwait(P, bk)
                    P.wait("tensor", c_s, v_abb)
                    P.op("tensor", c_t, lambda q, bk=bk, h_i=h_i, tt=tt:
                         q.matmul(psb[bk][:, :], t_onesb[0:1, :],
                                  t_abb[0:1, h_i * T + tt * 512: h_i * T + (tt + 1) * 512],
                                  start=True, stop=True))
                bc_v = c_t.n
                P.wait("vector", c_t, bc_v)
                if ab_used[0]:
                    P.wait("vector", c_v, ab_used[0][0])
                    P.wait("vector", c_p, ab_used[0][1])
                P.op("vector", c_v, lambda q: q.tensor_copy(out=t_absb[:, 0:512], in_=psb[4][:, :]))
                vab_s = P.op("vector", c_v, lambda q: q.tensor_copy(out=t_absb[:, 512:1024],
                                                                    in_=psb[5][:, :]))
                last_v = 0
                for oc in range(NCH):
                    sl = oc % 2
                    a_eng = "vector" if oc % 2 == 0 else "gpsimd"
                    a_ctr = c_v if oc % 2 == 0 else c_p
                    P.wait(a_eng, c_v, vab_s)
                    if oc >= 2:
                        P.wait(a_eng, lnt_ctr.get((tt, oc - 2), c_v), lnt_used.get((tt, oc - 2), 0))
                    P.op(a_eng, a_ctr, lambda q, oc=oc, tt=tt, sl=oc % 2:
                         q.tensor_mul(out=t_lnt[:, sl * 512:(sl + 1) * 512],
                                      in0=t_r1[:, oc * T + tt * 512: oc * T + (tt + 1) * 512],
                                      in1=t_absb[:, 0:512]))
                    P.wait(a_eng, a_ctr, a_ctr.n)
                    if sink == "hbf":
                        v2 = P.op(a_eng, a_ctr, lambda q, oc=oc, tt=tt, sl=oc % 2:
                                  q.tensor_add(out=t_r1[:, oc * T + tt * 512: oc * T + (tt + 1) * 512],
                                               in0=t_lnt[:, sl * 512:(sl + 1) * 512],
                                               in1=t_absb[:, 512:1024]))
                        lnt_used[(tt, oc)] = v2
                        lnt_ctr[(tt, oc)] = a_ctr
                        P.wait("scalar", a_ctr, v2)
                        vh = P.op("scalar", c_s, lambda q, oc=oc, tt=tt:
                                  q.copy(out=t_hbf[:, oc * T + tt * 512: oc * T + (tt + 1) * 512],
                                         in_=t_r1[:, oc * T + tt * 512: oc * T + (tt + 1) * 512]))
                        hbf_cast[(oc, tt)] = vh
                        r1_final[(oc, tt)] = (a_ctr, v2)
                        last_v = v2
                    else:
                        osl = (oc % 2)
                        if out_cnt[osl] > 0:
                            P.wait(a_eng, dq_o[osl], 16 * out_cnt[osl])
                        v2 = P.op(a_eng, a_ctr, lambda q, oc=oc, osl=osl, sl=oc % 2:
                                  q.tensor_add(out=t_outst[:, osl * 512:(osl + 1) * 512],
                                               in0=t_lnt[:, sl * 512:(sl + 1) * 512],
                                               in1=t_absb[:, 512:1024]))
                        lnt_used[(tt, oc)] = v2
                        lnt_ctr[(tt, oc)] = a_ctr
                        P.wait("scalar", a_ctr, v2)
                        P.op("scalar", dq_o[osl], lambda q, oc=oc, tt=tt, osl=osl:
                             q.dma_start(out=outT[oc * 128:(oc + 1) * 128,
                                                  b * T + tt * 512: b * T + (tt + 1) * 512],
                                         in_=t_outst[:, osl * 512:(osl + 1) * 512]), 16)
                        out_cnt[osl] += 1
                        last_v = v2
                ab_used[0] = (max(lnt_used[(tt, oc)] for oc in range(NCH)
                                  if lnt_ctr[(tt, oc)] is c_v),
                              max(lnt_used[(tt, oc)] for oc in range(NCH)
                                  if lnt_ctr[(tt, oc)] is c_p))
                bank_free[4] = (c_v, vab_s)
                bank_free[5] = (c_v, vab_s)

        for b in range(n_b):
            if b == 0:
                P = Prog()
                for src, th in [(dr["cosT"], t_cos), (dr["sinT"], t_sin),
                                (dr["selE2"], t_selE), (dr["selO2"], t_selO),
                                (dr["hselc"], t_hsel), (dr["negsel"], t_negsel),
                                (dr["ident"], t_ident), (dr["onesf"], t_onesf),
                                (dr["onesb"], t_onesb), (dr["onescol"], t_onescol),
                                (dr["gw_t"], t_gw), (dr["bp_c"], t_bp),
                                (dr["identb"], t_identb)]:
                    P.op("sync", dq_init, lambda q, s=src, d=th:
                         q.dma_start(out=d[:, :], in_=s[:, :]), 16)
                P.wait("scalar", dq_init, dq_init.n)
                P.op("vector", c_v, lambda q: q.memset(t_eps[0:1, 0:1], 1e-5))
                P.op("scalar", c_s, lambda q: q.copy(out=t_gwb[:, :], in_=t_gw[:, :]))
                P.into_block(nc)

            P = Prog()
            if b > 0:
                for eng in ("sync", "gpsimd", "scalar", "vector"):
                    P.wait(eng, dq_o[0], dq_o[0].n)
                    P.wait(eng, dq_o[1], dq_o[1].n)
            P.op("sync", dq_cs, lambda q: q.dma_start(out=t_cos[:, :], in_=dr["cosT"][:, :]), 16)
            P.op("sync", dq_cs, lambda q: q.dma_start(out=t_sin[:, :], in_=dr["sinT"][:, :]), 16)
            P.op("sync", dq_cs, lambda q: q.dma_start(out=t_bqk[:, :], in_=dr["bqk_c"][:, :]), 16)
            P.op("sync", dq_cs, lambda q: q.dma_start(out=t_bv[:, :], in_=dr["bv_row"][:, :]), 16)
            cs_v = dq_cs.n
            P.wait("scalar", dq_cs, cs_v)
            bvb_v = P.op("scalar", c_s, lambda q: q.copy(out=t_bvb[:, :], in_=t_bv[:, :]))

            xcast_v = {}
            xhist = []
            for i, (tt, dc) in enumerate([(tt, dc) for tt in range(TT) for dc in range(NCH)]):
                if i >= 2:
                    P.wait("sync", c_p, xhist[i - 2])
                P.op("sync", dq_x[i % 2], lambda q, dc=dc, tt=tt, sl=i % 2:
                     q.dma_start(out=t_xst[:, sl * 512:(sl + 1) * 512],
                                 in_=xT[dc * 128:(dc + 1) * 128,
                                        b * T + tt * 512: b * T + (tt + 1) * 512]), 16)
                P.wait("gpsimd", dq_x[i % 2], dq_x[i % 2].n)
                v = P.op("gpsimd", c_p, lambda q, dc=dc, tt=tt, sl=i % 2:
                         q.tensor_copy(out=t_xbf[:, dc * T + tt * 512: dc * T + (tt + 1) * 512],
                                       in_=t_xst[:, sl * 512:(sl + 1) * 512]))
                xcast_v[(tt, dc)] = v
                xhist.append(v)
            xcast_tt = {tt: max(xcast_v[(tt, dc)] for dc in range(NCH)) for tt in range(TT)}

            wcast_v = {}
            qkv_mm_last = {}

            def emit_load_cast(s, preloaded=False):
                if not preloaded:
                    if s >= 2:
                        P.wait("sync", c_s, wcast_v[s - 2])
                    P.op("sync", dq_w[s % 2], lambda q, s=s, sl=(s % 2):
                         q.dma_start(out=t_wqst[:, sl * 1024:(sl + 1) * 1024],
                                     in_=dr["wq_t"][s, :, :]), 16)
                P.wait("scalar", dq_w[s % 2], dq_w[s % 2].n)
                if s < 16:
                    if s - 4 in qkv_mm_last:
                        P.wait("scalar", c_t, qkv_mm_last[s - 4])
                    v = P.op("scalar", c_s, lambda q, sl=(s % 2), bl=(s % 4):
                             q.copy(out=t_wqbf[:, bl * 1024:(bl + 1) * 1024],
                                    in_=t_wqst[:, sl * 1024:(sl + 1) * 1024]))
                else:
                    for dc in range(NCH):
                        v = P.op("scalar", c_s, lambda q, sl=(s % 2), dc=dc, j=s - 16:
                                 q.copy(out=t_wvbf[:, dc * 1024 + j * 128: dc * 1024 + (j + 1) * 128],
                                        in_=t_wqst[:, sl * 1024 + dc * 128: sl * 1024 + (dc + 1) * 128]))
                wcast_v[s] = v

            re_v, ro_v, evacE_v, evacO_v = {}, {}, {}, {}
            perm_dma_v = {}
            rope_prev = [0, 0]
            o_slot_last = {}

            def emit_rope(pair, c2):
                par = pair % 2
                csz = min(2, TT)
                tt0 = c2 * csz
                w = csz * 512
                rsl = par * 2048 + tt0 * 512
                eh = t_ropeEh[:, par * T + tt0 * 512: par * T + tt0 * 512 + w]
                osl = t_ropeO[:, tt0 * 512: tt0 * 512 + w]
                cosx = t_cos[:, tt0 * 512: tt0 * 512 + w]
                sinx = t_sin[:, tt0 * 512: tt0 * 512 + w]
                P.wait("vector", c_v, rope_prev[0])
                P.wait("vector", c_p, rope_prev[1])
                P.wait("gpsimd", c_v, rope_prev[0])
                P.wait("gpsimd", c_p, rope_prev[1])
                ev_e = max(evacE_v[(pair, tt0 + j)] for j in range(csz))
                ev_o = max(evacO_v[(pair, tt0 + j)] for j in range(csz))
                P.wait("vector", c_s, ev_e)
                P.wait("vector", dq_cs, cs_v)
                v_m1 = P.op("vector", c_v, lambda q, eh=eh, cosx=cosx:
                            q.tensor_mul(out=t_m1[:, 0:w], in0=eh, in1=cosx))
                P.wait("gpsimd", c_s, ev_o)
                P.wait("gpsimd", dq_cs, cs_v)
                v_m2 = P.op("gpsimd", c_p, lambda q, osl=osl, sinx=sinx:
                            q.tensor_mul(out=t_m2[:, 0:w], in0=osl, in1=sinx))
                v_m4 = P.op("gpsimd", c_p, lambda q, osl=osl, cosx=cosx:
                            q.tensor_mul(out=t_m4[:, 0:w], in0=osl, in1=cosx))
                pv = perm_dma_v.get(pair - 2)
                if pv is not None:
                    P.wait("vector", dq_pm[par], pv[0])
                    P.wait("vector", dq_pmh[par], pv[1])
                P.wait("vector", c_p, v_m2)
                P.wait("vector", c_v, v_m1)
                v_re = P.op("vector", c_v, lambda q, rsl=rsl: q.tensor_sub(
                    out=t_rebf[:, rsl:rsl + w], in0=t_m1[:, 0:w], in1=t_m2[:, 0:w]))
                v_m3 = P.op("vector", c_v, lambda q, eh=eh, sinx=sinx:
                            q.tensor_mul(out=t_m3[:, 0:w], in0=eh, in1=sinx))
                P.wait("gpsimd", c_v, v_m3)
                if pv is not None:
                    P.wait("gpsimd", dq_pm[par], pv[0])
                    P.wait("gpsimd", dq_pmh[par], pv[1])
                P.wait("gpsimd", c_p, v_m4)
                v_ro = P.op("gpsimd", c_p, lambda q, rsl=rsl: q.tensor_add(
                    out=t_robf[:, rsl:rsl + w], in0=t_m3[:, 0:w], in1=t_m4[:, 0:w]))
                for j in range(csz):
                    re_v[(pair, tt0 + j)] = (v_re, v_m3)
                    ro_v[(pair, tt0 + j)] = v_ro
                rope_prev[0] = v_re
                rope_prev[1] = v_ro

            def emit_qkv_slot(si):
                s = SLOTS_QK[si]
                pair = si // 2
                is_O = (si % 2 == 1)
                par = pair % 2
                for tt in range(TT):
                    bk = tt
                    bwait(P, bk)
                    P.wait("tensor", c_p, xcast_tt[tt])
                    P.wait("tensor", c_s, wcast_v[si])
                    builders = [
                        (lambda q, bk=bk, sl=si % 4, dc=dc, tt=tt:
                         q.matmul(psb[bk][:, :],
                                  t_wqbf[:, sl * 1024 + dc * 128: sl * 1024 + (dc + 1) * 128],
                                  t_xbf[:, dc * T + tt * 512: dc * T + (tt + 1) * 512],
                                  start=(dc == 0), stop=(dc == NCH - 1)))
                        for dc in range(NCH)]
                    mm_v = P.group("tensor", c_t, builders)
                    qkv_mm_last[si] = mm_v
                    P.wait("scalar", c_t, mm_v)
                    P.wait("scalar", dq_cs, cs_v)
                    if not is_O:
                        prevE = (pair - 2, tt)
                        if prevE in re_v:
                            P.wait("scalar", c_v, re_v[prevE][1])
                        v = P.op("scalar", c_s, lambda q, par=par, tt=tt, bk=bk, si=si:
                                 q.activation(out=t_ropeEh[:, par * T + tt * 512: par * T + (tt + 1) * 512],
                                              in_=psb[bk][:, :], func=AF.Identity,
                                              bias=t_bqk[:, si:si + 1], scale=1.0))
                        evacE_v[(pair, tt)] = v
                    else:
                        prevO = o_slot_last.get(tt % 4)
                        if prevO is not None:
                            P.wait("scalar", c_p, ro_v[prevO])
                        o_slot_last[tt % 4] = (pair, tt)
                        v = P.op("scalar", c_s, lambda q, tt=tt, bk=bk, si=si:
                                 q.activation(out=t_ropeO[:, (tt % 4) * 512: (tt % 4 + 1) * 512],
                                              in_=psb[bk][:, :], func=AF.Identity,
                                              bias=t_bqk[:, si:si + 1], scale=1.0))
                        evacO_v[(pair, tt)] = v
                    bank_free[bk] = (c_s, c_s.n)
                    if is_O and (TT == 1 or tt % 2 == 1):
                        emit_rope(pair, tt // min(2, TT))

            def emit_perm(pair):
                qk = 0 if pair < 4 else 1
                ccl = pair % 4
                dstt = t_qs if qk == 0 else t_ks
                par = pair % 2
                wall = TT * 512
                P.wait("scalar", c_v, max(re_v[(pair, tt)][0] for tt in range(TT)))
                P.wait("gpsimd", c_p, max(ro_v[(pair, tt)] for tt in range(TT)))
                for j in range(4):
                    cc = 2 * ccl + j // 2
                    rbase = 64 * (j % 2)
                    P.op("scalar", dq_pmh[par], lambda q, j=j, cc=cc, rbase=rbase, par=par, dstt=dstt, wall=wall:
                         q.dma_start(out=dstt[rbase:rbase + 32, cc * T: cc * T + wall],
                                     in_=t_rebf[32 * j:32 * j + 32, par * 2048: par * 2048 + wall]), 16)
                    P.op("gpsimd", dq_pm[par], lambda q, j=j, cc=cc, rbase=rbase, par=par, dstt=dstt, wall=wall:
                         q.dma_start(out=dstt[rbase + 32:rbase + 64, cc * T: cc * T + wall],
                                     in_=t_robf[32 * j:32 * j + 32, par * 2048: par * 2048 + wall]), 16)
                perm_dma_v[pair] = (dq_pm[par].n, dq_pmh[par].n)

            emit_load_cast(0)
            emit_load_cast(1)
            for si in range(16):
                if si + 2 < 16:
                    emit_load_cast(si + 2)
                emit_qkv_slot(si)
                if si >= 3 and si % 2 == 1:
                    emit_perm((si - 3) // 2)
            emit_perm(7)
            P.wait("sync", c_s, wcast_v[14])
            P.op("sync", dq_w[0], lambda q: q.dma_start(out=t_wqst[:, 0:1024],
                                                        in_=dr["wq_t"][16, :, :]), 16)
            P.wait("sync", c_s, wcast_v[15])
            P.op("sync", dq_w[1], lambda q: q.dma_start(out=t_wqst[:, 1024:2048],
                                                        in_=dr["wq_t"][17, :, :]), 16)
            P.into_block(nc)

            P = Prog()
            for s2 in range(16, 24):
                emit_load_cast(s2, preloaded=(s2 < 18))

            sq_mm_last = {}
            b0_prev = [0]

            def b0_unit(qk, cc):
                srct = t_qs if qk == 0 else t_ks
                m2t = t_mq2 if qk == 0 else t_mk2
                pair = 4 * qk + cc // 2
                pv = perm_dma_v[pair]
                for tt in range(TT):
                    sl = tt % 2
                    P.wait("scalar", dq_pm[pair % 2], pv[0])
                    P.wait("scalar", dq_pmh[pair % 2], pv[1])
                    if sq_mm_last.get(sl):
                        P.wait("scalar", c_t, sq_mm_last[sl])
                    vs = P.op("scalar", c_s, lambda q, srct=srct, cc=cc, tt=tt, sl=sl:
                              q.activation(out=t_sqbf[:, sl * 512:(sl + 1) * 512],
                                           in_=srct[:, cc * T + tt * 512: cc * T + (tt + 1) * 512],
                                           func=AF.Square))
                    bk = tt % 2
                    bwait(P, bk)
                    P.wait("tensor", c_s, vs)
                    vm = P.op("tensor", c_t, lambda q, bk=bk, sl=sl:
                              q.matmul(psb[bk][0:2, :], t_hsel[:, :],
                                       t_sqbf[:, sl * 512:(sl + 1) * 512],
                                       start=True, stop=True))
                    sq_mm_last[sl] = vm
                    P.wait("vector", c_t, vm)
                    P.wait("vector", c_v, b0_prev[0])
                    vr = P.op("vector", c_v, lambda q, bk=bk:
                              q.reduce_max(out=t_msr[0:2, 0:1], in_=psb[bk][0:2, :], axis=AX.X))
                    bank_free[bk] = (c_v, vr)
                    P.wait("vector", c_v, vr)
                    if tt == 0:
                        vmax = P.op("vector", c_v, lambda q, m2t=m2t, cc=cc:
                                    q.tensor_copy(out=m2t[0:2, cc:cc + 1], in_=t_msr[0:2, 0:1]))
                    else:
                        vmax = P.op("vector", c_v, lambda q, m2t=m2t, cc=cc:
                                    q.tensor_max(out=m2t[0:2, cc:cc + 1],
                                                 in0=m2t[0:2, cc:cc + 1],
                                                 in1=t_msr[0:2, 0:1]))
                    b0_prev[0] = vmax

            b0_units = [(qk, cc) for qk in range(2) for cc in range(NCH)]

            vones_v = P.op("gpsimd", c_p, lambda q: q.memset(
                bass.AP(t_vsb, 64, [[W_VSB, 128], [1152, NTB], [72, 16], [1, 1]]), 1.0))
            wv_all = wcast_v[23]
            for i, (tb, dvt) in enumerate([(tb, dvt) for tb in range(NTB) for dvt in range(2)]):
                bk = 4 + (i % 4)
                bwait(P, bk)
                P.wait("tensor", c_p, max(xcast_tt[tb // 4], vones_v))
                P.wait("tensor", c_s, max(wv_all, bvb_v))
                builders = [
                    (lambda q, bk=bk, dc=dc, tb=tb, dvt=dvt:
                     q.matmul(psb[bk][:, :],
                              t_xbf[:, dc * T + tb * 128: dc * T + (tb + 1) * 128],
                              t_wvbf[:, dc * 1024 + dvt * 512: dc * 1024 + (dvt + 1) * 512],
                              start=(dc == 0), stop=False))
                    for dc in range(NCH)]
                builders.append(lambda q, bk=bk, dvt=dvt:
                                q.matmul(psb[bk][:, :], t_onesb[:, :],
                                         t_bvb[0:1, dvt * 512:(dvt + 1) * 512],
                                         start=False, stop=True))
                vt = P.group("tensor", c_t, builders)
                P.wait("vector", c_t, vt)
                for j in range(8):
                    v = P.op("vector", c_v, lambda q, bk=bk, tb=tb, dvt=dvt, j=j:
                             q.tensor_copy(out=bass.AP(t_vsb,
                                                       1152 * tb + 72 * (8 * dvt + j),
                                                       [[W_VSB, 128], [1, 64]]),
                                           in_=psb[bk][:, j * 64:(j + 1) * 64]))
                bank_free[bk] = (c_v, v)
                if i % 2 == 1 and b0_units:
                    b0_unit(*b0_units.pop(0))
            while b0_units:
                b0_unit(*b0_units.pop(0))
            P.wait("vector", c_v, c_v.n)
            P.op("vector", c_v, lambda q: q.tensor_mul(out=t_msr[0:2, :], in0=t_mq2[0:2, :],
                                                       in1=t_mk2[0:2, :]))
            P.wait("scalar", c_v, c_v.n)
            P.op("scalar", c_s, lambda q: q.activation(out=t_mq2[0:2, :], in_=t_msr[0:2, :],
                                                       func=AF.Sqrt, scale=1.0 / 64.0))
            sqrt_v = c_s.n
            for cc in range(NCH):
                bk = cc % 2
                bwait(P, bk)
                P.wait("tensor", c_s, sqrt_v)
                P.op("tensor", c_t, lambda q, bk=bk, cc=cc:
                     q.matmul(psb[bk][0:128, 0:1],
                              t_negsel[0:2, 256:384],
                              t_mq2[0:2, cc:cc + 1], start=True, stop=True))
                P.wait("scalar", c_t, c_t.n)
                v = P.op("scalar", c_s, lambda q, bk=bk, cc=cc:
                         q.copy(out=t_mb[:, cc:cc + 1], in_=psb[bk][0:128, 0:1]))
                bank_free[bk] = (c_s, v)
            mb_v = c_s.n
            if debug and b == 0:
                for ctr in (c_t, c_s, c_v, c_p):
                    P.wait("gpsimd", ctr, ctr.n)
                for nm, th in [("d_qs", t_qs), ("d_ks", t_ks), ("d_vsb", t_vsb)]:
                    P.op("gpsimd", dq_dbg, lambda q, nm=nm, th=th:
                         q.dma_start(out=dbg[nm][:, :], in_=th[:, :]), 16)
                P.wait("gpsimd", dq_dbg, dq_dbg.n)
            P.into_block(nc)

            P = Prog()
            for eng in ("tensor", "scalar", "gpsimd", "vector", "sync"):
                P.wait(eng, dq_pm[0], dq_pm[0].n)
                P.wait(eng, dq_pm[1], dq_pm[1].n)
                P.wait(eng, dq_pmh[0], dq_pmh[0].n)
                P.wait(eng, dq_pmh[1], dq_pmh[1].n)
            for i in range(4):
                P.op("sync", dq_cs, lambda q, i=i:
                     q.dma_start(out=t_mask[:, i * 1024:(i + 1) * 1024],
                                 in_=dr["maskT"][i, :, :]), 16)
            mask_v = dq_cs.n
            pass

            prev_bc = [0]
            prev_vb = [0]
            pending_fin = [None]
            flash_g = [0]
            pv_gv = {}
            for cc in range(NCH):
                hA, hB = 2 * cc, 2 * cc + 1
                for qt in range(TT):
                    nkb = min(4 * qt + 4, NTB)
                    pv_mm_v = {}
                    exp_done = {}

                    def emit_scores(kb):
                        for h_i in range(2):
                            sb = (kb % 2) * 2 + h_i
                            base = 64 * h_i
                            bwait(P, sb)
                            mm_v = P.op("tensor", c_t, lambda q, sb=sb, base=base, cc=cc, kb=kb, qt=qt:
                                        q.matmul(psb[sb][:, :],
                                                 t_ks[base:base + 64,
                                                      cc * T + kb * 128: cc * T + (kb + 1) * 128],
                                                 t_qs[base:base + 64,
                                                      cc * T + qt * 512: cc * T + (qt + 1) * 512],
                                                 start=True, stop=True))
                        g = flash_g[0]
                        flash_g[0] += 1
                        poff = (g % 3) * 1024
                        P.wait("scalar", c_t, mm_v)
                        P.wait("scalar", c_s, mb_v)
                        if g - 3 in pv_gv:
                            P.wait("scalar", c_t, pv_gv[g - 3])
                        v_exp = P.op("scalar", c_s, lambda q, kb=kb, poff=poff, cc=cc:
                                     q.activation(out=t_pab[:, poff:poff + 1024],
                                                  in_=psw[kb % 2][:, :], func=AF.Exp,
                                                  bias=t_mb[:, cc:cc + 1], scale=0.125))
                        bank_free[(kb % 2) * 2] = (c_s, v_exp)
                        bank_free[(kb % 2) * 2 + 1] = (c_s, v_exp)
                        straddle = kb >= 4 * qt
                        if straddle:
                            di = kb - 4 * qt
                            P.wait("gpsimd", dq_cs, mask_v)
                            P.wait("gpsimd", c_s, v_exp)
                            v_exp = P.op("gpsimd", c_p, lambda q, poff=poff, di=di:
                                         q.tensor_mul(out=t_pab[:, poff:poff + 1024],
                                                      in0=t_pab[:, poff:poff + 1024],
                                                      in1=t_mask[:, di * 1024:(di + 1) * 1024]))
                        exp_done[(kb, 0)] = (g, v_exp, straddle)
                        exp_done[(kb, 1)] = (g, v_exp, straddle)

                    def emit_pv(kb, exp_done=exp_done, pv_mm_v=pv_mm_v,
                                nkb=nkb, hA=hA, hB=hB, cc=cc, qt=qt):
                        g, vv, straddle = exp_done[(kb, 0)]
                        for h_i in range(2):
                            hh = hA if h_i == 0 else hB
                            poff = (g % 3) * 1024 + h_i * 512
                            pvb = 4 + h_i
                            if kb == 0:
                                bwait(P, pvb)
                            P.wait("tensor", c_p if straddle else c_s, vv)
                            vpv = P.op("tensor", c_t, lambda q, pvb=pvb, kb=kb, hh=hh, poff=poff,
                                       first=(kb == 0), last=(kb == nkb - 1):
                                       q.matmul(psb[pvb][0:65, :],
                                                bass.AP(t_vsb, 1152 * kb + 72 * hh,
                                                        [[W_VSB, 128], [1, 65]]),
                                                t_pab[:, poff:poff + 512],
                                                start=first, stop=last))
                            pv_mm_v[(kb, h_i)] = vpv

                    emit_scores(0)
                    emit_scores(1)
                    if pending_fin[0] is not None:
                        pending_fin[0]()
                        pending_fin[0] = None
                    for kb in range(2, nkb):
                        emit_scores(kb)
                        emit_pv(kb - 2)
                    emit_pv(nkb - 2)

                    def tile_tail(cc=cc, qt=qt, nkb=nkb, emit_pv=emit_pv,
                                  exp_done=exp_done, pv_mm_v=pv_mm_v):
                        emit_pv(nkb - 1)
                        return c_t.n

                    pv_done = None

                    def finalize(cc=cc, qt=qt, tile_tail=tile_tail):
                        pv_done = tile_tail()
                        P.wait("vector", c_t, pv_done)
                        P.wait("vector", c_t, prev_bc[0])
                        P.op("vector", c_v, lambda q: q.tensor_copy(out=t_sumr[0:1, 0:512],
                                                                    in_=psb[4][64:65, :]))
                        P.op("vector", c_v, lambda q: q.tensor_copy(out=t_sumr[0:1, 512:1024],
                                                                    in_=psb[5][64:65, :]))
                        P.wait("vector", c_v, c_v.n)
                        P.op("vector", c_v, lambda q: q.reciprocal(out=t_sumr[0:1, 1024:2048],
                                                                   in_=t_sumr[0:1, 0:1024]))
                        P.wait("vector", c_v, c_v.n)
                        v_rec = P.op("vector", c_v, lambda q: q.tensor_copy(
                            out=t_sumrb[0:1, 0:1024], in_=t_sumr[0:1, 1024:2048]))
                        for h_i in range(2):
                            bk = 6 + h_i
                            bwait(P, bk)
                            P.wait("tensor", c_v, v_rec)
                            P.op("tensor", c_t, lambda q, bk=bk, h_i=h_i:
                                 q.matmul(psb[bk][0:64, :], t_onesb[0:1, 0:64],
                                          t_sumrb[0:1, h_i * 512:(h_i + 1) * 512],
                                          start=True, stop=True))
                        bc_v = c_t.n
                        prev_bc[0] = bc_v
                        P.wait("vector", c_t, bc_v)
                        P.wait("vector", c_v, prev_vb[0])
                        for h_i in range(2):
                            P.op("vector", c_v, lambda q, h_i=h_i:
                                 q.tensor_copy(out=t_bcs[0:64, h_i * 512:(h_i + 1) * 512],
                                               in_=psb[6 + h_i][0:64, :]))
                        vbc = c_v.n
                        bank_free[6] = (c_v, vbc)
                        bank_free[7] = (c_v, vbc)
                        P.wait("vector", c_v, vbc)
                        vA = P.op("vector", c_v, lambda q, cc=cc, qt=qt:
                                  q.tensor_mul(out=t_attn[0:64, cc * T + qt * 512:
                                                          cc * T + (qt + 1) * 512],
                                               in0=psb[4][0:64, :], in1=t_bcs[0:64, 0:512]))
                        if dq_sh[qt % 2].n > 0:
                            P.wait("vector", dq_sh[qt % 2], dq_sh[qt % 2].n)
                        vB = P.op("vector", c_v, lambda q, sl=qt % 2:
                                  q.tensor_mul(out=t_attnB[0:64, sl * 512:(sl + 1) * 512],
                                               in0=psb[5][0:64, :], in1=t_bcs[0:64, 512:1024]))
                        bank_free[4] = (c_v, vA)
                        bank_free[5] = (c_v, vB)
                        prev_vb[0] = vB
                        P.wait("scalar", c_v, vB)
                        P.op("scalar", dq_sh[qt % 2], lambda q, cc=cc, qt=qt, sl=qt % 2:
                             q.dma_start(out=t_attn[64:128, cc * T + qt * 512:
                                                    cc * T + (qt + 1) * 512],
                                         in_=t_attnB[0:64, sl * 512:(sl + 1) * 512]), 16)
                    pending_fin[0] = finalize
            pending_fin[0]()
            pending_fin[0] = None
            if debug and b == 0:
                for ctr in (c_t, c_s, c_v, c_p):
                    P.wait("gpsimd", ctr, ctr.n)
                P.wait("gpsimd", dq_sh[0], dq_sh[0].n)
                P.wait("gpsimd", dq_sh[1], dq_sh[1].n)
                P.op("gpsimd", dq_dbg, lambda q: q.dma_start(out=dbg["d_attn"][:, :], in_=t_attn[:, :]), 16)
                P.op("gpsimd", dq_dbg, lambda q: q.dma_start(out=dbg["d_mb"][:, :], in_=t_mb[:, :]), 16)
                P.wait("gpsimd", dq_dbg, dq_dbg.n)
            P.into_block(nc)

            P = Prog()
            r1_written = {}
            hbf_cast = {}
            r1_final = {}
            sq_used = {}
            lnt_used = {}
            for eng in ("tensor", "sync", "scalar", "vector", "gpsimd"):
                P.wait(eng, dq_sh[0], dq_sh[0].n)
                P.wait(eng, dq_sh[1], dq_sh[1].n)
            wpcast_v = []
            xre_hist = []
            proj_mm_last = {}
            for oc in range(NCH):
                if oc >= 2:
                    P.wait("sync", c_s, wpcast_v[oc - 2])
                P.op("sync", dq_w[oc % 2], lambda q, oc=oc, sl=oc % 2:
                     q.dma_start(out=t_wpst[:, sl * 1024:(sl + 1) * 1024],
                                 in_=dr["wp_t"][oc, :, :]), 16)
                P.wait("scalar", dq_w[oc % 2], dq_w[oc % 2].n)
                if oc - 2 in proj_mm_last:
                    P.wait("scalar", c_t, proj_mm_last[oc - 2])
                vw = P.op("scalar", c_s, lambda q, sl=oc % 2:
                          q.copy(out=t_wpbf[:, sl * 1024:(sl + 1) * 1024],
                                 in_=t_wpst[:, sl * 1024:(sl + 1) * 1024]))
                wpcast_v.append(vw)
                for tt in range(TT):
                    i = oc * TT + tt
                    if i >= 2:
                        P.wait("sync", c_v, xre_hist[i - 2])
                    P.op("sync", dq_x[i % 2], lambda q, oc=oc, tt=tt, sl=i % 2:
                         q.dma_start(out=t_xre[:, sl * 512:(sl + 1) * 512],
                                     in_=xT[oc * 128:(oc + 1) * 128,
                                            b * T + tt * 512: b * T + (tt + 1) * 512]), 16)
                    xre_dma = (i % 2, dq_x[i % 2].n)
                    bk = (oc % 2) * 4 + tt
                    bwait(P, bk)
                    P.wait("tensor", c_s, vw)
                    builders = [
                        (lambda q, bk=bk, sl=oc % 2, dc=dc, tt=tt:
                         q.matmul(psb[bk][:, :],
                                  t_wpbf[:, sl * 1024 + dc * 128: sl * 1024 + (dc + 1) * 128],
                                  t_attn[:, dc * T + tt * 512: dc * T + (tt + 1) * 512],
                                  start=(dc == 0), stop=(dc == NCH - 1)))
                        for dc in range(NCH)]
                    vt = P.group("tensor", c_t, builders)
                    proj_mm_last[oc] = vt
                    P.wait("vector", c_t, vt)
                    P.wait("vector", dq_x[xre_dma[0]], xre_dma[1])
                    vv = P.op("vector", c_v, lambda q, bk=bk, oc=oc, tt=tt, sl=i % 2:
                              q.scalar_tensor_tensor(
                                  out=t_r1[:, oc * T + tt * 512: oc * T + (tt + 1) * 512],
                                  in0=psb[bk][:, :], scalar=t_bp[:, oc:oc + 1],
                                  in1=t_xre[:, sl * 512:(sl + 1) * 512],
                                  op0=ALU.add, op1=ALU.add))
                    r1_written[(oc, tt)] = vv
                    xre_hist.append(vv)
                    bank_free[bk] = (c_v, vv)
            emit_ln(P, sink="hbf")
            if debug and b == 0:
                for ctr in (c_t, c_s, c_v, c_p):
                    P.wait("gpsimd", ctr, ctr.n)
                P.op("gpsimd", dq_dbg, lambda q: q.dma_start(out=dbg["d_r1"][:, :], in_=t_r1[:, :]), 16)
                P.op("gpsimd", dq_dbg, lambda q: q.dma_start(out=dbg["d_h"][:, :], in_=t_hbf[:, :]), 16)
                P.wait("gpsimd", dq_dbg, dq_dbg.n)
            P.into_block(nc)

            P = Prog()
            prev_comb = [0]
            lg_mm = {}
            for tt in range(TT):
                bk = 4 + tt % 2
                bwait(P, bk)
                builders = [
                    (lambda q, bk=bk, dc=dc, tt=tt:
                     q.matmul(psb[bk][0:8, :], t_gwb[:, dc * 8:(dc + 1) * 8],
                              t_hbf[:, dc * T + tt * 512: dc * T + (tt + 1) * 512],
                              start=(dc == 0), stop=(dc == NCH - 1)))
                    for dc in range(NCH)]
                vt = P.group("tensor", c_t, builders)
                P.wait("scalar", c_t, vt)
                if tt >= 2:
                    P.wait("scalar", c_t, lg_mm.get(tt - 2, 0))
                vlg = P.op("scalar", c_s, lambda q, bk=bk, sl=tt % 2:
                           q.copy(out=t_lgb[0:8, sl * 512:(sl + 1) * 512], in_=psb[bk][0:8, :]))
                bank_free[bk] = (c_s, vlg)
                for i in range(4):
                    tb = tt * 4 + i
                    bk2 = tb % 2
                    bwait(P, bk2)
                    P.wait("tensor", c_s, vlg)
                    vtr = P.op("tensor", c_t, lambda q, bk2=bk2, sl=tt % 2, i=i:
                               q.transpose(psb[bk2][0:128, 0:8],
                                           t_lgb[0:8, sl * 512 + i * 128: sl * 512 + (i + 1) * 128],
                                           t_ident[0:8, 0:8]))
                    lg_mm[tt] = vtr
                    P.wait("vector", c_t, vtr)
                    P.wait("vector", c_v, prev_comb[0])
                    P.op("vector", c_v, lambda q, bk2=bk2: q.reduce_max(
                        out=t_cm1[:, :], in_=psb[bk2][0:128, 0:8], axis=AX.X))
                    P.wait("vector", c_v, c_v.n)
                    vd = P.op("vector", c_v, lambda q, bk2=bk2: q.tensor_scalar_sub(
                        out=t_cd[:, :], in0=psb[bk2][0:128, 0:8], scalar1=t_cm1[:, 0:1]))
                    bank_free[bk2] = (c_v, vd)
                    P.wait("vector", c_v, vd)
                    P.op("vector", c_v, lambda q: q.tensor_single_scalar(
                        out=t_ceq[:, :], in_=t_cd[:, :], scalar=0.0, op=ALU.is_equal))
                    P.wait("vector", c_v, c_v.n)
                    P.op("vector", c_v, lambda q: q.scalar_tensor_tensor(
                        out=t_ceq[:, :], in0=t_ceq[:, :], scalar=-1e30, in1=t_cd[:, :],
                        op0=ALU.mult, op1=ALU.add))
                    P.wait("vector", c_v, c_v.n)
                    P.op("vector", c_v, lambda q: q.reduce_max(out=t_cm2[:, :], in_=t_ceq[:, :],
                                                               axis=AX.X))
                    P.wait("vector", c_v, c_v.n)
                    vk = P.op("vector", c_v, lambda q: q.tensor_single_scalar(
                        out=t_ceq[:, :], in_=t_cd[:, :], scalar=t_cm2[:, 0:1], op=ALU.is_ge))
                    P.wait("scalar", c_v, vd)
                    ve = P.op("scalar", c_s, lambda q: q.activation(out=t_cnum[:, :],
                                                                   in_=t_cd[:, :], func=AF.Exp))
                    P.wait("vector", c_s, ve)
                    P.wait("vector", c_v, vk)
                    P.op("vector", c_v, lambda q: q.tensor_mul(out=t_cnum[:, :], in0=t_cnum[:, :],
                                                               in1=t_ceq[:, :]))
                    P.wait("vector", c_v, c_v.n)
                    P.op("vector", c_v, lambda q: q.reduce_sum(out=t_cden[:, :], in_=t_cnum[:, :],
                                                               axis=AX.X))
                    P.wait("vector", c_v, c_v.n)
                    P.op("vector", c_v, lambda q: q.reciprocal(out=t_cm1[:, :], in_=t_cden[:, :]))
                    P.wait("vector", c_v, c_v.n)
                    prev_comb[0] = P.op("vector", c_v, lambda q, tb=tb: q.tensor_scalar_mul(
                        out=t_comb[:, tb * 8:(tb + 1) * 8], in0=t_cnum[:, :],
                        scalar1=t_cm1[:, 0:1]))
            comb_done = c_v.n
            if debug and b == 0:
                for ctr in (c_t, c_s, c_v, c_p):
                    P.wait("gpsimd", ctr, ctr.n)
                P.op("gpsimd", dq_dbg, lambda q: q.dma_start(out=dbg["d_comb"][:, :], in_=t_comb[:, :]), 16)
                P.wait("gpsimd", dq_dbg, dq_dbg.n)
            P.into_block(nc)

            P = Prog()
            combsb_v = {}
            gcm_last = {}
            rowt_used = {}

            def emit_extract(e):
                prev_gcm = gcm_last.get(e - 1)
                for tt in range(TT):
                    evs = []
                    for i in range(4):
                        tb = tt * 4 + i
                        bwait(P, 0)
                        P.op("tensor", c_t, lambda q, tb=tb, e=e:
                             q.transpose(psb[0][0:1, 0:128],
                                         t_comb[:, tb * 8 + e: tb * 8 + e + 1],
                                         t_ident[:, :]))
                        P.wait("scalar", c_t, c_t.n)
                        if rowt_used.get(0):
                            P.wait("scalar", c_t, rowt_used[0])
                        v = P.op("scalar", c_s, lambda q, i=i:
                                 q.copy(out=t_rowt[0:1, i * 128:(i + 1) * 128],
                                        in_=psb[0][0:1, 0:128]))
                        evs.append(v)
                        bank_free[0] = (c_s, v)
                    bwait(P, 1)
                    P.wait("tensor", c_s, max(evs))
                    rowt_used[0] = P.op("tensor", c_t, lambda q:
                                        q.matmul(psb[1][:, :], t_onesf[0:1, :],
                                                 t_rowt[0:1, 0:512],
                                                 start=True, stop=True))
                    P.wait("vector", c_t, c_t.n)
                    if prev_gcm is not None:
                        P.wait("vector", c_p, prev_gcm)
                    v = P.op("vector", c_v, lambda q, e=e, tt=tt:
                             q.tensor_copy(out=t_combsb[:, tt * 512:(tt + 1) * 512],
                                           in_=psb[1][:, :]))
                    bank_free[1] = (c_v, v)
                combsb_v[e] = c_v.n

            emit_extract(0)
            su_done_v = 0
            gbuf_used = {}
            wgucast = []
            wdcast = []
            gu_mm_last = []
            dn_mm_last = []
            for ei in range(9):
                is_sh = (ei == 8)
                e_w = ei
                for fp in range(NFC):
                    if len(wgucast) >= 1:
                        P.wait("sync", c_s, wgucast[-1])
                    P.op("sync", dq_wgu, lambda q, e_w=e_w, fp=fp:
                         q.dma_start(out=t_wgust[:, :],
                                     in_=dr["wgu_t"][e_w, fp, :, :]), 16)
                    P.wait("scalar", dq_wgu, dq_wgu.n)
                    if len(gu_mm_last) >= 2:
                        P.wait("scalar", c_t, gu_mm_last[-2])
                    v = P.op("scalar", c_s, lambda q, sl=fp % 2:
                             q.copy(out=t_wgubf[:, sl * 2048:(sl + 1) * 2048],
                                    in_=t_wgust[:, :]))
                    wgucast.append(v)
                    wguv = v
                    for tt in range(TT):
                        bk = tt
                        bwait(P, bk)
                        P.wait("tensor", c_s, wguv)
                        if b == 0 and ei == 0:
                            pass
                        builders = [
                            (lambda q, bk=bk, sl=fp % 2, dc=dc, tt=tt:
                             q.matmul(psb[bk][:, :],
                                      t_wgubf[:, sl * 2048 + dc * 256: sl * 2048 + dc * 256 + 128],
                                      t_hbf[:, dc * T + tt * 512: dc * T + (tt + 1) * 512],
                                      start=(dc == 0), stop=(dc == NCH - 1)))
                            for dc in range(NCH)]
                        vg = P.group("tensor", c_t, builders)
                        P.wait("scalar", c_t, vg)
                        gu = gbuf_used.get((fp % 2, tt))
                        if gu is not None:
                            P.wait("scalar", c_v, gu)
                        vsil = P.op("scalar", c_s, lambda q, bk=bk, tt=tt, gsl=fp % 2:
                                    q.activation(out=t_gbuf[:, gsl * T + tt * 512: gsl * T + (tt + 1) * 512],
                                                 in_=psb[bk][:, :], func=AF.Silu))
                        bank_free[bk] = (c_s, vsil)
                        if not is_sh:
                            P.wait("gpsimd", c_s, vsil)
                            P.wait("gpsimd", c_v, combsb_v[ei])
                            vsil2 = P.op("gpsimd", c_p, lambda q, e=ei, tt=tt, gsl=fp % 2:
                                         q.tensor_mul(out=t_gbuf[:, gsl * T + tt * 512: gsl * T + (tt + 1) * 512],
                                                      in0=t_gbuf[:, gsl * T + tt * 512: gsl * T + (tt + 1) * 512],
                                                      in1=t_combsb[:, tt * 512:(tt + 1) * 512]))
                            gcm_last[ei] = vsil2
                        bku = 4 + tt
                        bwait(P, bku)
                        builders = [
                            (lambda q, bk=bku, sl=fp % 2, dc=dc, tt=tt:
                             q.matmul(psb[bk][:, :],
                                      t_wgubf[:, sl * 2048 + dc * 256 + 128: sl * 2048 + (dc + 1) * 256],
                                      t_hbf[:, dc * T + tt * 512: dc * T + (tt + 1) * 512],
                                      start=(dc == 0), stop=(dc == NCH - 1)))
                            for dc in range(NCH)]
                        vu = P.group("tensor", c_t, builders)
                        if tt == TT - 1:
                            gu_mm_last.append(vu)
                        P.wait("vector", c_t, vu)
                        if is_sh:
                            P.wait("vector", c_s, vsil)
                        else:
                            P.wait("vector", c_p, vsil2)
                        vsu = P.op("vector", c_v, lambda q, bk=bku, fp=fp, tt=tt, gsl=fp % 2:
                                   q.tensor_mul(out=t_su[:, fp * T + tt * 512: fp * T + (tt + 1) * 512],
                                                in0=psb[bk][:, :],
                                                in1=t_gbuf[:, gsl * T + tt * 512: gsl * T + (tt + 1) * 512]))
                        gbuf_used[(fp % 2, tt)] = vsu
                        bank_free[bku] = (c_v, vsu)
                su_done_v = c_v.n
                if ei < 7:
                    emit_extract(ei + 1)
                for dd in range(NCH):
                    if len(wdcast) >= 2:
                        P.wait("sync", c_p, wdcast[-2])
                    P.op("sync", dq_wd[dd % 2], lambda q, e_w=e_w, dd=dd, sl=dd % 2:
                         q.dma_start(out=t_wdst[:, sl * FF:(sl + 1) * FF],
                                     in_=dr["wd_t"][e_w, dd, :, :]), 16)
                    P.wait("gpsimd", dq_wd[dd % 2], dq_wd[dd % 2].n)
                    if len(dn_mm_last) >= 2:
                        P.wait("gpsimd", c_t, dn_mm_last[-2])
                    vwd = P.op("gpsimd", c_p, lambda q, sl=dd % 2:
                               q.tensor_copy(out=t_wdbf[:, sl * FF:(sl + 1) * FF],
                                             in_=t_wdst[:, sl * FF:(sl + 1) * FF]))
                    wdcast.append(vwd)
                    for tt in range(TT):
                        bk = (dd % 2) * 4 + tt
                        bwait(P, bk)
                        P.wait("tensor", c_p, vwd)
                        P.wait("tensor", c_v, su_done_v)
                        builders = [
                            (lambda q, bk=bk, sl=dd % 2, fc=fc, tt=tt:
                             q.matmul(psb[bk][:, :],
                                      t_wdbf[:, sl * FF + fc * 128: sl * FF + (fc + 1) * 128],
                                      t_su[:, fc * T + tt * 512: fc * T + (tt + 1) * 512],
                                      start=(fc == 0), stop=(fc == NFC - 1)))
                            for fc in range(NFC)]
                        vt = P.group("tensor", c_t, builders)
                        if tt == TT - 1:
                            dn_mm_last.append(vt)
                        P.wait("vector", c_t, vt)
                        prev = r1_final.get((dd, tt))
                        if prev is not None:
                            if isinstance(prev, tuple):
                                P.wait("vector", prev[0], prev[1])
                            else:
                                P.wait("vector", c_v, prev)
                        va = P.op("vector", c_v, lambda q, bk=bk, dd=dd, tt=tt:
                                  q.tensor_add(out=t_r1[:, dd * T + tt * 512: dd * T + (tt + 1) * 512],
                                               in0=psb[bk][:, :],
                                               in1=t_r1[:, dd * T + tt * 512: dd * T + (tt + 1) * 512]))
                        r1_final[(dd, tt)] = va
                        bank_free[bk] = (c_v, va)
            P.into_block(nc)

            P = Prog()
            r1_written = {k: v for k, v in r1_final.items()}
            sq_used = {}
            lnt_used = {}
            emit_ln(P, sink="out")
            P.into_block(nc)

        P = Prog()
        P.wait("scalar", dq_o[0], dq_o[0].n)
        P.wait("scalar", dq_o[1], dq_o[1].n)
        P.into_block(nc)

        for ctr in S.values():
            assert ctr.n < 60000, f"semaphore overflow risk: {ctr.name}={ctr.n}"
        meta = dict(total_sbuf=total_sbuf, n_inst=nc.n_instructions(),
                    sems={k: v.n for k, v in S.items()})
    return nc, meta


def qk_perm():
    perm = np.zeros(1024, dtype=np.int64)
    for h in range(16):
        for i in range(32):
            perm[32 * h + i] = 64 * h + 2 * i
            perm[512 + 32 * h + i] = 64 * h + 2 * i + 1
    return perm


def host_constants(T):
    bf = ml_dtypes.bfloat16
    c = {}
    freqs = 10000.0 ** (-np.arange(32) / 32.0)
    ang = np.arange(T)[:, None] * freqs[None, :]
    rowf = np.tile(freqs, 4)
    angs = np.arange(T)[None, :] * rowf[:, None]
    c["cosT"] = np.cos(angs).astype(np.float32)
    c["sinT"] = np.sin(angs).astype(np.float32)
    m = np.zeros((4, 128, 512), np.float32)
    for d in range(4):
        p = np.arange(128)[:, None]
        cc = np.arange(512)[None, :]
        m[d] = (cc >= d * 128 + p).astype(np.float32)
    c["maskT"] = np.concatenate([m, m], axis=2).astype(bf)
    selE = np.zeros((128, 128), np.float32)
    selO = np.zeros((128, 128), np.float32)
    for p in range(64):
        mE = p if p < 32 else p + 32
        mO = p + 32 if p < 32 else p + 64
        selE[p, mE] = 1.0
        selE[64 + p, mE] = 1.0
        selO[p, mO] = 1.0
        selO[64 + p, mO] = 1.0
    c["selE2"] = selE.astype(bf)
    c["selO2"] = selO.astype(bf)
    hs = np.zeros((128, 2), np.float32)
    hs[0:64, 0] = 1.0
    hs[64:128, 1] = 1.0
    c["hselc"] = hs.astype(bf)
    ns = np.zeros((2, 384), np.float32)
    ns[0, 0:128] = -1.0
    ns[1, 128:256] = -1.0
    ns[:, 256:384] = -1.0
    c["negsel"] = ns
    c["ident"] = np.eye(128, dtype=np.float32)
    c["identb"] = np.eye(128, dtype=np.float32).astype(bf)
    c["onesf"] = np.ones((1, 128), np.float32)
    c["onesb"] = np.ones((1, 128), np.float32).astype(bf)
    c["onescol"] = np.ones((128, 1), np.float32)
    return c


def host_weights(W_qkv, b_qkv, W_proj, b_proj, gate_w, w_up, w_down,
                 sg_gate, sg_up, sg_down):
    out = {}
    perm = qk_perm()
    full_perm = np.concatenate([perm, 1024 + perm, 2048 + np.arange(1024)])
    Wq = np.asarray(W_qkv)[:, full_perm]
    bq = np.asarray(b_qkv)[full_perm]
    slots = SLOTS_QK + list(range(16, 24))
    wq_t = np.zeros((24, 128, 1024), np.float32)
    for s_i, jc in enumerate(slots):
        blk = Wq[:, jc * 128:(jc + 1) * 128]
        wq_t[s_i] = blk.reshape(8, 128, 128).transpose(1, 0, 2).reshape(128, 1024)
    out["wq_t"] = np.ascontiguousarray(wq_t)
    bqk = np.zeros((128, 16), np.float32)
    for s_i in range(16):
        jc = SLOTS_QK[s_i]
        bqk[:, s_i] = bq[jc * 128:(jc + 1) * 128]
    out["bqk_c"] = bqk
    out["bv_row"] = np.ascontiguousarray(bq[2048:].reshape(1, 1024).astype(np.float32))
    Wp = np.asarray(W_proj)
    wp_t = np.zeros((8, 128, 1024), np.float32)
    for oc in range(8):
        blk = Wp[:, oc * 128:(oc + 1) * 128]
        wp_t[oc] = blk.reshape(8, 128, 128).transpose(1, 0, 2).reshape(128, 1024)
    out["wp_t"] = wp_t
    out["bp_c"] = np.ascontiguousarray(np.asarray(b_proj).reshape(8, 128).T.astype(np.float32))
    out["gw_t"] = np.ascontiguousarray(np.asarray(gate_w).T.reshape(8, 128, 8)
                                       .transpose(1, 0, 2).reshape(128, 64).astype(np.float32))
    wgu_t = np.zeros((9, NFC, 128, 2048), np.float32)
    wd_t = np.zeros((9, 8, 128, FF), np.float32)
    wu = np.asarray(w_up)
    wd = np.asarray(w_down)
    for e in range(8):
        wgT = wu[e, :FF, :].T
        wuT = wu[e, FF:, :].T
        wdT = wd[e].T
        for fp in range(NFC):
            g = wgT[:, fp * 128:(fp + 1) * 128].reshape(8, 128, 128)
            u = wuT[:, fp * 128:(fp + 1) * 128].reshape(8, 128, 128)
            gu = np.concatenate([g, u], axis=2)
            wgu_t[e, fp] = gu.transpose(1, 0, 2).reshape(128, 2048)
        for dd in range(8):
            blk = wdT[:, dd * 128:(dd + 1) * 128]
            wd_t[e, dd] = blk.reshape(NFC, 128, 128).transpose(1, 0, 2).reshape(128, FF)
    sgT = np.concatenate([np.asarray(sg_gate).T, np.asarray(sg_up).T], axis=1)
    sdT = np.asarray(sg_down).T
    for fp in range(NFC):
        g = sgT[:, fp * 128:(fp + 1) * 128].reshape(8, 128, 128)
        u = sgT[:, FF + fp * 128: FF + (fp + 1) * 128].reshape(8, 128, 128)
        gu = np.concatenate([g, u], axis=2)
        wgu_t[8, fp] = gu.transpose(1, 0, 2).reshape(128, 2048)
    for dd in range(8):
        blk = sdT[:, dd * 128:(dd + 1) * 128]
        wd_t[8, dd] = blk.reshape(NFC, 128, 128).transpose(1, 0, 2).reshape(128, FF)
    out["wgu_t"] = wgu_t
    out["wd_t"] = wd_t
    return out


_T = 2048
_NB = 4
_NCORES = 8


def kernel(L0_attn_x, L0_attn_W_qkv, L0_attn_b_qkv, L0_attn_W_proj, L0_attn_b_proj,
           L0_ffn_gate_w, L0_ffn_w_up, L0_ffn_w_down, L0_ffn_sg_gate, L0_ffn_sg_up,
           L0_ffn_sg_down):
    from concourse.bass_utils import run_bass_kernel_spmd
    x = np.ascontiguousarray(np.asarray(L0_attn_x, dtype=np.float32))
    m = host_weights(L0_attn_W_qkv, L0_attn_b_qkv, L0_attn_W_proj, L0_attn_b_proj,
                     L0_ffn_gate_w, L0_ffn_w_up, L0_ffn_w_down,
                     L0_ffn_sg_gate, L0_ffn_sg_up, L0_ffn_sg_down)
    m.update(host_constants(_T))
    for k in list(m):
        if m[k].dtype == np.float32:
            m[k] = np.ascontiguousarray(m[k])
    in_maps = []
    for c in range(_NCORES):
        mc = dict(m)
        xc = x[c * _NB:(c + 1) * _NB]
        mc["xT"] = np.ascontiguousarray(xc.transpose(2, 0, 1).reshape(1024, _NB * _T))
        in_maps.append(mc)
    nc, _meta = build_program(T=_T, n_b=_NB, debug=False)
    res = run_bass_kernel_spmd(nc, in_maps, core_ids=list(range(_NCORES)))
    outs = []
    for c in range(_NCORES):
        oT = np.asarray(res.results[c]["outT"])
        outs.append(oT.reshape(1024, _NB, _T).transpose(1, 2, 0))
    return np.ascontiguousarray(np.concatenate(outs, axis=0), dtype=np.float32)
